# version 14
# baseline (speedup 1.0000x reference)
import collections
import contextlib
import numpy as np
import concourse.bass as bass
import concourse.mybir as mybir
from concourse.bass_utils import run_bass_kernel_spmd

F32 = mybir.dt.float32
BF16 = mybir.dt.bfloat16
AF = mybir.ActivationFunctionType
ALU = mybir.AluOpType
AX = mybir.AxisListType

T, D, HB = 4096, 1024, 128
NIB, IB, HP = 8, 512, 8
NEX = 8
EPS_LN, EPS_B = 1e-5, 1e-6
SHIFT = 2.0
INV_SQRT_D = 1.0 / 32.0
INV_D = 1.0 / float(D)


class Fut:
    __slots__ = ("v",)

    def __init__(self, v=None):
        self.v = v


def build_program():
    import os as _os0
    NIB_EMIT = int(_os0.environ.get("KLIM", NIB))
    KCUT = int(_os0.environ.get("KCUT", 3))
    KCH = int(_os0.environ.get("KCH", 9))
    nc = bass.Bass("TRN2", target_bir_lowering=False)

    x_in = nc.dram_tensor("x_in", [T, D], F32, kind="ExternalInput").ap()
    out_t = nc.dram_tensor("out", [T, D], F32, kind="ExternalOutput").ap()
    wq_d = nc.dram_tensor("wq", [D, D], F32, kind="ExternalInput").ap()
    wk_d = nc.dram_tensor("wk", [D, D], F32, kind="ExternalInput").ap()
    wv_d = nc.dram_tensor("wv", [D, D], F32, kind="ExternalInput").ap()
    wo_d = nc.dram_tensor("wo", [D, D], F32, kind="ExternalInput").ap()
    bias_d = nc.dram_tensor("biases", [4, D], F32, kind="ExternalInput").ap()
    fw_d = nc.dram_tensor("fw", [NEX, D, D], F32, kind="ExternalInput").ap()
    fb_d = nc.dram_tensor("fbm", [NEX, D], F32, kind="ExternalInput").ap()
    flog_d = nc.dram_tensor("flog", [1, NEX], F32, kind="ExternalInput").ap()
    fu_d = nc.dram_tensor("fu", [1, NEX], F32, kind="ExternalInput").ap()
    wqb_d = nc.dram_tensor("wqb", [HB, D], F32, kind="ExternalInput").ap()
    wkb_d = nc.dram_tensor("wkb", [HB, D], F32, kind="ExternalInput").ap()
    wvb_d = nc.dram_tensor("wvb", [HB, D], F32, kind="ExternalInput").ap()
    wob_d = nc.dram_tensor("wob", [D, HB], F32, kind="ExternalInput").ap()
    bqb_d = nc.dram_tensor("bqb", [1, HB], F32, kind="ExternalInput").ap()
    bkb_d = nc.dram_tensor("bkb", [1, HB], F32, kind="ExternalInput").ap()
    bvb_d = nc.dram_tensor("bvb", [1, HB], F32, kind="ExternalInput").ap()
    bob_d = nc.dram_tensor("bob", [1, D], F32, kind="ExternalInput").ap()
    bcolh_d = nc.dram_tensor("bcolh", [128, 32], F32, kind="ExternalInput").ap()
    bobch_d = nc.dram_tensor("bobch", [128, 8], F32, kind="ExternalInput").ap()
    kt_scr = nc.dram_tensor("kt_scr", [D, T], BF16, kind="Internal").ap()
    v_scr = nc.dram_tensor("v_scr", [T, D], BF16, kind="Internal").ap()

    Q = {"pe": [], "ve": [], "sc": [], "sy": [], "gp": []}
    CT = {"ld": 0, "g": 0, "pe": 0, "ve": 0, "sc": 0}

    with contextlib.ExitStack() as _st:
        def sbt(name, shape, dt):
            return _st.enter_context(nc.sbuf_tensor(name, shape, dt))

        POOLB = 118272
        PARB = 94560
        _pool = sbt("pool", [128, POOLB // 2], BF16)
        _parena = sbt("parena", [128, PARB // 2], BF16)

        class Alloc2:
            def __init__(self, buf, cap):
                self.buf = buf
                self.cap = cap
                self.off = 0

            def take(self, shape, dt):
                nel = 1
                for s in shape[1:]:
                    nel *= s
                esz = 4 if dt == F32 else 2
                nb = nel * esz
                off = self.off
                self.off = (off + nb + 63) // 64 * 64
                assert self.off <= self.cap, f"pool overflow {self.off}"
                v = self.buf[:, off // 2:(off + nb) // 2]
                if dt == F32:
                    v = v.bitcast(F32)
                if len(shape) == 3:
                    v = v.rearrange("p (a b) -> p a b", a=shape[1])
                elif len(shape) == 4:
                    v = v.rearrange("p (a b c) -> p a b c", a=shape[1], b=shape[2])
                if shape[0] != 128:
                    v = v[0:shape[0]]
                return v

        _pa = Alloc2(_parena, PARB)
        wqT = _pa.take([128, 8, D], BF16)
        woT = _pa.take([128, 8, D], BF16)
        weff = _pa.take([128, 8, D], BF16)
        wbT = _pa.take([128, 8, 3 * HB], BF16)
        wobT = _pa.take([128, 8, HB], BF16)
        bcol = _pa.take([128, 8, 4], F32)
        beffc = _pa.take([128, 8], F32)
        bobc = _pa.take([128, 8], F32)
        bqkvB = _pa.take([128, 3, HB], F32)
        negcsW = _pa.take([1, D], BF16)
        negcsB = _pa.take([1, 3 * HB], BF16)
        onesrow = _pa.take([1, 128], BF16)
        onescol = _pa.take([128, 1], BF16)
        identb = _pa.take([128, 128], BF16)
        identf = _pa.take([128, 128], F32)
        ident32 = _pa.take([128, 32], F32)
        masku = _pa.take([128, 128], F32)
        cpool = _pa.take([128, 4], F32)
        hrow = _pa.take([1, 16], F32)
        hrowb = _pa.take([1, 8], BF16)
        hcol = _pa.take([8, 64], BF16)[:, 0:1]
        hBc = _pa.take([128, 8], F32)
        lnst = _pa.take([128, 32, 2], F32)
        a_sb = _pa.take([128, HB], F32)
        mt_sb = _pa.take([128, HB], F32)
        a_bf = _pa.take([128, HB], BF16)
        mt_bf = _pa.take([128, HB], BF16)
        xt = _pa.take([128, D], F32)
        xlnf = _pa.take([128, D], F32)
        xlnb = _pa.take([128, D], BF16)
        xlnT = _pa.take([128, 8, IB], BF16)
        srow = _pa.take([1, 3, IB], F32)
        mr1s = _pa.take([1, 3, IB], BF16)
        mr2t = _pa.take([1, IB], BF16)

        import os as _os
        if _os.environ.get("KSIZE"):
            print("parena used:", _pa.off)
        _s1 = Alloc2(_pool, POOLB)
        wkvT = _s1.take([128, 16, D], BF16)
        wtmp = _s1.take([128, 4, D], F32)
        wtmpb = _s1.take([128, 4, D], BF16)
        eacc = _s1.take([128, 4, D], F32)
        ksb = _s1.take([128, 8, 512], BF16)
        vsb = _s1.take([128, 4, D], BF16)
        brow = _s1.take([1, D], F32)
        browb = _s1.take([1, D], BF16)
        babrow = _s1.take([1, 512], F32)
        babrowb = _s1.take([1, 512], BF16)
        fbmat = _s1.take([8, D], F32)
        fbmatb = _s1.take([8, D], BF16)
        bvB = _s1.take([128, D], F32)

        if _os.environ.get("KSIZE"):
            print("s1 pool used:", _s1.off)
        _mn = Alloc2(_pool, POOLB)
        kst = _mn.take([128, 2, T], BF16)
        vst = _mn.take([128, 2, 32, 130], BF16)
        qbt = _mn.take([128, IB], BF16)
        ptile = _mn.take([128, 2, 1024], BF16)
        oib = _mn.take([128, 2, 8, IB], BF16)
        ohp = _mn.take([64, 8, IB], BF16)
        rres = _mn.take([128, 2, 8, IB], F32)
        xbf = _mn.take([128, 8, IB], BF16)
        rcp = _mn.take([1, 1024], BF16)
        stg = _mn.take([128, D], F32)
        sqb = stg[:, 0:512].bitcast(BF16).rearrange("p (a b) -> p a b", a=2)
        rcpB = ptile[:, 0, :]
        abq = _mn.take([128, HB], F32)
        abk = _mn.take([128, HB], F32)
        abv = _mn.take([128, HB], F32)
        abu = _mn.take([128, HB], F32)
        abscr = _mn.take([128, HB], F32)
        absml = _mn.take([128, 8], F32)
        ubf = _mn.take([128, HB], BF16)
        uft = _mn.take([128, HB], BF16)
        qbf = _mn.take([128, HB], BF16)
        qft = _mn.take([128, HB], BF16)
        vsbf = _mn.take([128, HB], BF16)
        pfbf = _mn.take([128, HB], BF16)
        ptbf = _mn.take([128, HB], BF16)
        wms = _mn.take([128, 128], F32)
        prt = _mn.take([128, 32], F32)
        rowb = _mn.take([128, 33], F32)
        rtl = _mn.take([128, 32], F32)
        altc = _mn.take([128, 32], F32)
        ltbf = _mn.take([128, 128], BF16)
        prbf = _mn.take([128, 32], BF16)
        panbf = _mn.take([128, 128], BF16)
        yts = _mn.take([128, HB], F32)
        ytb = _mn.take([128, HB], BF16)
        accb = _mn.take([128, HB], BF16)
        ztb = _mn.take([128, HB], BF16)
        zft = _mn.take([128, HB], BF16)
        nyb = _mn.take([128, HB], BF16)
        pzb = _mn.take([128, 128], BF16)
        obf = _mn.take([128, HB], BF16)

        if _os.environ.get("KSIZE"):
            print("mn pool used:", _mn.off)
        psA = _st.enter_context(nc.psum_tensor("psA", [128, 1024], F32))
        psB = _st.enter_context(nc.psum_tensor("psB", [128, 1024], F32))
        psO = _st.enter_context(nc.psum_tensor("psO", [128, 1024], F32))
        psG = _st.enter_context(nc.psum_tensor("psG", [128, 512], F32))
        psH = _st.enter_context(nc.psum_tensor("psH", [128, 512], F32))
        s_ld = _st.enter_context(nc.semaphore("s_ld"))
        s_g = _st.enter_context(nc.semaphore("s_g"))
        lds = [_st.enter_context(nc.semaphore(f"s_l{i}")) for i in range(6)]
        gds = [_st.enter_context(nc.semaphore(f"s_gd{i}")) for i in range(4)]
        ods = [_st.enter_context(nc.semaphore(f"s_od{i}")) for i in range(2)]
        oms = [_st.enter_context(nc.semaphore(f"s_om{i}")) for i in range(2)]
        s_pe = _st.enter_context(nc.semaphore("s_pe"))
        s_ve = _st.enter_context(nc.semaphore("s_ve"))
        s_sc = _st.enter_context(nc.semaphore("s_sc"))
        blk = _st.enter_context(nc.Block())

        SEM = {"ld": s_ld, "g": s_g, "pe": s_pe, "ve": s_ve, "sc": s_sc}

        def W(eng, cname, val):
            if isinstance(val, Fut):
                val = val.v
            if val is not None and val > 0:
                Q[eng].append(lambda e, s=SEM[cname], v=val: e.wait_ge(s, v))

        def E(eng, fn):
            Q[eng].append(fn)

        def inc(c, n=1):
            CT[c] += n
            return CT[c]

        ld_state = {"rr": 0, "tot": [0] * 6}
        gd_state = {"rr": 0, "tot": [0] * 4}
        om_state = {"tot": [0] * 2}
        od_state = {"tot": [0] * 2}

        def WT(eng, tok):
            if tok is None:
                return
            if isinstance(tok, tuple):
                kind, i, v = tok
                sem = {"L": lds, "G": gds, "O": ods, "M": oms}[kind][i]
                if v > 0:
                    Q[eng].append(lambda e, s=sem, v=v: e.wait_ge(s, v))
            else:
                raise AssertionError(tok)

        def ld(out_ap, in_ap):
            i = ld_state["rr"] % 6
            ld_state["rr"] += 1
            if ld_state["tot"][i] > 0:
                Q["sy"].append(lambda e, s=lds[i], v=ld_state["tot"][i]: e.wait_ge(s, v))
            E("sy", lambda e, o=out_ap, i2=in_ap, s=lds[i]: e.dma_start(out=o, in_=i2)
              .then_inc(s, 16))
            ld_state["tot"][i] += 16
            return ("L", i, ld_state["tot"][i])

        def gdma(out_ap, in_ap):
            i = gd_state["rr"] % 4
            gd_state["rr"] += 1
            if gd_state["tot"][i] > 0:
                Q["gp"].append(lambda e, s=gds[i], v=gd_state["tot"][i]: e.wait_ge(s, v))
            E("gp", lambda e, o=out_ap, i2=in_ap, s=gds[i]: e.dma_start(out=o, in_=i2)
              .then_inc(s, 16))
            gd_state["tot"][i] += 16
            return ("G", i, gd_state["tot"][i])

        TQ = collections.deque()

        def task_w(eng, cname, val):
            return ("w", eng, cname, val)

        def task_i(eng, cname, fn, n=1):
            f = Fut()
            return ("i", eng, cname, fn, f, n), f

        def queue(*entries):
            TQ.append(list(entries))

        def pump(n):
            done = 0
            while TQ and done < n:
                for ent in TQ.popleft():
                    if ent[0] == "w":
                        if isinstance(ent[3], tuple):
                            WT(ent[1], ent[3])
                        else:
                            W(ent[1], ent[2], ent[3])
                    elif ent[0] == "d":
                        E(ent[1], ent[2])
                    else:
                        _, eng, cname, fn, fut, nn = ent
                        E(eng, fn)
                        fut.v = inc(cname, nn)
                done += 1

        def pump_all():
            pump(len(TQ) + 1)

        def pump_until(fut):
            while TQ and fut.v is None:
                pump(1)

        psfree = {"G": 0, "H": 0}

        _ftog = [0]

        def filler(e):
            c = 12 + _ftog[0]
            _ftog[0] ^= 1
            e.tensor_copy(hrow[0:1, c:c + 1], cpool[0:1, 3:4])

        t_bcol = ld(bcol[:].rearrange("p a c -> p (a c)"), bcolh_d[:, :])
        t_bvrow = ld(brow[:, :], bias_d[2:3, :])
        t_log = ld(hrow[:, 0:8], flog_d[:, :])
        t_u = ld(hrow[:, 8:16], fu_d[:, :])
        t_fb = ld(fbmat[:, :], fb_d[:, :])
        t_bab0 = ld(babrow[:, 0:HB], bqb_d[:, :])
        t_bab1 = ld(babrow[:, HB:2 * HB], bkb_d[:, :])
        t_bab = ld(babrow[:, 2 * HB:3 * HB], bvb_d[:, :])
        t_bob = ld(bobc[:], bobch_d[:, :])

        def _constsv(e):
            e.memset(cpool[:, 0:1], -SHIFT)
            e.memset(cpool[:, 1:2], 0.0)
            e.memset(cpool[:, 2:3], EPS_LN)
            e.memset(cpool[:, 3:4], 1.0)
            e.memset(onesrow[:], 1.0)
            e.memset(onescol[:], 1.0)
            e.memset(identb[:], 0.0)
            e.memset(identf[:], 0.0)
            e.memset(ident32[:], 0.0)
            e.memset(masku[:], 1.0)
            for cc in range(32):
                e.memset(altc[:, cc:cc + 1], 1.0 if cc % 2 == 0 else -1.0)
            e.drain().then_inc(s_ve, 1)
        E("ve", _constsv)
        t_constsv = inc("ve", 1)
        W("gp", "ve", t_constsv)
        def _consts(e):
            e.affine_select(out=identb[:], in_=identb[:], compare_op=ALU.not_equal,
                            fill=1.0, base=0, pattern=[[-1, 128]], channel_multiplier=1)
            e.affine_select(out=identf[:], in_=identf[:], compare_op=ALU.not_equal,
                            fill=1.0, base=0, pattern=[[-1, 128]], channel_multiplier=1)
            for q0 in range(0, 128, 32):
                e.affine_select(out=ident32[q0:q0 + 32, :], in_=ident32[q0:q0 + 32, :],
                                compare_op=ALU.not_equal, fill=1.0, base=0,
                                pattern=[[-1, 32]], channel_multiplier=1)
            e.affine_select(out=masku[:], in_=masku[:], compare_op=ALU.is_ge,
                            fill=0.0, base=0, pattern=[[1, 128]], channel_multiplier=-1)
            e.drain().then_inc(s_g, 1)
        E("gp", _consts)
        t_consts = inc("g", 1)

        W("ve", "g", t_consts)
        def _ainit(e):
            e.tensor_copy(a_sb[:, :], identf[:, :])
            e.tensor_copy(a_bf[:, :], identb[:, :])
            e.memset(mt_sb[:, :], 0.0)
            e.memset(mt_bf[:, :], 0.0)
            e.memset(ytb[:, :], 0.0)
            e.drain().then_inc(s_ve, 1)
        E("ve", _ainit)
        inc("ve", 1)

        WT("sc", t_log)
        WT("sc", t_u)
        W("sc", "g", t_consts)
        def _gum(e):
            e.activation(hrow[:, 8:16], hrow[:, 8:16], AF.Ln, bias=cpool[0:1, 1:2], scale=1.0)
            e.drain()
            e.activation(hrow[:, 8:16], hrow[:, 8:16], AF.Ln,
                         bias=cpool[0:1, 1:2], scale=-1.0).then_inc(s_sc, 1)
        E("sc", _gum)
        t_gum = inc("sc", 1)
        W("ve", "sc", t_gum)
        def _hsel(e):
            e.tensor_tensor(hrow[:, 0:8], hrow[:, 0:8], hrow[:, 8:16], ALU.subtract)
            e.drain()
            e.reduce_max(hrow[:, 8:9], hrow[:, 0:8], AX.X)
            e.drain()
            e.tensor_scalar(hrow[:, 0:8], hrow[:, 0:8], hrow[0:1, 8:9], None, ALU.is_ge)
            e.drain()
            e.tensor_copy(hrowb[:, 0:8], hrow[:, 0:8])
            e.drain().then_inc(s_ve, 1)
        E("ve", _hsel)
        t_hsel = inc("ve", 1)

        W("pe", "ve", t_hsel)
        def _hb(e):
            e.matmul(psG[:, 0:8], onesrow[:, :], hrowb[:, 0:8], start=True, stop=True)
            e.drain()
            e.matmul(psH[0:8, 0:128], hrow[:, 0:8], identf[0:1, 0:128],
                     start=True, stop=True).then_inc(s_pe, 1)
            e.drain()
        E("pe", _hb)
        t_hb = inc("pe", 1)
        W("ve", "pe", t_hb)
        def _hbe(e):
            e.tensor_copy(hBc[:, :], psG[:, 0:8])
            e.tensor_copy(hcol[0:8, 0:1], psH[0:8, 0:1])
            e.drain().then_inc(s_ve, 1)
        E("ve", _hbe)
        t_hbe = inc("ve", 1)

        WT("ve", t_fb)
        def _fbc(e):
            e.tensor_copy(fbmatb[:, :], fbmat[:, :]).then_inc(s_ve, 1)
        E("ve", _fbc)
        t_fbc = inc("ve", 1)
        W("pe", "ve", t_hbe)
        W("pe", "ve", t_fbc)
        def _beff(e):
            for eb in range(8):
                mm = e.matmul(psG[:, eb:eb + 1], fbmatb[:, eb * 128:(eb + 1) * 128],
                              hcol[0:8, 0:1], start=True, stop=True)
            mm.then_inc(s_pe, 1)
        E("pe", _beff)
        t_beffp = inc("pe", 1)
        W("ve", "pe", t_beffp)
        def _beffe(e):
            e.tensor_copy(beffc[:, :], psG[:, 0:8]).then_inc(s_ve, 1)
        E("ve", _beffe)
        inc("ve", 1)

        wt_full = wtmp[:].rearrange("p a f -> p (a f)")
        wtb_full = wtmpb[:].rearrange("p a f -> p (a f)")

        def lc(src_ap, out_f, out_b):
            W("sy", "ve", CT["ve"])
            tl = ld(out_f, src_ap)
            WT("ve", tl)
            W("ve", "pe", CT["pe"])
            def _cast(e, out_b=out_b, out_f=out_f):
                e.tensor_copy(out_b, out_f).then_inc(s_ve, 1)
            E("ve", _cast)
            return inc("ve", 1)

        def transpose_blocks(blocks, tc):
            for idx, (src, dst) in enumerate(blocks):
                bank = "G" if idx % 2 == 0 else "H"
                ps = psG if bank == "G" else psH
                W("pe", "ve", tc)
                W("pe", "ve", psfree[bank])
                def _t(e, src=src, ps=ps):
                    e.matmul(ps[:, 0:128], src, identb[:],
                             start=True, stop=True).then_inc(s_pe, 1)
                E("pe", _t)
                tp = inc("pe", 1)
                W("ve", "pe", tp)
                def _e(e, dst=dst, ps=ps):
                    e.tensor_copy(dst, ps[:, 0:128]).then_inc(s_ve, 1)
                E("ve", _e)
                psfree[bank] = inc("ve", 1)
            return CT["ve"]

        def load_weightT(src_d, dst_fn):
            last = 0
            for h2 in range(2):
                tc = lc(src_d.rearrange("(a p) f -> p a f", p=128)[:, 4 * h2:4 * h2 + 4, :],
                        wtmp[:], wtb_full)
                last = transpose_blocks(
                    [(wtmpb[:, a, fb * 128:(fb + 1) * 128], dst_fn(4 * h2 + a, fb))
                     for a in range(4) for fb in range(8)], tc)
            return last

        t_wk = load_weightT(wk_d, lambda a, fb: wkvT[:, fb, a * 128:(a + 1) * 128])
        t_wv = load_weightT(wv_d, lambda a, fb: wkvT[:, 8 + fb, a * 128:(a + 1) * 128])
        t_wq = load_weightT(wq_d, lambda a, fb: wqT[:, fb, a * 128:(a + 1) * 128])
        t_wo = load_weightT(wo_d, lambda a, fb: woT[:, fb, a * 128:(a + 1) * 128])
        for slot, wsrc in enumerate((wqb_d, wkb_d, wvb_d)):
            tc = lc(wsrc[:, :], wtmp[:, 0, :], wtmpb[:, 0, :])
            t_wb = transpose_blocks(
                [(wtmpb[:, 0, a * 128:(a + 1) * 128], wbT[:, a, slot * 128:(slot + 1) * 128])
                 for a in range(8)], tc)
        tc = lc(wob_d.rearrange("(a p) h -> p a h", p=128)[:, 0:4, :], wtmp[:, 0:4, 0:HB],
                wtmpb[:, 0:4, 0:HB])
        t_wob = transpose_blocks([(wtmpb[:, a, 0:HB], wobT[:, a, :]) for a in range(4)], tc)
        tc = lc(wob_d.rearrange("(a p) h -> p a h", p=128)[:, 4:8, :], wtmp[:, 0:4, 0:HB],
                wtmpb[:, 0:4, 0:HB])
        t_wob = transpose_blocks([(wtmpb[:, a, 0:HB], wobT[:, 4 + a, :]) for a in range(4)], tc)

        WT("ve", t_bvrow)
        def _bvc(e):
            e.tensor_copy(browb[:, :], brow[:, :]).then_inc(s_ve, 1)
        E("ve", _bvc)
        t_bvc = inc("ve", 1)
        W("pe", "ve", t_bvc)
        W("pe", "ve", psfree["G"])
        W("pe", "ve", psfree["H"])
        def _bvb(e):
            e.matmul(psG[:, :], onesrow[:, :], browb[0:1, 0:512], start=True, stop=True)
            e.matmul(psH[:, :], onesrow[:, :], browb[0:1, 512:1024],
                     start=True, stop=True).then_inc(s_pe, 1)
        E("pe", _bvb)
        t_bvb = inc("pe", 1)
        W("ve", "pe", t_bvb)
        def _bvbe(e):
            e.tensor_copy(bvB[:, 0:512], psG[:, :])
            e.tensor_copy(bvB[:, 512:1024], psH[:, :])
            e.drain().then_inc(s_ve, 1)
        E("ve", _bvbe)
        psfree["G"] = psfree["H"] = inc("ve", 1)

        WT("ve", t_bab0)
        WT("ve", t_bab1)
        WT("ve", t_bab)
        def _babc(e):
            e.tensor_copy(babrowb[:, 0:3 * HB], babrow[:, 0:3 * HB]).then_inc(s_ve, 1)
        E("ve", _babc)
        t_babc = inc("ve", 1)
        W("pe", "ve", t_babc)
        W("pe", "ve", psfree["G"])
        def _babb(e):
            e.matmul(psG[:, 0:128], onesrow[:, :], babrowb[0:1, 0:128], start=True, stop=True)
            e.matmul(psG[:, 128:256], onesrow[:, :], babrowb[0:1, 128:256], start=True, stop=True)
            e.matmul(psG[:, 256:384], onesrow[:, :], babrowb[0:1, 256:384],
                     start=True, stop=True).then_inc(s_pe, 1)
        E("pe", _babb)
        t_babb = inc("pe", 1)
        W("ve", "pe", t_babb)
        def _babe(e):
            e.tensor_copy(bqkvB[:].rearrange("p c h -> p (c h)"),
                          psG[:, 0:384]).then_inc(s_ve, 1)
        E("ve", _babe)
        psfree["G"] = inc("ve", 1)

        W("pe", "ve", t_wb)
        W("pe", "ve", psfree["G"])
        def _ncb(e):
            for fb in range(8):
                mm = e.matmul(psG[0:1, 0:384], onescol[:, :], wbT[:, fb, :],
                              start=(fb == 0), stop=(fb == 7))
            mm.then_inc(s_pe, 1)
        E("pe", _ncb)
        t_ncb = inc("pe", 1)
        W("ve", "pe", t_ncb)
        def _ncbe(e):
            e.tensor_scalar(negcsB[:, :], psG[0:1, 0:384], -1.0, None,
                            ALU.mult).then_inc(s_ve, 1)
        E("ve", _ncbe)
        psfree["G"] = inc("ve", 1)
        t_allcasts = CT["ve"]

        W("gp", "ve", max(t_allcasts, t_hbe))
        t_cmb = 0
        for h2 in range(2):
            for i in range(NEX):
                W("gp", "ve", t_cmb)
                t_exp = gdma(wtmp[:], fw_d[i].rearrange("(a p) f -> p a f", p=128)[
                    :, 4 * h2:4 * h2 + 4, :])
                WT("ve", t_exp)
                if i == 0:
                    def _c0(e):
                        e.tensor_scalar(eacc[:].rearrange("p a f -> p (a f)"), wt_full,
                                        hBc[:, 0:1], None, ALU.mult)
                        e.drain().then_inc(s_ve, 1)
                    E("ve", _c0)
                else:
                    def _ci(e, i=i):
                        e.scalar_tensor_tensor(eacc[:].rearrange("p a f -> p (a f)"), wt_full,
                                               hBc[:, i:i + 1],
                                               eacc[:].rearrange("p a f -> p (a f)"),
                                               ALU.mult, ALU.add)
                        e.drain().then_inc(s_ve, 1)
                    E("ve", _ci)
                t_cmb = inc("ve", 1)
            def _ecast(e, h2=h2):
                e.tensor_copy(weff[:, 4 * h2:4 * h2 + 4, :].rearrange("p a f -> p (a f)"),
                              eacc[:].rearrange("p a f -> p (a f)"))
                e.drain().then_inc(s_ve, 1)
            E("ve", _ecast)
            t_cmb = inc("ve", 1)
        t_weffv = t_cmb

        W("pe", "ve", t_weffv)
        W("pe", "ve", psfree["G"])
        W("pe", "ve", psfree["H"])
        def _ncw(e):
            for fb in range(8):
                e.matmul(psG[0:1, 0:512], onescol[:, :], weff[:, fb, 0:512],
                         start=(fb == 0), stop=(fb == 7))
            for fb in range(8):
                mm = e.matmul(psH[0:1, 0:512], onescol[:, :], weff[:, fb, 512:1024],
                              start=(fb == 0), stop=(fb == 7))
            mm.then_inc(s_pe, 1)
        E("pe", _ncw)
        t_ncw = inc("pe", 1)
        W("ve", "pe", t_ncw)
        def _ncwe(e):
            e.tensor_scalar(negcsW[:, 0:512], psG[0:1, :], -1.0, None, ALU.mult)
            e.tensor_scalar(negcsW[:, 512:1024], psH[0:1, :], -1.0, None, ALU.mult)
            e.drain().then_inc(s_ve, 1)
        E("ve", _ncwe)
        psfree["G"] = psfree["H"] = inc("ve", 1)

        def emit_ln1_tile(tt, t_load):
            WT("ve", t_load)
            def _stats(e, tt=tt):
                e.tensor_tensor(xlnf[:, :], xt[:, :], xt[:, :], ALU.mult)
                e.reduce_sum(lnst[:, tt, 0:1], xt[:, :], AX.X)
                e.drain()
                e.reduce_sum(lnst[:, tt, 1:2], xlnf[:, :], AX.X)
                e.tensor_scalar(lnst[:, tt, 0:1], lnst[:, tt, 0:1], INV_D, None, ALU.mult)
                e.drain()
                e.tensor_scalar(xlnf[:, 0:1], lnst[:, tt, 1:2], INV_D, None, ALU.mult)
                e.drain()
                e.scalar_tensor_tensor(xlnf[:, 1:2], lnst[:, tt, 0:1], lnst[:, tt, 0:1],
                                       xlnf[:, 0:1], ALU.mult, ALU.subtract)
                e.drain().then_inc(s_ve, 1)
            E("ve", _stats)
            tv = inc("ve", 1)
            W("sc", "ve", tv)
            def _rstd(e, tt=tt):
                e.activation(xlnf[:, 2:3], xlnf[:, 1:2], AF.Ln, bias=cpool[:, 2:3], scale=-1.0)
                e.drain()
                e.activation(lnst[:, tt, 1:2], xlnf[:, 2:3], AF.Exp,
                             bias=cpool[:, 1:2], scale=-0.5).then_inc(s_sc, 1)
            E("sc", _rstd)
            ts_ = inc("sc", 1)
            W("ve", "sc", ts_)
            def _aff(e, tt=tt):
                e.tensor_scalar(xlnf[:, :], xt[:, :], lnst[:, tt, 0:1], lnst[:, tt, 1:2],
                                ALU.subtract, ALU.mult)
                e.drain()
                e.tensor_copy(xlnb[:, :], xlnf[:, :])
                e.drain().then_inc(s_ve, 1)
            E("ve", _aff)
            return inc("ve", 1)

        def xlnT_transposes(k, ta, regions):
            outs = []
            for (ps_reg, gate, set_free, half) in regions:
                W("pe", "ve", ta)
                W("pe", "ve", gate)
                def _tr(e, ps_reg=ps_reg, half=half):
                    for q2 in range(4):
                        fb = half * 4 + q2
                        mm = e.matmul(ps_reg[:, q2 * 128:(q2 + 1) * 128],
                                      xlnb[:, fb * 128:(fb + 1) * 128], identb[:],
                                      start=True, stop=True)
                    mm.then_inc(s_pe, 1)
                E("pe", _tr)
                tp = inc("pe", 1)
                W("ve", "pe", tp)
                def _ev(e, k=k, ps_reg=ps_reg, half=half):
                    e.tensor_copy(xlnT[:, half * 4:half * 4 + 4, k * 128:(k + 1) * 128],
                                  ps_reg[:, 0:512].rearrange("p (q f) -> p q f", q=4))
                    e.drain().then_inc(s_ve, 1)
                E("ve", _ev)
                tv = inc("ve", 1)
                set_free(tv)
                outs.append(tv)
            return max(outs)

        def _setG(v):
            psfree["G"] = v

        def _setH(v):
            psfree["H"] = v

        ksb_free = None
        vsb_free = None
        for g in range(8):
            for k in range(4):
                tt = 4 * g + k
                W("sy", "pe", CT["pe"])
                tl = ld(xt, x_in.rearrange("(n p) f -> n p f", p=128)[tt])
                ta = emit_ln1_tile(tt, tl)
                xlnT_transposes(k, ta, [
                    (psG[:, :], psfree["G"], _setG, 0),
                    (psH[:, :], psfree["H"], _setH, 1),
                ])
            t_xlnT = CT["ve"]
            W("pe", "ve", t_xlnT)
            W("pe", "ve", t_wk)
            for rb in range(8):
                bank = "G" if rb % 2 == 0 else "H"
                ps = psG if bank == "G" else psH
                W("pe", "ve", psfree[bank])
                def _km(e, rb=rb, ps=ps):
                    for fb in range(8):
                        mm = e.matmul(ps[:, :], wkvT[:, fb, rb * 128:(rb + 1) * 128],
                                      xlnT[:, fb, :], start=(fb == 0), stop=(fb == 7))
                    mm.then_inc(s_pe, 1)
                E("pe", _km)
                tp = inc("pe", 1)
                W("ve", "pe", tp)
                WT("ve", ksb_free)
                def _kev(e, rb=rb, ps=ps):
                    e.tensor_scalar(ksb[:, rb, :], ps[:, :], bcol[:, rb, 1:2], None,
                                    ALU.add).then_inc(s_ve, 1)
                E("ve", _kev)
                psfree[bank] = inc("ve", 1)
            t_kdone = CT["ve"]
            W("pe", "ve", t_wv)
            for k in range(4):
                for dv in range(2):
                    bank = "G" if (k * 2 + dv) % 2 == 0 else "H"
                    ps = psG if bank == "G" else psH
                    W("pe", "ve", psfree[bank])
                    def _vm(e, k=k, dv=dv, ps=ps):
                        for fb in range(8):
                            mm = e.matmul(ps[:, :], xlnT[:, fb, k * 128:(k + 1) * 128],
                                          wkvT[:, 8 + fb, dv * 512:(dv + 1) * 512],
                                          start=(fb == 0), stop=(fb == 7))
                        mm.then_inc(s_pe, 1)
                    E("pe", _vm)
                    tp = inc("pe", 1)
                    W("ve", "pe", tp)
                    WT("ve", vsb_free)
                    def _vev(e, k=k, dv=dv, ps=ps):
                        e.tensor_tensor(vsb[:, k, dv * 512:(dv + 1) * 512], ps[:, :],
                                        bvB[:, dv * 512:(dv + 1) * 512],
                                        ALU.add).then_inc(s_ve, 1)
                    E("ve", _vev)
                    psfree[bank] = inc("ve", 1)
            t_vdone = CT["ve"]
            W("gp", "ve", t_kdone)
            ksb_free = gdma(
                kt_scr.rearrange("(rb p) t -> p rb t", p=128)[:, :, g * 512:(g + 1) * 512],
                ksb[:, :, :])
            W("gp", "ve", t_vdone)
            vsb_free = gdma(
                v_scr.rearrange("(n p) d -> p n d", p=128)[:, 4 * g:4 * g + 4, :],
                vsb[:, :, :])
        t_s1_ve = CT["ve"]

        def _vstones(e):
            e.memset(vst[:, 0, :, 64:65], 1.0)
            e.memset(vst[:, 0, :, 129:130], 1.0)
            e.memset(vst[:, 1, :, 64:65], 1.0)
            e.memset(vst[:, 1, :, 129:130], 1.0)
            e.drain().then_inc(s_g, 1)
        W("gp", "ve", t_s1_ve)
        E("gp", _vstones)
        t_vstones = inc("g", 1)
        t_weff = t_vstones

        kst_free = [0, 0]
        vst_free = [0, 0]
        t_kst = {}
        t_vst = {}

        def stream_loads(idx):
            ib, hp = divmod(idx, HP)
            buf = idx % 2
            WT("sy", ksb_free)
            WT("sy", vsb_free)
            W("sy", "g", t_weff)
            W("sy", "pe", kst_free[buf])
            t_kst[idx] = ld(kst[:, buf, :], kt_scr.rearrange("(hh p) t -> hh p t", p=128)[hp])
            W("sy", "pe", vst_free[buf])
            vv = vst[:, buf].rearrange("p j (hh c) -> p j hh c", hh=2)
            src3 = v_scr.rearrange("(j p) (h d) -> p j h d", p=128, d=64)
            tv0 = ld(vv[:, :, 0, 0:64], src3[:, :, 2 * hp, :])
            tv1 = ld(vv[:, :, 1, 0:64], src3[:, :, 2 * hp + 1, :])
            t_vst[idx] = (tv0, tv1)

        stream_loads(0)
        stream_loads(1)

        hist = {
            "tail_stg_pe": [None] * NIB,
            "proj_pe": [None] * NIB,
            "omov": [None] * NIB,
            "qm_last": 0,
            "exp_last": 0,
        }

        def queue_lnfm(slot, prev_done, rbuf, prev_out):
            mr1 = mr1s[0:1, slot, :]
            mr2 = mr2t[:, :]
            eg, fg = task_i("gp", "g", lambda e, rbuf=rbuf: (
                [e.tensor_copy(xbf[:, fb, :],
                               rres[:, rbuf, fb, :]).then_inc(s_g, 1)
                 for fb in range(8)]), n=8)
            queue(task_w("gp", "ve", prev_done), eg)
            e1, f1 = task_i("pe", "pe", lambda e: (
                [e.matmul(psG[0:1, :], onescol[:, :], xbf[:, fb, :],
                          start=(fb == 0), stop=(fb == 7)) for fb in range(8)][-1]
                .then_inc(s_pe, 1)))
            queue(task_w("pe", "g", fg), task_w("pe", "ve", psfree["G"]), e1)
            psfree["G"] = None
            mm2_f = None
            sqv = [None, None]
            for fb in range(8):
                ev, fv = task_i("ve", "ve", lambda e, fb=fb: (
                    e.tensor_tensor(sqb[:, fb % 2, :], xbf[:, fb, :], xbf[:, fb, :],
                                    ALU.mult).then_inc(s_ve, 1)))
                ents = [task_w("ve", "g", fg)]
                if prev_out is not None and fb < 2:
                    ents.append(task_w("ve", "g", prev_out))
                if sqv[fb % 2] is not None:
                    ents.append(task_w("ve", "pe", mm2_f))
                ents.append(ev)
                queue(*ents)
                sqv[fb % 2] = fv
                wts = [task_w("pe", "ve", fv)]
                if fb == 0:
                    wts.append(task_w("pe", "ve", psfree["H"]))
                em_, fm_ = task_i("pe", "pe", lambda e, fb=fb: (
                    e.matmul(psH[0:1, :], onescol[:, :], sqb[:, fb % 2, :],
                             start=(fb == 0), stop=(fb == 7)).then_inc(s_pe, 1)))
                queue(*wts, em_)
                mm2_f = fm_
            psfree["H"] = None
            er, fr = task_i("ve", "ve", lambda e: (
                e.tensor_scalar(srow[0:1, 0, :], psG[0:1, :], INV_D, None, ALU.mult),
                e.tensor_scalar(srow[0:1, 1, :], psH[0:1, :], INV_D, None, ALU.mult),
                e.drain(),
                e.tensor_tensor(srow[0:1, 2, :], srow[0:1, 0, :], srow[0:1, 0, :], ALU.mult),
                e.drain(),
                e.tensor_tensor(srow[0:1, 1, :], srow[0:1, 1, :], srow[0:1, 2, :],
                                ALU.subtract).then_inc(s_ve, 1)))
            queue(task_w("ve", "pe", mm2_f), er)
            psfree["G"] = fr
            psfree["H"] = fr
            es, fs = task_i("sc", "sc", lambda e: (
                e.activation(srow[0:1, 2, :], srow[0:1, 1, :], AF.Ln,
                             bias=cpool[0:1, 2:3], scale=1.0),
                e.drain(),
                e.activation(srow[0:1, 1, :], srow[0:1, 2, :], AF.Exp,
                             bias=cpool[0:1, 1:2], scale=-0.5).then_inc(s_sc, 1)))
            queue(task_w("sc", "ve", fr), es)
            em2, fm2v = task_i("ve", "ve", lambda e, mr1=mr1, mr2=mr2: (
                e.tensor_tensor(srow[0:1, 2, :], srow[0:1, 0, :], srow[0:1, 1, :], ALU.mult),
                e.tensor_copy(mr2, srow[0:1, 1, :]),
                e.drain(),
                e.tensor_copy(mr1, srow[0:1, 2, :]).then_inc(s_ve, 1)))
            queue(task_w("ve", "sc", fs), em2)
            eb_, fb_ = task_i("pe", "pe", lambda e, mr2=mr2: (
                e.matmul(psG[:, :], onesrow[:, :], mr2,
                         start=True, stop=True).then_inc(s_pe, 1)))
            queue(task_w("pe", "ve", fm2v), task_w("pe", "ve", psfree["G"]), eb_)
            ea, fa = task_i("ve", "ve", lambda e, rbuf=rbuf: (
                [e.tensor_tensor(xbf[:, fb, :], rres[:, rbuf, fb, :],
                                 psG[:, :], ALU.mult).then_inc(s_ve, 1)
                 for fb in range(8)]), n=8)
            queue(task_w("ve", "pe", fb_), ea)
            psfree["G"] = fa
            return fa

        def queue_ffn(mr1, rbuf, ready):
            last = None
            for eb in range(8):
                bank = "G" if eb % 2 == 0 else "H"
                ps = psG if bank == "G" else psH
                e1, f1 = task_i("pe", "pe", lambda e, eb=eb, ps=ps, mr1=mr1: (
                    [e.matmul(ps[:, :], weff[:, fb, eb * 128:(eb + 1) * 128], xbf[:, fb, :],
                              start=(fb == 0), stop=False) for fb in range(8)],
                    e.matmul(ps[:, :], negcsW[0:1, eb * 128:(eb + 1) * 128], mr1,
                             start=False, stop=True).then_inc(s_pe, 1)))
                e2, f2 = task_i("ve", "ve", lambda e, eb=eb, ps=ps, rbuf=rbuf: (
                    e.scalar_tensor_tensor(rres[:, rbuf, eb, :], ps[:, :], beffc[:, eb:eb + 1],
                                           rres[:, rbuf, eb, :], ALU.add,
                                           ALU.add).then_inc(s_ve, 1)))
                queue(task_w("pe", "ve", ready), task_w("pe", "ve", psfree[bank]),
                      task_w("pe", "g", t_weff), e1, task_w("ve", "pe", f1), e2)
                psfree[bank] = f2
                last = f2
            return last

        def queue_chunk(rbuf, c, ln3f, prev_pe):
            cs = c * 128
            fv2 = None
            for slot, dstf in ((0, abq), (1, abk), (2, abv)):
                bank = "G" if slot % 2 == 0 else "H"
                ps = psG if bank == "G" else psH
                e1, f1 = task_i("pe", "pe", lambda e, slot=slot, ps=ps, cs=cs: (
                    [e.matmul(ps[:, 0:128], xbf[:, fb, cs:cs + 128],
                              wbT[:, fb, slot * 128:(slot + 1) * 128],
                              start=(fb == 0), stop=False) for fb in range(8)],
                    e.matmul(ps[:, 0:128], mr1s[0:1, 1, cs:cs + 128],
                             negcsB[0:1, slot * 128:(slot + 1) * 128],
                             start=False, stop=True).then_inc(s_pe, 1)))
                e2, f2 = task_i("ve", "ve", lambda e, slot=slot, ps=ps, dstf=dstf: (
                    e.tensor_tensor(dstf[:, :], ps[:, 0:128], bqkvB[:, slot, :],
                                    ALU.add).then_inc(s_ve, 1)))
                ents = [task_w("pe", "ve", ln3f), task_w("pe", "ve", psfree[bank])]
                if prev_pe is not None:
                    ents.append(task_w("pe", "ve", prev_pe))
                queue(*ents, e1, task_w("ve", "pe", f1), e2)
                psfree[bank] = f2
                if slot == 2:
                    fv2 = f2
            if KCH <= 1:
                return fv2

            ev1, fv1 = task_i("ve", "ve", lambda e: (
                e.tensor_tensor(abscr[:, :], abk[:, :], abk[:, :], ALU.mult),
                e.tensor_tensor(yts[:, :], abk[:, :], abq[:, :], ALU.mult),
                e.drain(),
                e.reduce_sum(absml[:, 0:1], abscr[:, :], AX.X).then_inc(s_ve, 1),
                e.reduce_sum(absml[:, 1:2], yts[:, :], AX.X).then_inc(s_ve, 1)), n=2)
            queue(task_w("ve", "ve", fv2), ev1)
            es1, fs1 = task_i("sc", "sc", lambda e: (
                e.activation(absml[:, 2:3], absml[:, 0:1], AF.Ln, bias=cpool[:, 1:2], scale=1.0),
                e.drain(),
                e.activation(absml[:, 2:3], absml[:, 2:3], AF.Exp,
                             bias=cpool[:, 1:2], scale=0.5).then_inc(s_sc, 1)))
            queue(task_w("sc", "ve", fv1), es1)
            ev2, fv2b = task_i("ve", "ve", lambda e: (
                e.tensor_scalar(absml[:, 3:4], absml[:, 2:3], EPS_B, None, ALU.add),
                e.tensor_scalar(vsbf[:, :], abv[:, :], absml[:, 1:2], None, ALU.mult),
                e.tensor_copy(qbf[:, :], abq[:, :]),
                e.drain(),
                e.reciprocal(absml[:, 3:4], absml[:, 3:4]),
                e.drain(),
                e.tensor_scalar(abu[:, :], abk[:, :], absml[:, 3:4], None,
                                ALU.mult).then_inc(s_ve, 1),
                e.tensor_scalar(ubf[:, :], abk[:, :], absml[:, 3:4], None,
                                ALU.mult).then_inc(s_ve, 1)), n=2)
            queue(task_w("ve", "sc", fs1), ev2)

            e3, f3 = task_i("pe", "pe", lambda e: (
                e.matmul(psG[:, 0:128], ubf[:, :], identb[:], start=True, stop=True),
                e.matmul(psG[:, 128:256], qbf[:, :], identb[:],
                         start=True, stop=True).then_inc(s_pe, 1)))
            e4, f4 = task_i("ve", "ve", lambda e: (
                e.tensor_copy(uft[:, :], psG[:, 0:128]).then_inc(s_ve, 1),
                e.tensor_copy(qft[:, :], psG[:, 128:256]).then_inc(s_ve, 1)), n=2)
            queue(task_w("pe", "ve", fv2b), task_w("pe", "ve", psfree["G"]), e3,
                  task_w("ve", "pe", f3), e4)
            psfree["G"] = f4

            e5, f5 = task_i("pe", "pe", lambda e: (
                e.matmul(psH[:, 0:128], uft[:, :], a_bf[:, :], start=True, stop=True),
                e.matmul(psH[:, 128:256], a_bf[:, :], uft[:, :],
                         start=True, stop=True).then_inc(s_pe, 1)))
            e6, f6 = task_i("ve", "ve", lambda e: (
                e.tensor_copy(ptbf[:, :], psH[:, 0:128]).then_inc(s_ve, 1),
                e.tensor_copy(pfbf[:, :], psH[:, 128:256]).then_inc(s_ve, 1)), n=2)
            queue(task_w("pe", "ve", f4), task_w("pe", "ve", psfree["H"]), e5,
                  task_w("ve", "pe", f5), e6)
            psfree["H"] = f6

            e7, f7 = task_i("pe", "pe", lambda e: (
                e.matmul(psG[:, 0:128], pfbf[:, :], uft[:, :],
                         start=True, stop=True).then_inc(s_pe, 1)))
            queue(task_w("pe", "ve", f6), task_w("pe", "ve", psfree["G"]), e7)

            e8, f8 = task_i("ve", "ve", lambda e: (
                e.scalar_tensor_tensor(wms[:, :], identf[:, :], -1.0, psG[:, 0:128],
                                       ALU.mult, ALU.subtract).then_inc(s_ve, 1),
                e.tensor_copy(prt[:, :], ident32[:, :]).then_inc(s_ve, 1),
                e.memset(ltbf[:, :], 0.0).then_inc(s_ve, 1)), n=3)
            queue(task_w("ve", "pe", f7), e8)
            psfree["G"] = f8
            if KCH <= 2:
                return f8

            yf = [None] * 4
            dvf = [None] * 4
            leaf_gate = f8
            for jl in range(4):
                q0 = jl * 32
                for (g0, g1) in ((0, 4), (4, 8), (8, 12), (12, 16),
                                 (16, 20), (20, 24), (24, 28), (28, 32)):
                    def _leafgrp(e, q0=q0, g0=g0, g1=g1):
                        for i in range(g0, g1):
                            e.stream_shuffle(rowb[q0:q0 + 32, i:32],
                                             wms[q0:q0 + 32, q0 + i:q0 + 32], [i] * 32)
                            e.drain()
                            e.reciprocal(absml[q0:q0 + 32, 6:7],
                                         rowb[q0:q0 + 32, i:i + 1])
                            e.drain()
                            e.tensor_tensor(absml[q0:q0 + 32, 7:8],
                                            wms[q0:q0 + 32, q0 + i:q0 + i + 1],
                                            absml[q0:q0 + 32, 6:7], ALU.mult)
                            if i < 31:
                                e.tensor_tensor(rowb[q0:q0 + 32, 32:33],
                                                prt[q0:q0 + 32, i:i + 1],
                                                absml[q0:q0 + 32, 6:7], ALU.mult)
                            e.drain()
                            e.scalar_tensor_tensor(wms[q0:q0 + 32, q0 + i:q0 + 32],
                                                   rowb[q0:q0 + 32, i:32],
                                                   absml[q0:q0 + 32, 7:8],
                                                   wms[q0:q0 + 32, q0 + i:q0 + 32],
                                                   ALU.mult, ALU.subtract)
                            if i < 31:
                                e.scalar_tensor_tensor(prt[q0:q0 + 32, i + 1:32],
                                                       rowb[q0:q0 + 32, i + 1:32],
                                                       rowb[q0:q0 + 32, 32:33],
                                                       prt[q0:q0 + 32, i + 1:32],
                                                       ALU.mult, ALU.subtract)
                            if i < g1 - 1:
                                e.drain()
                        e.drain().then_inc(s_ve, 1)
                    el, fl = task_i("ve", "ve", _leafgrp)
                    if g0 == 0:
                        queue(task_w("ve", "ve", leaf_gate), el)
                    else:
                        queue(el)
                if KCH <= 3:
                    continue
                ec, fc = task_i("ve", "ve", lambda e, q0=q0: (
                    e.tensor_tensor(prbf[q0:q0 + 32, :], prt[q0:q0 + 32, :],
                                    altc[q0:q0 + 32, :], ALU.mult).then_inc(s_ve, 1),
                    (e.tensor_copy(panbf[q0:q0 + 32, q0 + 32:128],
                                   wms[q0:q0 + 32, q0 + 32:128]).then_inc(s_ve, 1)
                     if q0 + 32 < 128 else e.engine_nop().then_inc(s_ve, 1))), n=2)
                queue(ec)

                if jl == 0:
                    em1, fm1 = task_i("pe", "pe", lambda e: (
                        e.matmul(psH[0:32, 0:HB], prbf[0:32, :], ptbf[0:32, :],
                                 start=True, stop=True).then_inc(s_pe, 1)))
                    queue(task_w("pe", "ve", fc), task_w("pe", "ve", psfree["H"]), em1)
                    ey, fy = task_i("ve", "ve", lambda e: (
                        e.tensor_copy(yts[0:32, :], psH[0:32, 0:HB]).then_inc(s_ve, 1),
                        e.tensor_copy(ytb[0:32, :], psH[0:32, 0:HB]).then_inc(s_ve, 1)),
                        n=2)
                    queue(task_w("ve", "pe", fm1), ey)
                    psfree["H"] = fy
                    yf[0] = fy
                else:
                    em1, fm1 = task_i("pe", "pe", lambda e, jl=jl, q0=q0: (
                        e.matmul(psH[0:32, 0:HB], ltbf[:, q0:q0 + 32], ytb[:, :],
                                 start=True, stop=True).then_inc(s_pe, 1)))
                    queue(task_w("pe", "ve", fc), task_w("pe", "ve", psfree["H"]), em1)
                    ea2, fa2 = task_i("ve", "ve", lambda e, q0=q0: (
                        e.tensor_tensor(accb[q0:q0 + 32, :], ptbf[q0:q0 + 32, :],
                                        psH[0:32, 0:HB], ALU.add).then_inc(s_ve, 1)))
                    queue(task_w("ve", "pe", fm1), ea2)
                    em2, fm2 = task_i("pe", "pe", lambda e, q0=q0: (
                        e.matmul(psH[0:32, 0:HB], prbf[q0:q0 + 32, :], accb[q0:q0 + 32, :],
                                 start=True, stop=True,
                                 tile_position=(q0, 0)).then_inc(s_pe, 1)))
                    queue(task_w("pe", "ve", fa2), em2)
                    ey, fy = task_i("ve", "ve", lambda e, q0=q0: (
                        e.tensor_copy(yts[q0:q0 + 32, :], psH[0:32, 0:HB]).then_inc(s_ve, 1),
                        e.tensor_copy(ytb[q0:q0 + 32, :], psH[0:32, 0:HB]).then_inc(s_ve, 1)),
                        n=2)
                    queue(task_w("ve", "pe", fm2), ey)
                    psfree["H"] = fy
                    yf[jl] = fy

                ed, fd = task_i("ve", "ve", lambda e, q0=q0: (
                    e.tensor_tensor(abscr[q0:q0 + 32, :], abu[q0:q0 + 32, :],
                                    yts[q0:q0 + 32, :], ALU.mult),
                    e.drain(),
                    e.reduce_sum(absml[q0:q0 + 32, 4:5], abscr[q0:q0 + 32, :], AX.X),
                    e.drain(),
                    e.tensor_scalar(absml[q0:q0 + 32, 4:5], absml[q0:q0 + 32, 4:5], 1.0,
                                    None, ALU.add),
                    e.drain(),
                    e.reciprocal(absml[q0:q0 + 32, 5:6],
                                 absml[q0:q0 + 32, 4:5]).then_inc(s_ve, 1)))
                queue(task_w("ve", "ve", yf[jl]), ed)
                dvf[jl] = fd

                if q0 + 32 < 128:
                    wdt = 128 - q0 - 32
                    em3, fm3 = task_i("pe", "pe", lambda e, q0=q0, wdt=wdt: (
                        e.matmul(psG[0:32, 0:wdt], prbf[q0:q0 + 32, :],
                                 panbf[q0:q0 + 32, q0 + 32:128],
                                 start=True, stop=True).then_inc(s_pe, 1)))
                    queue(task_w("pe", "ve", fc), task_w("pe", "ve", psfree["G"]), em3)
                    ep, fp = task_i("ve", "ve", lambda e, q0=q0, wdt=wdt: (
                        e.tensor_scalar(ltbf[q0:q0 + 32, q0 + 32:128], psG[0:32, 0:wdt],
                                        absml[q0:q0 + 32, 5:6], None,
                                        ALU.mult).then_inc(s_ve, 1),
                        e.tensor_copy(panbf[q0:q0 + 32, q0 + 32:128],
                                      psG[0:32, 0:wdt]).then_inc(s_ve, 1)), n=2)
                    queue(task_w("ve", "pe", fm3), task_w("ve", "ve", fd), ep)
                    psfree["G"] = fp
                    em4, fm4 = task_i("pe", "pe", lambda e, q0=q0, wdt=wdt: (
                        e.matmul(psG[0:wdt, 0:wdt], ltbf[q0:q0 + 32, q0 + 32:128],
                                 panbf[q0:q0 + 32, q0 + 32:128],
                                 start=True, stop=True).then_inc(s_pe, 1)))
                    queue(task_w("pe", "ve", fp), em4)
                    base = q0 + 32
                    es2, fs2 = task_i("ve", "ve", lambda e, base=base, wdt=wdt: (
                        [e.tensor_tensor(wms[qq:qq + 32, base:128], wms[qq:qq + 32, base:128],
                                         psG[qq - base:qq - base + 32, 0:wdt],
                                         ALU.add).then_inc(s_ve, 1)
                         for qq in range(base, 128, 32)]), n=(128 - base) // 32)
                    queue(task_w("ve", "pe", fm4), es2)
                    psfree["G"] = fs2
                    leaf_gate = fs2

            if KCH <= 4:
                return fl
            ez, fz = task_i("ve", "ve", lambda e: (
                e.tensor_scalar(ztb[:, :], yts[:, :], absml[:, 5:6], None,
                                ALU.mult).then_inc(s_ve, 1),
                e.tensor_scalar(nyb[:, :], yts[:, :], -1.0, None,
                                ALU.mult).then_inc(s_ve, 1)), n=2)
            queue(task_w("ve", "ve", dvf[3]), ez)

            e9, f9 = task_i("pe", "pe", lambda e: (
                e.matmul(psG[:, 0:128], ztb[:, :], identb[:],
                         start=True, stop=True).then_inc(s_pe, 1)))
            queue(task_w("pe", "ve", fz), task_w("pe", "ve", psfree["G"]), e9)
            e10, f10 = task_i("ve", "ve", lambda e: (
                e.tensor_copy(zft[:, :], psG[:, 0:128]).then_inc(s_ve, 1)))
            queue(task_w("ve", "pe", f9), e10)
            psfree["G"] = f10

            e11, f11 = task_i("pe", "pe", lambda e: (
                e.matmul(psG[:, 0:128], zft[:, :], qft[:, :],
                         start=True, stop=True).then_inc(s_pe, 1)))
            queue(task_w("pe", "ve", f10), e11)
            e12, f12 = task_i("ve", "ve", lambda e: (
                e.tensor_tensor(pzb[:, :], psG[:, 0:128], masku[:, :],
                                ALU.mult).then_inc(s_ve, 1)))
            queue(task_w("ve", "pe", f11), e12)
            psfree["G"] = f12

            e13, f13 = task_i("pe", "pe", lambda e: (
                e.matmul(psH[:, 0:128], mt_bf[:, :], qft[:, :], start=True, stop=False),
                e.matmul(psH[:, 0:128], vsbf[:, :], pzb[:, :],
                         start=False, stop=True).then_inc(s_pe, 1)))
            queue(task_w("pe", "ve", f12), task_w("pe", "ve", psfree["H"]), e13)
            e14, f14 = task_i("ve", "ve", lambda e: (
                e.tensor_copy(obf[:, :], psH[:, 0:128]).then_inc(s_ve, 1)))
            queue(task_w("ve", "pe", f13), e14)
            psfree["H"] = f14

            for fb2 in range(8):
                bank = "G" if fb2 % 2 == 0 else "H"
                ps = psG if bank == "G" else psH
                ep1, fp1 = task_i("pe", "pe", lambda e, fb2=fb2, ps=ps: (
                    e.matmul(ps[:, 0:128], wobT[:, fb2, :], obf[:, :],
                             start=True, stop=True).then_inc(s_pe, 1)))
                ep2, fp2 = task_i("ve", "ve", lambda e, fb2=fb2, ps=ps, rbuf=rbuf, cs=cs: (
                    e.scalar_tensor_tensor(rres[:, rbuf, fb2, cs:cs + 128], ps[:, 0:128],
                                           bobc[:, fb2:fb2 + 1],
                                           rres[:, rbuf, fb2, cs:cs + 128], ALU.add,
                                           ALU.add).then_inc(s_ve, 1)))
                queue(task_w("pe", "ve", f14), task_w("pe", "ve", psfree[bank]), ep1,
                      task_w("ve", "pe", fp1), ep2)
                psfree[bank] = fp2

            e15, f15 = task_i("pe", "pe", lambda e: (
                e.matmul(psG[:, 0:128], nyb[:, :], ztb[:, :], start=True, stop=True),
                e.matmul(psG[:, 128:256], ztb[:, :], vsbf[:, :],
                         start=True, stop=True).then_inc(s_pe, 1)))
            queue(task_w("pe", "ve", f14), task_w("pe", "ve", psfree["G"]), e15)
            e16, f16 = task_i("ve", "ve", lambda e: (
                e.tensor_tensor(a_sb[:, :], a_sb[:, :], psG[:, 0:128], ALU.add),
                e.tensor_tensor(mt_sb[:, :], mt_sb[:, :], psG[:, 128:256], ALU.add),
                e.drain(),
                e.tensor_copy(a_bf[:, :], a_sb[:, :]).then_inc(s_ve, 1),
                e.tensor_copy(mt_bf[:, :], mt_sb[:, :]).then_inc(s_ve, 1)), n=2)
            queue(task_w("ve", "pe", f15), e16)
            psfree["G"] = f16
            return f16

        def queue_tail(ib, fmov, prev_out):
            rbuf = ib % 2
            obuf = ib % 2
            proj_last = None
            for fb in range(8):
                bank = "G" if fb % 2 == 0 else "H"
                ps = psG if bank == "G" else psH
                e1, f1 = task_i("pe", "pe", lambda e, fb=fb, ps=ps, obuf=obuf: (
                    [e.matmul(ps[:, :], woT[:, hb, fb * 128:(fb + 1) * 128],
                              oib[:, obuf, hb, :], start=(hb == 0), stop=(hb == 7))
                     for hb in range(8)][-1].then_inc(s_pe, 1)))
                e2, f2 = task_i("ve", "ve", lambda e, fb=fb, ps=ps, rbuf=rbuf: (
                    e.scalar_tensor_tensor(rres[:, rbuf, fb, :], ps[:, :], bcol[:, fb, 3:4],
                                           rres[:, rbuf, fb, :], ALU.add,
                                           ALU.add).then_inc(s_ve, 1)))
                queue(task_w("pe", "g", fmov), task_w("pe", "ve", psfree[bank]), e1,
                      task_w("ve", "pe", f1), e2)
                psfree[bank] = f2
                proj_last = f2
                if fb == 7:
                    hist["proj_pe"][ib] = f1
            ln2 = queue_lnfm(0, proj_last, rbuf, prev_out)
            ffn1 = queue_ffn(mr1s[0:1, 0, :], rbuf, ln2)
            ln3 = queue_lnfm(1, ffn1, rbuf, None)
            last_c = ln3
            for c in (range(4) if KCUT >= 3 else ()):
                last_c = queue_chunk(rbuf, c, ln3, last_c)
            ln4 = queue_lnfm(2, last_c, rbuf, None)
            ffn2 = queue_ffn(mr1s[0:1, 2, :], rbuf, ln4)
            fo = None
            stg_pe_last = None
            for k in range(4):
                stg_ve_last = None
                for half in range(2):
                    bank = "G" if half == 0 else "H"
                    ps = psG if bank == "G" else psH
                    e1, f1 = task_i("pe", "pe",
                                    lambda e, k=k, half=half, ps=ps, rbuf=rbuf: (
                        e.drain(),
                        [e.matmul(ps[:, q2 * 128:(q2 + 1) * 128],
                                  rres[:, rbuf, half * 4 + q2, k * 128:(k + 1) * 128],
                                  identf[:], start=True, stop=True)
                         for q2 in range(4)][-1].then_inc(s_pe, 1),
                        e.drain()))
                    e2, f2 = task_i("ve", "ve", lambda e, half=half, ps=ps: (
                        e.tensor_copy(stg[:, half * 512:(half + 1) * 512],
                                      ps[:, :]).then_inc(s_ve, 1)))
                    ents = [task_w("pe", "ve", ffn2), task_w("pe", "ve", psfree[bank]), e1,
                            task_w("ve", "pe", f1)]
                    if fo is not None:
                        ents.append(task_w("ve", "g", fo))
                    ents.append(e2)
                    queue(*ents)
                    psfree[bank] = f2
                    stg_pe_last = f1
                    stg_ve_last = f2
                oi = ib % 2
                prev_tot = od_state["tot"][oi]
                od_state["tot"][oi] += 16
                new_tot = od_state["tot"][oi]
                eo = ("d", "gp", lambda e, ib=ib, k=k, s=ods[oi]: e.dma_start(
                    out=out_t.rearrange("(n p) f -> p n f", p=128)[
                        :, 4 * ib + k:4 * ib + k + 1, :],
                    in_=stg[:, :].rearrange("p (o f) -> p o f", o=1)).then_inc(s, 16))
                ents2 = [task_w("gp", "ve", stg_ve_last)]
                if prev_tot > 0:
                    ents2.append(task_w("gp", "", ("O", oi, prev_tot)))
                ents2.append(eo)
                queue(*ents2)
                fo = ("O", oi, new_tot)
            hist["tail_stg_pe"][ib] = stg_pe_last
            return fo

        out_futs = []
        for ib in range(NIB_EMIT):
            rbuf = ib % 2
            obuf = ib % 2
            if ib == 0:
                W("ve", "g", t_weff)
                W("sc", "g", t_weff)
                W("pe", "g", t_vstones)
            if ib >= 1:
                WT("ve", hist["omov"][ib - 1])
            if ib >= 2 and KCUT >= 2:
                pump_until(hist["tail_stg_pe"][ib - 2])
                W("ve", "pe", hist["tail_stg_pe"][ib - 2])
                pump_until(hist["proj_pe"][ib - 2])
                W("ve", "pe", hist["proj_pe"][ib - 2])
            W("ve", "pe", hist["qm_last"])

            tA = tB = 0
            tO = 0

            def _sA(v):
                nonlocal tA
                tA = v

            def _sB(v):
                nonlocal tB
                tB = v

            for k in range(4):
                tt = 4 * ib + k
                if ib == 0 and k == 0:
                    W("sy", "ve", t_s1_ve)
                    W("sy", "g", t_weff)
                else:
                    W("sy", "pe", CT["pe"])
                tl = ld(xt, x_in.rearrange("(n p) f -> n p f", p=128)[tt])
                WT("ve", tl)
                def _aff(e, tt=tt):
                    e.tensor_scalar(xlnf[:, :], xt[:, :], lnst[:, tt, 0:1], lnst[:, tt, 1:2],
                                    ALU.subtract, ALU.mult)
                    e.drain()
                    e.tensor_copy(xlnb[:, :], xlnf[:, :])
                    e.drain().then_inc(s_ve, 1)
                E("ve", _aff)
                ta = inc("ve", 1)
                W("pe", "sc", hist["exp_last"])
                xlnT_transposes(k, ta, [
                    (psA[:, 0:512], tA, _sA, 0),
                    (psB[:, 0:512], tB, _sB, 1),
                ])
                for half in range(2):
                    ps_reg = psA[:, 512:1024] if half == 0 else psB[:, 512:1024]
                    gate = tA if half == 0 else tB
                    W("pe", "ve", ta)
                    W("pe", "ve", gate)
                    def _trf(e, ps_reg=ps_reg, half=half):
                        e.drain()
                        for q2 in range(4):
                            fb = half * 4 + q2
                            mm = e.matmul(ps_reg[:, q2 * 128:(q2 + 1) * 128],
                                          xt[:, fb * 128:(fb + 1) * 128], identf[:],
                                          start=True, stop=True)
                        mm.then_inc(s_pe, 1)
                        e.drain()
                    E("pe", _trf)
                    tp = inc("pe", 1)
                    W("ve", "pe", tp)
                    def _evf(e, k=k, half=half, ps_reg=ps_reg, rbuf=rbuf):
                        e.tensor_copy(rres[:, rbuf, half * 4:half * 4 + 4, k * 128:(k + 1) * 128],
                                      ps_reg.rearrange("p (q f) -> p q f",
                                                       q=4)).then_inc(s_ve, 1)
                    E("ve", _evf)
                    tvv = inc("ve", 1)
                    if half == 0:
                        tA = tvv
                    else:
                        tB = tvv
                pump(3)
            t_xprep = CT["ve"]

            for hp in range(HP):
                idx = ib * HP + hp
                sbuf_i = idx % 2
                W("pe", "ve", t_xprep)
                W("pe", "ve", tO)
                W("pe", "ve", t_wq)
                def _qm(e, hp=hp):
                    for fb in range(8):
                        mm = e.matmul(psO[:, 0:512], wqT[:, fb, hp * 128:(hp + 1) * 128],
                                      xlnT[:, fb, :], start=(fb == 0), stop=(fb == 7))
                    mm.then_inc(s_pe, 1)
                E("pe", _qm)
                tqm = inc("pe", 1)
                hist["qm_last"] = tqm
                W("ve", "pe", tqm)
                def _qev(e):
                    e.tensor_scalar(qbt[:, :], psO[:, 0:512], bcol[:, hp, 0:1], None, ALU.add) \
                        if False else None
                E("ve", lambda e, hp=hp: (
                    e.tensor_scalar(qbt[:, :], psO[:, 0:512], bcol[:, hp, 0:1], None,
                                    ALU.add).then_inc(s_ve, 1)))
                tqev = inc("ve", 1)

                W("pe", "ve", tqev)
                WT("pe", t_kst[idx])
                WT("pe", t_vst[idx][0])
                WT("pe", t_vst[idx][1])

                exp_cnt = [0] * 32
                av_cnt = [0] * 32

                def _avj(j):
                    W("pe", "sc", exp_cnt[j])
                    def _av(e, j=j, sbuf_i=sbuf_i):
                        e.matmul(psO[0:65, 0:512], vst[:, sbuf_i, j, 0:65],
                                 ptile[:, j % 2, 0:512], start=(j == 0), stop=(j == 31))
                        e.matmul(psO[0:65, 512:1024], vst[:, sbuf_i, j, 65:130],
                                 ptile[:, j % 2, 512:1024], start=(j == 0),
                                 stop=(j == 31)).then_inc(s_pe, 1)
                    E("pe", _av)
                    av_cnt[j] = inc("pe", 1)

                for j in range(32):
                    ps_s = psA if j % 2 == 0 else psB
                    if j >= 2:
                        W("pe", "sc", exp_cnt[j - 2])
                    def _scj(e, j=j, ps_s=ps_s, sbuf_i=sbuf_i):
                        e.matmul(ps_s[:, 0:512], kst[0:64, sbuf_i, j * 128:(j + 1) * 128],
                                 qbt[0:64, :], start=True, stop=True, tile_position=(0, 0))
                        e.matmul(ps_s[:, 512:1024], kst[64:128, sbuf_i, j * 128:(j + 1) * 128],
                                 qbt[64:128, :], start=True, stop=True,
                                 tile_position=(64, 0)).then_inc(s_pe, 1)
                    E("pe", _scj)
                    t_scj = inc("pe", 1)
                    W("sc", "pe", t_scj)
                    if j >= 2:
                        W("sc", "pe", av_cnt[j - 2])
                    def _exp(e, j=j, ps_s=ps_s):
                        e.activation(ptile[:, j % 2, :], ps_s[:, :], AF.Exp,
                                     bias=cpool[:, 0:1], scale=INV_SQRT_D).then_inc(s_sc, 1)
                    E("sc", _exp)
                    exp_cnt[j] = inc("sc", 1)
                    if j >= 1:
                        _avj(j - 1)
                _avj(31)
                hist["exp_last"] = exp_cnt[31]
                t_av = av_cnt[31]
                kst_free[sbuf_i] = t_av
                vst_free[sbuf_i] = t_av
                if idx + 2 < NIB_EMIT * HP:
                    stream_loads(idx + 2)
                W("ve", "pe", t_av)
                def _rc(e):
                    with nc.allow_low_precision(reason="softmax denominators; probs are bf16"):
                        e.reciprocal(rcp[0:1, 0:512], psO[64:65, 0:512]).then_inc(s_ve, 1)
                        e.reciprocal(rcp[0:1, 512:1024],
                                     psO[64:65, 512:1024]).then_inc(s_ve, 1)
                E("ve", _rc)
                trc = inc("ve", 2)
                W("pe", "ve", trc)
                W("pe", "sc", exp_cnt[31])
                def _rb(e):
                    e.matmul(psA[:, 0:512], onesrow[:, :],
                             rcp[0:1, 0:512], start=True, stop=True)
                    e.matmul(psB[:, 0:512], onesrow[:, :],
                             rcp[0:1, 512:1024], start=True, stop=True).then_inc(s_pe, 1)
                E("pe", _rb)
                trb = inc("pe", 1)
                W("ve", "pe", trb)
                def _nrm(e, hp=hp, obuf=obuf):
                    e.tensor_copy(rcpB[:, 0:512], psA[:, 0:512])
                    e.tensor_copy(rcpB[:, 512:1024], psB[:, 0:512])
                    e.drain()
                    e.tensor_tensor(oib[0:64, obuf, hp, :], psO[0:64, 0:512],
                                    rcpB[0:64, 0:512], ALU.mult).then_inc(s_ve, 1)
                    e.tensor_tensor(ohp[0:64, hp, :], psO[0:64, 512:1024],
                                    rcpB[0:64, 512:1024], ALU.mult).then_inc(s_ve, 1)
                E("ve", _nrm)
                tnrm = inc("ve", 2)
                tA = tB = tO = tnrm
                pump(14)
            if KCUT >= 2:
                W("gp", "ve", CT["ve"])
                mi = ib % 2
                if om_state["tot"][mi] > 0:
                    Q["gp"].append(lambda e, s=oms[mi], v=om_state["tot"][mi]: e.wait_ge(s, v))
                E("gp", lambda e, s=oms[mi], obuf=obuf: e.dma_start(
                    out=oib[64:128, obuf, :, :], in_=ohp[0:64, :, :]).then_inc(s, 16))
                om_state["tot"][mi] += 16
                fmov = ("M", mi, om_state["tot"][mi])
                hist["omov"][ib] = fmov
                prev_out = out_futs[-1] if out_futs else None
                out_futs.append(queue_tail(ib, fmov, prev_out))
        pump_all()

        W("gp", "g", CT["g"])
        WT("gp", ("O", 0, od_state["tot"][0]))
        WT("gp", ("O", 1, od_state["tot"][1]))
        def _fin(e):
            e.drain().then_inc(s_g, 1)
        E("gp", _fin)
        inc("g", 1)

        @blk.sync
        def _(eng):
            for fn in Q["sy"]:
                fn(eng)

        @blk.gpsimd
        def _(eng):
            for fn in Q["gp"]:
                fn(eng)

        @blk.tensor
        def _(eng):
            for fn in Q["pe"]:
                fn(eng)

        @blk.vector
        def _(eng):
            for fn in Q["ve"]:
                fn(eng)

        @blk.scalar
        def _(eng):
            for fn in Q["sc"]:
                fn(eng)

    return nc


def kernel(**inputs):
    x = np.ascontiguousarray(inputs["L0_attn_a_x"], dtype=np.float32)
    biases = np.stack([inputs["L0_attn_a_bq"], inputs["L0_attn_a_bk"],
                       inputs["L0_attn_a_bv"], inputs["L0_attn_a_bo"]], 0).astype(np.float32)
    base = {
        "wq": np.ascontiguousarray(inputs["L0_attn_a_Wq"], np.float32),
        "wk": np.ascontiguousarray(inputs["L0_attn_a_Wk"], np.float32),
        "wv": np.ascontiguousarray(inputs["L0_attn_a_Wv"], np.float32),
        "wo": np.ascontiguousarray(inputs["L0_attn_a_Wo"], np.float32),
        "biases": biases,
        "fw": np.ascontiguousarray(inputs["L0_ffn_W"], np.float32),
        "fbm": np.ascontiguousarray(inputs["L0_ffn_b"], np.float32),
        "flog": np.ascontiguousarray(inputs["L0_ffn_logits"], np.float32).reshape(1, NEX),
        "fu": np.ascontiguousarray(inputs["L0_ffn_u"], np.float32).reshape(1, NEX),
        "wqb": np.ascontiguousarray(inputs["L0_attn_b_Wq"], np.float32),
        "wkb": np.ascontiguousarray(inputs["L0_attn_b_Wk"], np.float32),
        "wvb": np.ascontiguousarray(inputs["L0_attn_b_Wv"], np.float32),
        "wob": np.ascontiguousarray(inputs["L0_attn_b_Wo"], np.float32),
        "bqb": np.ascontiguousarray(inputs["L0_attn_b_bq"], np.float32).reshape(1, HB),
        "bkb": np.ascontiguousarray(inputs["L0_attn_b_bk"], np.float32).reshape(1, HB),
        "bvb": np.ascontiguousarray(inputs["L0_attn_b_bv"], np.float32).reshape(1, HB),
        "bob": np.ascontiguousarray(inputs["L0_attn_b_bo"], np.float32).reshape(1, D),
        "bcolh": np.ascontiguousarray(
            biases.reshape(4, 8, 128).transpose(2, 1, 0).reshape(128, 32)),
        "bobch": np.ascontiguousarray(
            np.asarray(inputs["L0_attn_b_bo"], np.float32).reshape(8, 128).T),
    }
    nc = build_program()
    in_maps = []
    for b in range(8):
        m = dict(base)
        m["x_in"] = np.ascontiguousarray(x[b])
        in_maps.append(m)
    res = run_bass_kernel_spmd(nc, in_maps, core_ids=list(range(8)))
    return np.stack([res.results[b]["out"] for b in range(8)], 0)


# version 15 (final)
# speedup vs baseline: 1.0118x; 1.0118x over previous
import collections
import contextlib
import numpy as np
import concourse.bass as bass
import concourse.mybir as mybir
from concourse.bass_utils import run_bass_kernel_spmd

F32 = mybir.dt.float32
BF16 = mybir.dt.bfloat16
AF = mybir.ActivationFunctionType
ALU = mybir.AluOpType
AX = mybir.AxisListType

T, D, HB = 4096, 1024, 128
NIB, IB, HP = 8, 512, 8
NEX = 8
EPS_LN, EPS_B = 1e-5, 1e-6
SHIFT = 2.0
INV_SQRT_D = 1.0 / 32.0
INV_D = 1.0 / float(D)


class Fut:
    __slots__ = ("v",)

    def __init__(self, v=None):
        self.v = v


def build_program():
    import os as _os0
    NIB_EMIT = int(_os0.environ.get("KLIM", NIB))
    KCUT = int(_os0.environ.get("KCUT", 3))
    KCH = int(_os0.environ.get("KCH", 9))
    nc = bass.Bass("TRN2", target_bir_lowering=False)

    x_in = nc.dram_tensor("x_in", [T, D], F32, kind="ExternalInput").ap()
    out_t = nc.dram_tensor("out", [T, D], F32, kind="ExternalOutput").ap()
    wq_d = nc.dram_tensor("wq", [D, D], F32, kind="ExternalInput").ap()
    wk_d = nc.dram_tensor("wk", [D, D], F32, kind="ExternalInput").ap()
    wv_d = nc.dram_tensor("wv", [D, D], F32, kind="ExternalInput").ap()
    wo_d = nc.dram_tensor("wo", [D, D], F32, kind="ExternalInput").ap()
    bias_d = nc.dram_tensor("biases", [4, D], F32, kind="ExternalInput").ap()
    fw_d = nc.dram_tensor("fw", [NEX, D, D], F32, kind="ExternalInput").ap()
    fb_d = nc.dram_tensor("fbm", [NEX, D], F32, kind="ExternalInput").ap()
    flog_d = nc.dram_tensor("flog", [1, NEX], F32, kind="ExternalInput").ap()
    fu_d = nc.dram_tensor("fu", [1, NEX], F32, kind="ExternalInput").ap()
    wqb_d = nc.dram_tensor("wqb", [HB, D], F32, kind="ExternalInput").ap()
    wkb_d = nc.dram_tensor("wkb", [HB, D], F32, kind="ExternalInput").ap()
    wvb_d = nc.dram_tensor("wvb", [HB, D], F32, kind="ExternalInput").ap()
    wob_d = nc.dram_tensor("wob", [D, HB], F32, kind="ExternalInput").ap()
    bqb_d = nc.dram_tensor("bqb", [1, HB], F32, kind="ExternalInput").ap()
    bkb_d = nc.dram_tensor("bkb", [1, HB], F32, kind="ExternalInput").ap()
    bvb_d = nc.dram_tensor("bvb", [1, HB], F32, kind="ExternalInput").ap()
    bob_d = nc.dram_tensor("bob", [1, D], F32, kind="ExternalInput").ap()
    bcolh_d = nc.dram_tensor("bcolh", [128, 32], F32, kind="ExternalInput").ap()
    bobch_d = nc.dram_tensor("bobch", [128, 8], F32, kind="ExternalInput").ap()
    kt_scr = nc.dram_tensor("kt_scr", [D, T], BF16, kind="Internal").ap()
    v_scr = nc.dram_tensor("v_scr", [T, D], BF16, kind="Internal").ap()

    Q = {"pe": [], "ve": [], "sc": [], "sy": [], "gp": []}
    CT = {"ld": 0, "g": 0, "pe": 0, "ve": 0, "sc": 0}

    with contextlib.ExitStack() as _st:
        def sbt(name, shape, dt):
            return _st.enter_context(nc.sbuf_tensor(name, shape, dt))

        POOLB = 118272
        PARB = 94560
        _pool = sbt("pool", [128, POOLB // 2], BF16)
        _parena = sbt("parena", [128, PARB // 2], BF16)

        class Alloc2:
            def __init__(self, buf, cap):
                self.buf = buf
                self.cap = cap
                self.off = 0

            def take(self, shape, dt):
                nel = 1
                for s in shape[1:]:
                    nel *= s
                esz = 4 if dt == F32 else 2
                nb = nel * esz
                off = self.off
                self.off = (off + nb + 63) // 64 * 64
                assert self.off <= self.cap, f"pool overflow {self.off}"
                v = self.buf[:, off // 2:(off + nb) // 2]
                if dt == F32:
                    v = v.bitcast(F32)
                if len(shape) == 3:
                    v = v.rearrange("p (a b) -> p a b", a=shape[1])
                elif len(shape) == 4:
                    v = v.rearrange("p (a b c) -> p a b c", a=shape[1], b=shape[2])
                if shape[0] != 128:
                    v = v[0:shape[0]]
                return v

        _pa = Alloc2(_parena, PARB)
        wqT = _pa.take([128, 8, D], BF16)
        woT = _pa.take([128, 8, D], BF16)
        weff = _pa.take([128, 8, D], BF16)
        wbT = _pa.take([128, 8, 3 * HB], BF16)
        wobT = _pa.take([128, 8, HB], BF16)
        bcol = _pa.take([128, 8, 4], F32)
        beffc = _pa.take([128, 8], F32)
        bobc = _pa.take([128, 8], F32)
        bqkvB = _pa.take([128, 3, HB], F32)
        negcsW = _pa.take([1, D], BF16)
        negcsB = _pa.take([1, 3 * HB], BF16)
        onesrow = _pa.take([1, 128], BF16)
        onescol = _pa.take([128, 1], BF16)
        identb = _pa.take([128, 128], BF16)
        identf = _pa.take([128, 128], F32)
        ident32 = _pa.take([128, 32], F32)
        masku = _pa.take([128, 128], F32)
        cpool = _pa.take([128, 4], F32)
        hrow = _pa.take([1, 16], F32)
        hrowb = _pa.take([1, 8], BF16)
        hcol = _pa.take([8, 64], BF16)[:, 0:1]
        hBc = _pa.take([128, 8], F32)
        lnst = _pa.take([128, 32, 2], F32)
        a_sb = _pa.take([128, HB], F32)
        mt_sb = _pa.take([128, HB], F32)
        a_bf = _pa.take([128, HB], BF16)
        mt_bf = _pa.take([128, HB], BF16)
        xt = _pa.take([128, D], F32)
        xlnf = _pa.take([128, D], F32)
        xlnb = _pa.take([128, D], BF16)
        xlnT = _pa.take([128, 8, IB], BF16)
        srow = _pa.take([1, 3, IB], F32)
        mr1s = _pa.take([1, 3, IB], BF16)
        mr2t = _pa.take([1, IB], BF16)

        import os as _os
        if _os.environ.get("KSIZE"):
            print("parena used:", _pa.off)
        _s1 = Alloc2(_pool, POOLB)
        wkvT = _s1.take([128, 16, D], BF16)
        wtmp = _s1.take([128, 4, D], F32)
        wtmpb = _s1.take([128, 4, D], BF16)
        eacc = _s1.take([128, 4, D], F32)
        ksb = _s1.take([128, 8, 512], BF16)
        vsb = _s1.take([128, 4, D], BF16)
        brow = _s1.take([1, D], F32)
        browb = _s1.take([1, D], BF16)
        babrow = _s1.take([1, 512], F32)
        babrowb = _s1.take([1, 512], BF16)
        fbmat = _s1.take([8, D], F32)
        fbmatb = _s1.take([8, D], BF16)
        bvB = _s1.take([128, D], F32)

        if _os.environ.get("KSIZE"):
            print("s1 pool used:", _s1.off)
        _mn = Alloc2(_pool, POOLB)
        kst = _mn.take([128, 2, T], BF16)
        vst = _mn.take([128, 2, 32, 130], BF16)
        qbt = _mn.take([128, IB], BF16)
        ptile = _mn.take([128, 2, 1024], BF16)
        oib = _mn.take([128, 2, 8, IB], BF16)
        ohp = _mn.take([64, 8, IB], BF16)
        rres = _mn.take([128, 2, 8, IB], F32)
        xbf = _mn.take([128, 8, IB], BF16)
        rcp = _mn.take([1, 1024], BF16)
        stg = _mn.take([128, D], F32)
        sqb = stg[:, 0:512].bitcast(BF16).rearrange("p (a b) -> p a b", a=2)
        rcpB = ptile[:, 0, :]
        abq = _mn.take([128, HB], F32)
        abk = _mn.take([128, HB], F32)
        abv = _mn.take([128, HB], F32)
        abu = _mn.take([128, HB], F32)
        abscr = _mn.take([128, HB], F32)
        absml = _mn.take([128, 8], F32)
        ubf = _mn.take([128, HB], BF16)
        uft = _mn.take([128, HB], BF16)
        qbf = _mn.take([128, HB], BF16)
        qft = _mn.take([128, HB], BF16)
        vsbf = _mn.take([128, HB], BF16)
        pfbf = _mn.take([128, HB], BF16)
        ptbf = _mn.take([128, HB], BF16)
        wms = _mn.take([128, 128], F32)
        prt = _mn.take([128, 32], F32)
        rowb = _mn.take([128, 33], F32)
        rtl = _mn.take([128, 32], F32)
        altc = _mn.take([128, 32], F32)
        ltbf = _mn.take([128, 128], BF16)
        prbf = _mn.take([128, 32], BF16)
        panbf = _mn.take([128, 128], BF16)
        yts = _mn.take([128, HB], F32)
        ytb = _mn.take([128, HB], BF16)
        accb = _mn.take([128, HB], BF16)
        ztb = _mn.take([128, HB], BF16)
        zft = _mn.take([128, HB], BF16)
        nyb = _mn.take([128, HB], BF16)
        pzb = _mn.take([128, 128], BF16)
        obf = _mn.take([128, HB], BF16)

        if _os.environ.get("KSIZE"):
            print("mn pool used:", _mn.off)
        psA = _st.enter_context(nc.psum_tensor("psA", [128, 1024], F32))
        psB = _st.enter_context(nc.psum_tensor("psB", [128, 1024], F32))
        psO = _st.enter_context(nc.psum_tensor("psO", [128, 1024], F32))
        psG = _st.enter_context(nc.psum_tensor("psG", [128, 512], F32))
        psH = _st.enter_context(nc.psum_tensor("psH", [128, 512], F32))
        s_ld = _st.enter_context(nc.semaphore("s_ld"))
        s_g = _st.enter_context(nc.semaphore("s_g"))
        lds = [_st.enter_context(nc.semaphore(f"s_l{i}")) for i in range(6)]
        gds = [_st.enter_context(nc.semaphore(f"s_gd{i}")) for i in range(4)]
        ods = [_st.enter_context(nc.semaphore(f"s_od{i}")) for i in range(2)]
        oms = [_st.enter_context(nc.semaphore(f"s_om{i}")) for i in range(2)]
        s_pe = _st.enter_context(nc.semaphore("s_pe"))
        s_ve = _st.enter_context(nc.semaphore("s_ve"))
        s_sc = _st.enter_context(nc.semaphore("s_sc"))
        blk = _st.enter_context(nc.Block())

        SEM = {"ld": s_ld, "g": s_g, "pe": s_pe, "ve": s_ve, "sc": s_sc}

        def W(eng, cname, val):
            if isinstance(val, Fut):
                val = val.v
            if val is not None and val > 0:
                Q[eng].append(lambda e, s=SEM[cname], v=val: e.wait_ge(s, v))

        def E(eng, fn):
            Q[eng].append(fn)

        def inc(c, n=1):
            CT[c] += n
            return CT[c]

        ld_state = {"rr": 0, "tot": [0] * 6}
        gd_state = {"rr": 0, "tot": [0] * 4}
        om_state = {"tot": [0] * 2}
        od_state = {"tot": [0] * 2}

        def WT(eng, tok):
            if tok is None:
                return
            if isinstance(tok, tuple):
                kind, i, v = tok
                sem = {"L": lds, "G": gds, "O": ods, "M": oms}[kind][i]
                if v > 0:
                    Q[eng].append(lambda e, s=sem, v=v: e.wait_ge(s, v))
            else:
                raise AssertionError(tok)

        def ld(out_ap, in_ap):
            i = ld_state["rr"] % 6
            ld_state["rr"] += 1
            if ld_state["tot"][i] > 0:
                Q["sy"].append(lambda e, s=lds[i], v=ld_state["tot"][i]: e.wait_ge(s, v))
            E("sy", lambda e, o=out_ap, i2=in_ap, s=lds[i]: e.dma_start(out=o, in_=i2)
              .then_inc(s, 16))
            ld_state["tot"][i] += 16
            return ("L", i, ld_state["tot"][i])

        def gdma(out_ap, in_ap):
            i = gd_state["rr"] % 4
            gd_state["rr"] += 1
            if gd_state["tot"][i] > 0:
                Q["gp"].append(lambda e, s=gds[i], v=gd_state["tot"][i]: e.wait_ge(s, v))
            E("gp", lambda e, o=out_ap, i2=in_ap, s=gds[i]: e.dma_start(out=o, in_=i2)
              .then_inc(s, 16))
            gd_state["tot"][i] += 16
            return ("G", i, gd_state["tot"][i])

        TQ = collections.deque()

        def task_w(eng, cname, val):
            return ("w", eng, cname, val)

        def task_i(eng, cname, fn, n=1):
            f = Fut()
            return ("i", eng, cname, fn, f, n), f

        def queue(*entries):
            TQ.append(list(entries))

        def pump(n):
            done = 0
            while TQ and done < n:
                for ent in TQ.popleft():
                    if ent[0] == "w":
                        if isinstance(ent[3], tuple):
                            WT(ent[1], ent[3])
                        else:
                            W(ent[1], ent[2], ent[3])
                    elif ent[0] == "d":
                        E(ent[1], ent[2])
                    else:
                        _, eng, cname, fn, fut, nn = ent
                        E(eng, fn)
                        fut.v = inc(cname, nn)
                done += 1

        def pump_all():
            pump(len(TQ) + 1)

        def pump_until(fut):
            while TQ and fut.v is None:
                pump(1)

        psfree = {"G": 0, "H": 0}

        _ftog = [0]

        def filler(e):
            c = 12 + _ftog[0]
            _ftog[0] ^= 1
            e.tensor_copy(hrow[0:1, c:c + 1], cpool[0:1, 3:4])

        t_bcol = ld(bcol[:].rearrange("p a c -> p (a c)"), bcolh_d[:, :])
        t_bvrow = ld(brow[:, :], bias_d[2:3, :])
        t_log = ld(hrow[:, 0:8], flog_d[:, :])
        t_u = ld(hrow[:, 8:16], fu_d[:, :])
        t_fb = ld(fbmat[:, :], fb_d[:, :])
        t_bab0 = ld(babrow[:, 0:HB], bqb_d[:, :])
        t_bab1 = ld(babrow[:, HB:2 * HB], bkb_d[:, :])
        t_bab = ld(babrow[:, 2 * HB:3 * HB], bvb_d[:, :])
        t_bob = ld(bobc[:], bobch_d[:, :])

        def _constsv(e):
            e.memset(cpool[:, 0:1], -SHIFT)
            e.memset(cpool[:, 1:2], 0.0)
            e.memset(cpool[:, 2:3], EPS_LN)
            e.memset(cpool[:, 3:4], 1.0)
            e.memset(onesrow[:], 1.0)
            e.memset(onescol[:], 1.0)
            e.memset(identb[:], 0.0)
            e.memset(identf[:], 0.0)
            e.memset(ident32[:], 0.0)
            e.memset(masku[:], 1.0)
            for cc in range(32):
                e.memset(altc[:, cc:cc + 1], 1.0 if cc % 2 == 0 else -1.0)
            e.drain().then_inc(s_ve, 1)
        E("ve", _constsv)
        t_constsv = inc("ve", 1)
        W("gp", "ve", t_constsv)
        def _consts(e):
            e.affine_select(out=identb[:], in_=identb[:], compare_op=ALU.not_equal,
                            fill=1.0, base=0, pattern=[[-1, 128]], channel_multiplier=1)
            e.affine_select(out=identf[:], in_=identf[:], compare_op=ALU.not_equal,
                            fill=1.0, base=0, pattern=[[-1, 128]], channel_multiplier=1)
            for q0 in range(0, 128, 32):
                e.affine_select(out=ident32[q0:q0 + 32, :], in_=ident32[q0:q0 + 32, :],
                                compare_op=ALU.not_equal, fill=1.0, base=0,
                                pattern=[[-1, 32]], channel_multiplier=1)
            e.affine_select(out=masku[:], in_=masku[:], compare_op=ALU.is_ge,
                            fill=0.0, base=0, pattern=[[1, 128]], channel_multiplier=-1)
            e.drain().then_inc(s_g, 1)
        E("gp", _consts)
        t_consts = inc("g", 1)

        W("ve", "g", t_consts)
        def _ainit(e):
            e.tensor_copy(a_sb[:, :], identf[:, :])
            e.tensor_copy(a_bf[:, :], identb[:, :])
            e.memset(mt_sb[:, :], 0.0)
            e.memset(mt_bf[:, :], 0.0)
            e.memset(ytb[:, :], 0.0)
            e.tensor_tensor(rowb[:, 0:32], ident32[:, :], altc[:, :], ALU.mult)
            e.drain()
            e.reduce_sum(rtl[:, 1:2], rowb[:, 0:32], AX.X)
            e.drain()
            e.tensor_scalar(rtl[:, 1:2], rtl[:, 1:2], -1.0, None, ALU.mult)
            e.drain().then_inc(s_ve, 1)
        E("ve", _ainit)
        inc("ve", 1)

        WT("sc", t_log)
        WT("sc", t_u)
        W("sc", "g", t_consts)
        def _gum(e):
            e.activation(hrow[:, 8:16], hrow[:, 8:16], AF.Ln, bias=cpool[0:1, 1:2], scale=1.0)
            e.drain()
            e.activation(hrow[:, 8:16], hrow[:, 8:16], AF.Ln,
                         bias=cpool[0:1, 1:2], scale=-1.0).then_inc(s_sc, 1)
        E("sc", _gum)
        t_gum = inc("sc", 1)
        W("ve", "sc", t_gum)
        def _hsel(e):
            e.tensor_tensor(hrow[:, 0:8], hrow[:, 0:8], hrow[:, 8:16], ALU.subtract)
            e.drain()
            e.reduce_max(hrow[:, 8:9], hrow[:, 0:8], AX.X)
            e.drain()
            e.tensor_scalar(hrow[:, 0:8], hrow[:, 0:8], hrow[0:1, 8:9], None, ALU.is_ge)
            e.drain()
            e.tensor_copy(hrowb[:, 0:8], hrow[:, 0:8])
            e.drain().then_inc(s_ve, 1)
        E("ve", _hsel)
        t_hsel = inc("ve", 1)

        W("pe", "ve", t_hsel)
        def _hb(e):
            e.matmul(psG[:, 0:8], onesrow[:, :], hrowb[:, 0:8], start=True, stop=True)
            e.drain()
            e.matmul(psH[0:8, 0:128], hrow[:, 0:8], identf[0:1, 0:128],
                     start=True, stop=True).then_inc(s_pe, 1)
            e.drain()
        E("pe", _hb)
        t_hb = inc("pe", 1)
        W("ve", "pe", t_hb)
        def _hbe(e):
            e.tensor_copy(hBc[:, :], psG[:, 0:8])
            e.tensor_copy(hcol[0:8, 0:1], psH[0:8, 0:1])
            e.drain().then_inc(s_ve, 1)
        E("ve", _hbe)
        t_hbe = inc("ve", 1)

        WT("ve", t_fb)
        def _fbc(e):
            e.tensor_copy(fbmatb[:, :], fbmat[:, :]).then_inc(s_ve, 1)
        E("ve", _fbc)
        t_fbc = inc("ve", 1)
        W("pe", "ve", t_hbe)
        W("pe", "ve", t_fbc)
        def _beff(e):
            for eb in range(8):
                mm = e.matmul(psG[:, eb:eb + 1], fbmatb[:, eb * 128:(eb + 1) * 128],
                              hcol[0:8, 0:1], start=True, stop=True)
            mm.then_inc(s_pe, 1)
        E("pe", _beff)
        t_beffp = inc("pe", 1)
        W("ve", "pe", t_beffp)
        def _beffe(e):
            e.tensor_copy(beffc[:, :], psG[:, 0:8]).then_inc(s_ve, 1)
        E("ve", _beffe)
        inc("ve", 1)

        wt_full = wtmp[:].rearrange("p a f -> p (a f)")
        wtb_full = wtmpb[:].rearrange("p a f -> p (a f)")

        def lc(src_ap, out_f, out_b):
            W("sy", "ve", CT["ve"])
            tl = ld(out_f, src_ap)
            WT("ve", tl)
            W("ve", "pe", CT["pe"])
            def _cast(e, out_b=out_b, out_f=out_f):
                e.tensor_copy(out_b, out_f).then_inc(s_ve, 1)
            E("ve", _cast)
            return inc("ve", 1)

        def transpose_blocks(blocks, tc):
            for idx, (src, dst) in enumerate(blocks):
                bank = "G" if idx % 2 == 0 else "H"
                ps = psG if bank == "G" else psH
                W("pe", "ve", tc)
                W("pe", "ve", psfree[bank])
                def _t(e, src=src, ps=ps):
                    e.matmul(ps[:, 0:128], src, identb[:],
                             start=True, stop=True).then_inc(s_pe, 1)
                E("pe", _t)
                tp = inc("pe", 1)
                W("ve", "pe", tp)
                def _e(e, dst=dst, ps=ps):
                    e.tensor_copy(dst, ps[:, 0:128]).then_inc(s_ve, 1)
                E("ve", _e)
                psfree[bank] = inc("ve", 1)
            return CT["ve"]

        def load_weightT(src_d, dst_fn):
            last = 0
            for h2 in range(2):
                tc = lc(src_d.rearrange("(a p) f -> p a f", p=128)[:, 4 * h2:4 * h2 + 4, :],
                        wtmp[:], wtb_full)
                last = transpose_blocks(
                    [(wtmpb[:, a, fb * 128:(fb + 1) * 128], dst_fn(4 * h2 + a, fb))
                     for a in range(4) for fb in range(8)], tc)
            return last

        t_wk = load_weightT(wk_d, lambda a, fb: wkvT[:, fb, a * 128:(a + 1) * 128])
        t_wv = load_weightT(wv_d, lambda a, fb: wkvT[:, 8 + fb, a * 128:(a + 1) * 128])
        t_wq = load_weightT(wq_d, lambda a, fb: wqT[:, fb, a * 128:(a + 1) * 128])
        t_wo = load_weightT(wo_d, lambda a, fb: woT[:, fb, a * 128:(a + 1) * 128])
        for slot, wsrc in enumerate((wqb_d, wkb_d, wvb_d)):
            tc = lc(wsrc[:, :], wtmp[:, 0, :], wtmpb[:, 0, :])
            t_wb = transpose_blocks(
                [(wtmpb[:, 0, a * 128:(a + 1) * 128], wbT[:, a, slot * 128:(slot + 1) * 128])
                 for a in range(8)], tc)
        tc = lc(wob_d.rearrange("(a p) h -> p a h", p=128)[:, 0:4, :], wtmp[:, 0:4, 0:HB],
                wtmpb[:, 0:4, 0:HB])
        t_wob = transpose_blocks([(wtmpb[:, a, 0:HB], wobT[:, a, :]) for a in range(4)], tc)
        tc = lc(wob_d.rearrange("(a p) h -> p a h", p=128)[:, 4:8, :], wtmp[:, 0:4, 0:HB],
                wtmpb[:, 0:4, 0:HB])
        t_wob = transpose_blocks([(wtmpb[:, a, 0:HB], wobT[:, 4 + a, :]) for a in range(4)], tc)

        WT("ve", t_bvrow)
        def _bvc(e):
            e.tensor_copy(browb[:, :], brow[:, :]).then_inc(s_ve, 1)
        E("ve", _bvc)
        t_bvc = inc("ve", 1)
        W("pe", "ve", t_bvc)
        W("pe", "ve", psfree["G"])
        W("pe", "ve", psfree["H"])
        def _bvb(e):
            e.matmul(psG[:, :], onesrow[:, :], browb[0:1, 0:512], start=True, stop=True)
            e.matmul(psH[:, :], onesrow[:, :], browb[0:1, 512:1024],
                     start=True, stop=True).then_inc(s_pe, 1)
        E("pe", _bvb)
        t_bvb = inc("pe", 1)
        W("ve", "pe", t_bvb)
        def _bvbe(e):
            e.tensor_copy(bvB[:, 0:512], psG[:, :])
            e.tensor_copy(bvB[:, 512:1024], psH[:, :])
            e.drain().then_inc(s_ve, 1)
        E("ve", _bvbe)
        psfree["G"] = psfree["H"] = inc("ve", 1)

        WT("ve", t_bab0)
        WT("ve", t_bab1)
        WT("ve", t_bab)
        def _babc(e):
            e.tensor_copy(babrowb[:, 0:3 * HB], babrow[:, 0:3 * HB]).then_inc(s_ve, 1)
        E("ve", _babc)
        t_babc = inc("ve", 1)
        W("pe", "ve", t_babc)
        W("pe", "ve", psfree["G"])
        def _babb(e):
            e.matmul(psG[:, 0:128], onesrow[:, :], babrowb[0:1, 0:128], start=True, stop=True)
            e.matmul(psG[:, 128:256], onesrow[:, :], babrowb[0:1, 128:256], start=True, stop=True)
            e.matmul(psG[:, 256:384], onesrow[:, :], babrowb[0:1, 256:384],
                     start=True, stop=True).then_inc(s_pe, 1)
        E("pe", _babb)
        t_babb = inc("pe", 1)
        W("ve", "pe", t_babb)
        def _babe(e):
            e.tensor_copy(bqkvB[:].rearrange("p c h -> p (c h)"),
                          psG[:, 0:384]).then_inc(s_ve, 1)
        E("ve", _babe)
        psfree["G"] = inc("ve", 1)

        W("pe", "ve", t_wb)
        W("pe", "ve", psfree["G"])
        def _ncb(e):
            for fb in range(8):
                mm = e.matmul(psG[0:1, 0:384], onescol[:, :], wbT[:, fb, :],
                              start=(fb == 0), stop=(fb == 7))
            mm.then_inc(s_pe, 1)
        E("pe", _ncb)
        t_ncb = inc("pe", 1)
        W("ve", "pe", t_ncb)
        def _ncbe(e):
            e.tensor_scalar(negcsB[:, :], psG[0:1, 0:384], -1.0, None,
                            ALU.mult).then_inc(s_ve, 1)
        E("ve", _ncbe)
        psfree["G"] = inc("ve", 1)
        t_allcasts = CT["ve"]

        W("gp", "ve", max(t_allcasts, t_hbe))
        t_cmb = 0
        for h2 in range(2):
            for i in range(NEX):
                W("gp", "ve", t_cmb)
                t_exp = gdma(wtmp[:], fw_d[i].rearrange("(a p) f -> p a f", p=128)[
                    :, 4 * h2:4 * h2 + 4, :])
                WT("ve", t_exp)
                if i == 0:
                    def _c0(e):
                        e.tensor_scalar(eacc[:].rearrange("p a f -> p (a f)"), wt_full,
                                        hBc[:, 0:1], None, ALU.mult)
                        e.drain().then_inc(s_ve, 1)
                    E("ve", _c0)
                else:
                    def _ci(e, i=i):
                        e.scalar_tensor_tensor(eacc[:].rearrange("p a f -> p (a f)"), wt_full,
                                               hBc[:, i:i + 1],
                                               eacc[:].rearrange("p a f -> p (a f)"),
                                               ALU.mult, ALU.add)
                        e.drain().then_inc(s_ve, 1)
                    E("ve", _ci)
                t_cmb = inc("ve", 1)
            def _ecast(e, h2=h2):
                e.tensor_copy(weff[:, 4 * h2:4 * h2 + 4, :].rearrange("p a f -> p (a f)"),
                              eacc[:].rearrange("p a f -> p (a f)"))
                e.drain().then_inc(s_ve, 1)
            E("ve", _ecast)
            t_cmb = inc("ve", 1)
        t_weffv = t_cmb

        W("pe", "ve", t_weffv)
        W("pe", "ve", psfree["G"])
        W("pe", "ve", psfree["H"])
        def _ncw(e):
            for fb in range(8):
                e.matmul(psG[0:1, 0:512], onescol[:, :], weff[:, fb, 0:512],
                         start=(fb == 0), stop=(fb == 7))
            for fb in range(8):
                mm = e.matmul(psH[0:1, 0:512], onescol[:, :], weff[:, fb, 512:1024],
                              start=(fb == 0), stop=(fb == 7))
            mm.then_inc(s_pe, 1)
        E("pe", _ncw)
        t_ncw = inc("pe", 1)
        W("ve", "pe", t_ncw)
        def _ncwe(e):
            e.tensor_scalar(negcsW[:, 0:512], psG[0:1, :], -1.0, None, ALU.mult)
            e.tensor_scalar(negcsW[:, 512:1024], psH[0:1, :], -1.0, None, ALU.mult)
            e.drain().then_inc(s_ve, 1)
        E("ve", _ncwe)
        psfree["G"] = psfree["H"] = inc("ve", 1)

        def emit_ln1_tile(tt, t_load):
            WT("ve", t_load)
            def _stats(e, tt=tt):
                e.tensor_tensor(xlnf[:, :], xt[:, :], xt[:, :], ALU.mult)
                e.reduce_sum(lnst[:, tt, 0:1], xt[:, :], AX.X)
                e.drain()
                e.reduce_sum(lnst[:, tt, 1:2], xlnf[:, :], AX.X)
                e.tensor_scalar(lnst[:, tt, 0:1], lnst[:, tt, 0:1], INV_D, None, ALU.mult)
                e.drain()
                e.tensor_scalar(xlnf[:, 0:1], lnst[:, tt, 1:2], INV_D, None, ALU.mult)
                e.drain()
                e.scalar_tensor_tensor(xlnf[:, 1:2], lnst[:, tt, 0:1], lnst[:, tt, 0:1],
                                       xlnf[:, 0:1], ALU.mult, ALU.subtract)
                e.drain().then_inc(s_ve, 1)
            E("ve", _stats)
            tv = inc("ve", 1)
            W("sc", "ve", tv)
            def _rstd(e, tt=tt):
                e.activation(xlnf[:, 2:3], xlnf[:, 1:2], AF.Ln, bias=cpool[:, 2:3], scale=-1.0)
                e.drain()
                e.activation(lnst[:, tt, 1:2], xlnf[:, 2:3], AF.Exp,
                             bias=cpool[:, 1:2], scale=-0.5).then_inc(s_sc, 1)
            E("sc", _rstd)
            ts_ = inc("sc", 1)
            W("ve", "sc", ts_)
            def _aff(e, tt=tt):
                e.tensor_scalar(xlnf[:, :], xt[:, :], lnst[:, tt, 0:1], lnst[:, tt, 1:2],
                                ALU.subtract, ALU.mult)
                e.drain()
                e.tensor_copy(xlnb[:, :], xlnf[:, :])
                e.drain().then_inc(s_ve, 1)
            E("ve", _aff)
            return inc("ve", 1)

        def xlnT_transposes(k, ta, regions):
            outs = []
            for (ps_reg, gate, set_free, half) in regions:
                W("pe", "ve", ta)
                W("pe", "ve", gate)
                def _tr(e, ps_reg=ps_reg, half=half):
                    for q2 in range(4):
                        fb = half * 4 + q2
                        mm = e.matmul(ps_reg[:, q2 * 128:(q2 + 1) * 128],
                                      xlnb[:, fb * 128:(fb + 1) * 128], identb[:],
                                      start=True, stop=True)
                    mm.then_inc(s_pe, 1)
                E("pe", _tr)
                tp = inc("pe", 1)
                W("ve", "pe", tp)
                def _ev(e, k=k, ps_reg=ps_reg, half=half):
                    e.tensor_copy(xlnT[:, half * 4:half * 4 + 4, k * 128:(k + 1) * 128],
                                  ps_reg[:, 0:512].rearrange("p (q f) -> p q f", q=4))
                    e.drain().then_inc(s_ve, 1)
                E("ve", _ev)
                tv = inc("ve", 1)
                set_free(tv)
                outs.append(tv)
            return max(outs)

        def _setG(v):
            psfree["G"] = v

        def _setH(v):
            psfree["H"] = v

        ksb_free = None
        vsb_free = None
        for g in range(8):
            for k in range(4):
                tt = 4 * g + k
                W("sy", "pe", CT["pe"])
                tl = ld(xt, x_in.rearrange("(n p) f -> n p f", p=128)[tt])
                ta = emit_ln1_tile(tt, tl)
                xlnT_transposes(k, ta, [
                    (psG[:, :], psfree["G"], _setG, 0),
                    (psH[:, :], psfree["H"], _setH, 1),
                ])
            t_xlnT = CT["ve"]
            W("pe", "ve", t_xlnT)
            W("pe", "ve", t_wk)
            for rb in range(8):
                bank = "G" if rb % 2 == 0 else "H"
                ps = psG if bank == "G" else psH
                W("pe", "ve", psfree[bank])
                def _km(e, rb=rb, ps=ps):
                    for fb in range(8):
                        mm = e.matmul(ps[:, :], wkvT[:, fb, rb * 128:(rb + 1) * 128],
                                      xlnT[:, fb, :], start=(fb == 0), stop=(fb == 7))
                    mm.then_inc(s_pe, 1)
                E("pe", _km)
                tp = inc("pe", 1)
                W("ve", "pe", tp)
                WT("ve", ksb_free)
                def _kev(e, rb=rb, ps=ps):
                    e.tensor_scalar(ksb[:, rb, :], ps[:, :], bcol[:, rb, 1:2], None,
                                    ALU.add).then_inc(s_ve, 1)
                E("ve", _kev)
                psfree[bank] = inc("ve", 1)
            t_kdone = CT["ve"]
            W("pe", "ve", t_wv)
            for k in range(4):
                for dv in range(2):
                    bank = "G" if (k * 2 + dv) % 2 == 0 else "H"
                    ps = psG if bank == "G" else psH
                    W("pe", "ve", psfree[bank])
                    def _vm(e, k=k, dv=dv, ps=ps):
                        for fb in range(8):
                            mm = e.matmul(ps[:, :], xlnT[:, fb, k * 128:(k + 1) * 128],
                                          wkvT[:, 8 + fb, dv * 512:(dv + 1) * 512],
                                          start=(fb == 0), stop=(fb == 7))
                        mm.then_inc(s_pe, 1)
                    E("pe", _vm)
                    tp = inc("pe", 1)
                    W("ve", "pe", tp)
                    WT("ve", vsb_free)
                    def _vev(e, k=k, dv=dv, ps=ps):
                        e.tensor_tensor(vsb[:, k, dv * 512:(dv + 1) * 512], ps[:, :],
                                        bvB[:, dv * 512:(dv + 1) * 512],
                                        ALU.add).then_inc(s_ve, 1)
                    E("ve", _vev)
                    psfree[bank] = inc("ve", 1)
            t_vdone = CT["ve"]
            W("gp", "ve", t_kdone)
            ksb_free = gdma(
                kt_scr.rearrange("(rb p) t -> p rb t", p=128)[:, :, g * 512:(g + 1) * 512],
                ksb[:, :, :])
            W("gp", "ve", t_vdone)
            vsb_free = gdma(
                v_scr.rearrange("(n p) d -> p n d", p=128)[:, 4 * g:4 * g + 4, :],
                vsb[:, :, :])
        t_s1_ve = CT["ve"]

        def _vstones(e):
            e.memset(vst[:, 0, :, 64:65], 1.0)
            e.memset(vst[:, 0, :, 129:130], 1.0)
            e.memset(vst[:, 1, :, 64:65], 1.0)
            e.memset(vst[:, 1, :, 129:130], 1.0)
            e.drain().then_inc(s_g, 1)
        W("gp", "ve", t_s1_ve)
        E("gp", _vstones)
        t_vstones = inc("g", 1)
        t_weff = t_vstones

        kst_free = [0, 0]
        vst_free = [0, 0]
        t_kst = {}
        t_vst = {}

        def stream_loads(idx):
            ib, hp = divmod(idx, HP)
            buf = idx % 2
            WT("sy", ksb_free)
            WT("sy", vsb_free)
            W("sy", "g", t_weff)
            W("sy", "pe", kst_free[buf])
            t_kst[idx] = ld(kst[:, buf, :], kt_scr.rearrange("(hh p) t -> hh p t", p=128)[hp])
            W("sy", "pe", vst_free[buf])
            vv = vst[:, buf].rearrange("p j (hh c) -> p j hh c", hh=2)
            src3 = v_scr.rearrange("(j p) (h d) -> p j h d", p=128, d=64)
            tv0 = ld(vv[:, :, 0, 0:64], src3[:, :, 2 * hp, :])
            tv1 = ld(vv[:, :, 1, 0:64], src3[:, :, 2 * hp + 1, :])
            t_vst[idx] = (tv0, tv1)

        stream_loads(0)
        stream_loads(1)

        hist = {
            "tail_stg_pe": [None] * NIB,
            "proj_pe": [None] * NIB,
            "omov": [None] * NIB,
            "qm_last": 0,
            "exp_last": 0,
        }

        def queue_lnfm(slot, prev_done, rbuf, prev_out):
            mr1 = mr1s[0:1, slot, :]
            mr2 = mr2t[:, :]
            eg, fg = task_i("gp", "g", lambda e, rbuf=rbuf: (
                [e.tensor_copy(xbf[:, fb, :],
                               rres[:, rbuf, fb, :]).then_inc(s_g, 1)
                 for fb in range(8)]), n=8)
            queue(task_w("gp", "ve", prev_done), eg)
            e1, f1 = task_i("pe", "pe", lambda e: (
                [e.matmul(psG[0:1, :], onescol[:, :], xbf[:, fb, :],
                          start=(fb == 0), stop=(fb == 7)) for fb in range(8)][-1]
                .then_inc(s_pe, 1)))
            queue(task_w("pe", "g", fg), task_w("pe", "ve", psfree["G"]), e1)
            psfree["G"] = None
            mm2_f = None
            sqv = [None, None]
            for fb in range(8):
                ev, fv = task_i("ve", "ve", lambda e, fb=fb: (
                    e.tensor_tensor(sqb[:, fb % 2, :], xbf[:, fb, :], xbf[:, fb, :],
                                    ALU.mult).then_inc(s_ve, 1)))
                ents = [task_w("ve", "g", fg)]
                if prev_out is not None and fb < 2:
                    ents.append(task_w("ve", "g", prev_out))
                if sqv[fb % 2] is not None:
                    ents.append(task_w("ve", "pe", mm2_f))
                ents.append(ev)
                queue(*ents)
                sqv[fb % 2] = fv
                wts = [task_w("pe", "ve", fv)]
                if fb == 0:
                    wts.append(task_w("pe", "ve", psfree["H"]))
                em_, fm_ = task_i("pe", "pe", lambda e, fb=fb: (
                    e.matmul(psH[0:1, :], onescol[:, :], sqb[:, fb % 2, :],
                             start=(fb == 0), stop=(fb == 7)).then_inc(s_pe, 1)))
                queue(*wts, em_)
                mm2_f = fm_
            psfree["H"] = None
            er, fr = task_i("ve", "ve", lambda e: (
                e.tensor_scalar(srow[0:1, 0, :], psG[0:1, :], INV_D, None, ALU.mult),
                e.tensor_scalar(srow[0:1, 1, :], psH[0:1, :], INV_D, None, ALU.mult),
                e.drain(),
                e.tensor_tensor(srow[0:1, 2, :], srow[0:1, 0, :], srow[0:1, 0, :], ALU.mult),
                e.drain(),
                e.tensor_tensor(srow[0:1, 1, :], srow[0:1, 1, :], srow[0:1, 2, :],
                                ALU.subtract).then_inc(s_ve, 1)))
            queue(task_w("ve", "pe", mm2_f), er)
            psfree["G"] = fr
            psfree["H"] = fr
            es, fs = task_i("sc", "sc", lambda e: (
                e.activation(srow[0:1, 2, :], srow[0:1, 1, :], AF.Ln,
                             bias=cpool[0:1, 2:3], scale=1.0),
                e.drain(),
                e.activation(srow[0:1, 1, :], srow[0:1, 2, :], AF.Exp,
                             bias=cpool[0:1, 1:2], scale=-0.5).then_inc(s_sc, 1)))
            queue(task_w("sc", "ve", fr), es)
            em2, fm2v = task_i("ve", "ve", lambda e, mr1=mr1, mr2=mr2: (
                e.tensor_tensor(srow[0:1, 2, :], srow[0:1, 0, :], srow[0:1, 1, :], ALU.mult),
                e.tensor_copy(mr2, srow[0:1, 1, :]),
                e.drain(),
                e.tensor_copy(mr1, srow[0:1, 2, :]).then_inc(s_ve, 1)))
            queue(task_w("ve", "sc", fs), em2)
            eb_, fb_ = task_i("pe", "pe", lambda e, mr2=mr2: (
                e.matmul(psG[:, :], onesrow[:, :], mr2,
                         start=True, stop=True).then_inc(s_pe, 1)))
            queue(task_w("pe", "ve", fm2v), task_w("pe", "ve", psfree["G"]), eb_)
            ea, fa = task_i("ve", "ve", lambda e, rbuf=rbuf: (
                [e.tensor_tensor(xbf[:, fb, :], rres[:, rbuf, fb, :],
                                 psG[:, :], ALU.mult).then_inc(s_ve, 1)
                 for fb in range(8)]), n=8)
            queue(task_w("ve", "pe", fb_), ea)
            psfree["G"] = fa
            return fa

        def queue_ffn(mr1, rbuf, ready):
            last = None
            for eb in range(8):
                bank = "G" if eb % 2 == 0 else "H"
                ps = psG if bank == "G" else psH
                e1, f1 = task_i("pe", "pe", lambda e, eb=eb, ps=ps, mr1=mr1: (
                    [e.matmul(ps[:, :], weff[:, fb, eb * 128:(eb + 1) * 128], xbf[:, fb, :],
                              start=(fb == 0), stop=False) for fb in range(8)],
                    e.matmul(ps[:, :], negcsW[0:1, eb * 128:(eb + 1) * 128], mr1,
                             start=False, stop=True).then_inc(s_pe, 1)))
                e2, f2 = task_i("ve", "ve", lambda e, eb=eb, ps=ps, rbuf=rbuf: (
                    e.scalar_tensor_tensor(rres[:, rbuf, eb, :], ps[:, :], beffc[:, eb:eb + 1],
                                           rres[:, rbuf, eb, :], ALU.add,
                                           ALU.add).then_inc(s_ve, 1)))
                queue(task_w("pe", "ve", ready), task_w("pe", "ve", psfree[bank]),
                      task_w("pe", "g", t_weff), e1, task_w("ve", "pe", f1), e2)
                psfree[bank] = f2
                last = f2
            return last

        def queue_chunk(rbuf, c, ln3f, prev_pe):
            cs = c * 128
            fv2 = None
            for slot, dstf in ((0, abq), (1, abk), (2, abv)):
                bank = "G" if slot % 2 == 0 else "H"
                ps = psG if bank == "G" else psH
                e1, f1 = task_i("pe", "pe", lambda e, slot=slot, ps=ps, cs=cs: (
                    [e.matmul(ps[:, 0:128], xbf[:, fb, cs:cs + 128],
                              wbT[:, fb, slot * 128:(slot + 1) * 128],
                              start=(fb == 0), stop=False) for fb in range(8)],
                    e.matmul(ps[:, 0:128], mr1s[0:1, 1, cs:cs + 128],
                             negcsB[0:1, slot * 128:(slot + 1) * 128],
                             start=False, stop=True).then_inc(s_pe, 1)))
                e2, f2 = task_i("ve", "ve", lambda e, slot=slot, ps=ps, dstf=dstf: (
                    e.tensor_tensor(dstf[:, :], ps[:, 0:128], bqkvB[:, slot, :],
                                    ALU.add).then_inc(s_ve, 1)))
                ents = [task_w("pe", "ve", ln3f), task_w("pe", "ve", psfree[bank])]
                if prev_pe is not None:
                    ents.append(task_w("pe", "ve", prev_pe))
                queue(*ents, e1, task_w("ve", "pe", f1), e2)
                psfree[bank] = f2
                if slot == 2:
                    fv2 = f2
            if KCH <= 1:
                return fv2

            ev1, fv1 = task_i("ve", "ve", lambda e: (
                e.tensor_tensor(abscr[:, :], abk[:, :], abk[:, :], ALU.mult),
                e.tensor_tensor(yts[:, :], abk[:, :], abq[:, :], ALU.mult),
                e.drain(),
                e.reduce_sum(absml[:, 0:1], abscr[:, :], AX.X).then_inc(s_ve, 1),
                e.reduce_sum(absml[:, 1:2], yts[:, :], AX.X).then_inc(s_ve, 1)), n=2)
            queue(task_w("ve", "ve", fv2), ev1)
            es1, fs1 = task_i("sc", "sc", lambda e: (
                e.activation(absml[:, 2:3], absml[:, 0:1], AF.Ln, bias=cpool[:, 1:2], scale=1.0),
                e.drain(),
                e.activation(absml[:, 2:3], absml[:, 2:3], AF.Exp,
                             bias=cpool[:, 1:2], scale=0.5).then_inc(s_sc, 1)))
            queue(task_w("sc", "ve", fv1), es1)
            ev2, fv2b = task_i("ve", "ve", lambda e: (
                e.tensor_scalar(absml[:, 3:4], absml[:, 2:3], EPS_B, None, ALU.add),
                e.tensor_scalar(vsbf[:, :], abv[:, :], absml[:, 1:2], None, ALU.mult),
                e.tensor_copy(qbf[:, :], abq[:, :]),
                e.drain(),
                e.reciprocal(absml[:, 3:4], absml[:, 3:4]),
                e.drain(),
                e.tensor_scalar(abu[:, :], abk[:, :], absml[:, 3:4], None,
                                ALU.mult).then_inc(s_ve, 1),
                e.tensor_scalar(ubf[:, :], abk[:, :], absml[:, 3:4], None,
                                ALU.mult).then_inc(s_ve, 1)), n=2)
            queue(task_w("ve", "sc", fs1), ev2)

            e3, f3 = task_i("pe", "pe", lambda e: (
                e.matmul(psG[:, 0:128], ubf[:, :], identb[:], start=True, stop=True),
                e.matmul(psG[:, 128:256], qbf[:, :], identb[:],
                         start=True, stop=True).then_inc(s_pe, 1)))
            e4, f4 = task_i("ve", "ve", lambda e: (
                e.tensor_copy(uft[:, :], psG[:, 0:128]).then_inc(s_ve, 1),
                e.tensor_copy(qft[:, :], psG[:, 128:256]).then_inc(s_ve, 1)), n=2)
            queue(task_w("pe", "ve", fv2b), task_w("pe", "ve", psfree["G"]), e3,
                  task_w("ve", "pe", f3), e4)
            psfree["G"] = f4

            e5, f5 = task_i("pe", "pe", lambda e: (
                e.matmul(psH[:, 0:128], uft[:, :], a_bf[:, :], start=True, stop=True),
                e.matmul(psH[:, 128:256], a_bf[:, :], uft[:, :],
                         start=True, stop=True).then_inc(s_pe, 1)))
            e6, f6 = task_i("ve", "ve", lambda e: (
                e.tensor_copy(ptbf[:, :], psH[:, 0:128]).then_inc(s_ve, 1),
                e.tensor_copy(pfbf[:, :], psH[:, 128:256]).then_inc(s_ve, 1)), n=2)
            queue(task_w("pe", "ve", f4), task_w("pe", "ve", psfree["H"]), e5,
                  task_w("ve", "pe", f5), e6)
            psfree["H"] = f6

            e7, f7 = task_i("pe", "pe", lambda e: (
                e.matmul(psG[:, 0:128], pfbf[:, :], uft[:, :],
                         start=True, stop=True).then_inc(s_pe, 1)))
            queue(task_w("pe", "ve", f6), task_w("pe", "ve", psfree["G"]), e7)

            e8, f8 = task_i("ve", "ve", lambda e: (
                e.scalar_tensor_tensor(wms[:, :], identf[:, :], -1.0, psG[:, 0:128],
                                       ALU.mult, ALU.subtract).then_inc(s_ve, 1),
                e.tensor_copy(prt[:, :], ident32[:, :]).then_inc(s_ve, 1),
                e.memset(ltbf[:, :], 0.0).then_inc(s_ve, 1)), n=3)
            queue(task_w("ve", "pe", f7), e8)
            psfree["G"] = f8
            if KCH <= 2:
                return f8

            yf = [None] * 4
            dvf = [None] * 4
            leaf_gate = f8
            for jl in range(4):
                q0 = jl * 32
                for (g0, g1) in ((0, 4), (4, 8), (8, 12), (12, 16),
                                 (16, 20), (20, 24), (24, 28), (28, 32)):
                    def _leafgrp(e, q0=q0, g0=g0, g1=g1):
                        for i in range(g0, g1):
                            e.stream_shuffle(rowb[q0:q0 + 32, i:32],
                                             wms[q0:q0 + 32, q0 + i:q0 + 32], [i] * 32)
                            e.drain()
                            e.reciprocal(absml[q0:q0 + 32, 6:7],
                                         rowb[q0:q0 + 32, i:i + 1])
                            e.drain()
                            e.tensor_tensor(absml[q0:q0 + 32, 7:8],
                                            wms[q0:q0 + 32, q0 + i:q0 + i + 1],
                                            absml[q0:q0 + 32, 6:7], ALU.mult)
                            if i < 31:
                                e.tensor_tensor(rowb[q0:q0 + 32, 32:33],
                                                prt[q0:q0 + 32, i:i + 1],
                                                absml[q0:q0 + 32, 6:7], ALU.mult)
                            e.drain()
                            if i < 31:
                                e.scalar_tensor_tensor(wms[q0:q0 + 32, q0 + i + 1:q0 + 32],
                                                       rowb[q0:q0 + 32, i + 1:32],
                                                       absml[q0:q0 + 32, 7:8],
                                                       wms[q0:q0 + 32, q0 + i + 1:q0 + 32],
                                                       ALU.mult, ALU.subtract)
                            if i < 31:
                                e.scalar_tensor_tensor(prt[q0:q0 + 32, i + 1:32],
                                                       rowb[q0:q0 + 32, i + 1:32],
                                                       rowb[q0:q0 + 32, 32:33],
                                                       prt[q0:q0 + 32, i + 1:32],
                                                       ALU.mult, ALU.subtract)
                            if i < g1 - 1:
                                e.drain()
                        e.drain().then_inc(s_ve, 1)
                    el, fl = task_i("ve", "ve", _leafgrp)
                    if g0 == 0:
                        queue(task_w("ve", "ve", leaf_gate), el)
                    else:
                        queue(el)
                if KCH <= 3:
                    continue
                ec, fc = task_i("ve", "ve", lambda e, q0=q0: (
                    e.tensor_tensor(prbf[q0:q0 + 32, :], prt[q0:q0 + 32, :],
                                    altc[q0:q0 + 32, :], ALU.mult).then_inc(s_ve, 1),
                    (e.tensor_copy(panbf[q0:q0 + 32, q0 + 32:128],
                                   wms[q0:q0 + 32, q0 + 32:128]).then_inc(s_ve, 1)
                     if q0 + 32 < 128 else e.engine_nop().then_inc(s_ve, 1))), n=2)
                queue(ec)

                ed, fd = task_i("ve", "ve", lambda e, q0=q0: (
                    e.tensor_tensor(rowb[q0:q0 + 32, 0:32], wms[q0:q0 + 32, q0:q0 + 32],
                                    ident32[q0:q0 + 32, :], ALU.mult),
                    e.drain(),
                    e.reduce_sum(rtl[q0:q0 + 32, 0:1], rowb[q0:q0 + 32, 0:32], AX.X),
                    e.drain(),
                    e.reciprocal(rtl[q0:q0 + 32, 0:1], rtl[q0:q0 + 32, 0:1]),
                    e.drain(),
                    e.tensor_tensor(rtl[q0:q0 + 32, 0:1], rtl[q0:q0 + 32, 0:1],
                                    rtl[q0:q0 + 32, 1:2], ALU.mult).then_inc(s_ve, 1)))
                queue(ed)
                dvf[jl] = fd
                if q0 + 32 < 128:
                    wdt = 128 - q0 - 32
                    em3, fm3 = task_i("pe", "pe", lambda e, q0=q0, wdt=wdt: (
                        e.matmul(psG[0:32, 0:wdt], prbf[q0:q0 + 32, :],
                                 panbf[q0:q0 + 32, q0 + 32:128],
                                 start=True, stop=True).then_inc(s_pe, 1)))
                    queue(task_w("pe", "ve", fc), task_w("pe", "ve", psfree["G"]), em3)
                    ep, fp = task_i("ve", "ve", lambda e, q0=q0, wdt=wdt: (
                        e.tensor_scalar(ltbf[q0:q0 + 32, q0 + 32:128], psG[0:32, 0:wdt],
                                        rtl[q0:q0 + 32, 0:1], None,
                                        ALU.mult).then_inc(s_ve, 1),
                        e.tensor_copy(panbf[q0:q0 + 32, q0 + 32:128],
                                      psG[0:32, 0:wdt]).then_inc(s_ve, 1)), n=2)
                    queue(task_w("ve", "pe", fm3), task_w("ve", "ve", fd), ep)
                    psfree["G"] = fp
                    em4, fm4 = task_i("pe", "pe", lambda e, q0=q0, wdt=wdt: (
                        e.matmul(psG[0:wdt, 0:wdt], ltbf[q0:q0 + 32, q0 + 32:128],
                                 panbf[q0:q0 + 32, q0 + 32:128],
                                 start=True, stop=True).then_inc(s_pe, 1)))
                    queue(task_w("pe", "ve", fp), em4)
                    base = q0 + 32
                    es2, fs2 = task_i("ve", "ve", lambda e, base=base, wdt=wdt: (
                        [e.tensor_tensor(wms[qq:qq + 32, base:128], wms[qq:qq + 32, base:128],
                                         psG[qq - base:qq - base + 32, 0:wdt],
                                         ALU.add).then_inc(s_ve, 1)
                         for qq in range(base, 128, 32)]), n=(128 - base) // 32)
                    queue(task_w("ve", "pe", fm4), es2)
                    psfree["G"] = fs2
                    leaf_gate = fs2

                if jl == 0:
                    em1, fm1 = task_i("pe", "pe", lambda e: (
                        e.matmul(psH[0:32, 0:HB], prbf[0:32, :], ptbf[0:32, :],
                                 start=True, stop=True).then_inc(s_pe, 1)))
                    queue(task_w("pe", "ve", fc), task_w("pe", "ve", psfree["H"]), em1)
                    ey, fy = task_i("ve", "ve", lambda e: (
                        e.tensor_copy(yts[0:32, :], psH[0:32, 0:HB]).then_inc(s_ve, 1),
                        e.tensor_copy(ytb[0:32, :], psH[0:32, 0:HB]).then_inc(s_ve, 1)),
                        n=2)
                    queue(task_w("ve", "pe", fm1), ey)
                    psfree["H"] = fy
                    yf[0] = fy
                else:
                    em1, fm1 = task_i("pe", "pe", lambda e, jl=jl, q0=q0: (
                        e.matmul(psH[0:32, 0:HB], ltbf[:, q0:q0 + 32], ytb[:, :],
                                 start=True, stop=True).then_inc(s_pe, 1)))
                    queue(task_w("pe", "ve", fc), task_w("pe", "ve", psfree["H"]), em1)
                    ea2, fa2 = task_i("ve", "ve", lambda e, q0=q0: (
                        e.tensor_tensor(accb[q0:q0 + 32, :], ptbf[q0:q0 + 32, :],
                                        psH[0:32, 0:HB], ALU.add).then_inc(s_ve, 1)))
                    queue(task_w("ve", "pe", fm1), ea2)
                    em2, fm2 = task_i("pe", "pe", lambda e, q0=q0: (
                        e.matmul(psH[0:32, 0:HB], prbf[q0:q0 + 32, :], accb[q0:q0 + 32, :],
                                 start=True, stop=True,
                                 tile_position=(q0, 0)).then_inc(s_pe, 1)))
                    queue(task_w("pe", "ve", fa2), em2)
                    ey, fy = task_i("ve", "ve", lambda e, q0=q0: (
                        e.tensor_copy(yts[q0:q0 + 32, :], psH[0:32, 0:HB]).then_inc(s_ve, 1),
                        e.tensor_copy(ytb[q0:q0 + 32, :], psH[0:32, 0:HB]).then_inc(s_ve, 1)),
                        n=2)
                    queue(task_w("ve", "pe", fm2), ey)
                    psfree["H"] = fy
                    yf[jl] = fy

            if KCH <= 4:
                return fl
            ez, fz = task_i("ve", "ve", lambda e: (
                e.tensor_scalar(ztb[:, :], yts[:, :], rtl[:, 0:1], None,
                                ALU.mult).then_inc(s_ve, 1),
                e.tensor_scalar(nyb[:, :], yts[:, :], -1.0, None,
                                ALU.mult).then_inc(s_ve, 1)), n=2)
            queue(task_w("ve", "ve", dvf[3]), task_w("ve", "ve", yf[3]), ez)

            e9, f9 = task_i("pe", "pe", lambda e: (
                e.matmul(psG[:, 0:128], ztb[:, :], identb[:],
                         start=True, stop=True).then_inc(s_pe, 1)))
            queue(task_w("pe", "ve", fz), task_w("pe", "ve", psfree["G"]), e9)
            e10, f10 = task_i("ve", "ve", lambda e: (
                e.tensor_copy(zft[:, :], psG[:, 0:128]).then_inc(s_ve, 1)))
            queue(task_w("ve", "pe", f9), e10)
            psfree["G"] = f10

            e11, f11 = task_i("pe", "pe", lambda e: (
                e.matmul(psG[:, 0:128], zft[:, :], qft[:, :],
                         start=True, stop=True).then_inc(s_pe, 1)))
            queue(task_w("pe", "ve", f10), e11)
            e12, f12 = task_i("ve", "ve", lambda e: (
                e.tensor_tensor(pzb[:, :], psG[:, 0:128], masku[:, :],
                                ALU.mult).then_inc(s_ve, 1)))
            queue(task_w("ve", "pe", f11), e12)
            psfree["G"] = f12

            e13, f13 = task_i("pe", "pe", lambda e: (
                e.matmul(psH[:, 0:128], mt_bf[:, :], qft[:, :], start=True, stop=False),
                e.matmul(psH[:, 0:128], vsbf[:, :], pzb[:, :],
                         start=False, stop=True).then_inc(s_pe, 1)))
            queue(task_w("pe", "ve", f12), task_w("pe", "ve", psfree["H"]), e13)
            e14, f14 = task_i("ve", "ve", lambda e: (
                e.tensor_copy(obf[:, :], psH[:, 0:128]).then_inc(s_ve, 1)))
            queue(task_w("ve", "pe", f13), e14)
            psfree["H"] = f14

            for fb2 in range(8):
                bank = "G" if fb2 % 2 == 0 else "H"
                ps = psG if bank == "G" else psH
                ep1, fp1 = task_i("pe", "pe", lambda e, fb2=fb2, ps=ps: (
                    e.matmul(ps[:, 0:128], wobT[:, fb2, :], obf[:, :],
                             start=True, stop=True).then_inc(s_pe, 1)))
                ep2, fp2 = task_i("ve", "ve", lambda e, fb2=fb2, ps=ps, rbuf=rbuf, cs=cs: (
                    e.scalar_tensor_tensor(rres[:, rbuf, fb2, cs:cs + 128], ps[:, 0:128],
                                           bobc[:, fb2:fb2 + 1],
                                           rres[:, rbuf, fb2, cs:cs + 128], ALU.add,
                                           ALU.add).then_inc(s_ve, 1)))
                queue(task_w("pe", "ve", f14), task_w("pe", "ve", psfree[bank]), ep1,
                      task_w("ve", "pe", fp1), ep2)
                psfree[bank] = fp2

            e15, f15 = task_i("pe", "pe", lambda e: (
                e.matmul(psG[:, 0:128], nyb[:, :], ztb[:, :], start=True, stop=True),
                e.matmul(psG[:, 128:256], ztb[:, :], vsbf[:, :],
                         start=True, stop=True).then_inc(s_pe, 1)))
            queue(task_w("pe", "ve", f14), task_w("pe", "ve", psfree["G"]), e15)
            e16, f16 = task_i("ve", "ve", lambda e: (
                e.tensor_tensor(a_sb[:, :], a_sb[:, :], psG[:, 0:128], ALU.add),
                e.tensor_tensor(mt_sb[:, :], mt_sb[:, :], psG[:, 128:256], ALU.add),
                e.drain(),
                e.tensor_copy(a_bf[:, :], a_sb[:, :]).then_inc(s_ve, 1),
                e.tensor_copy(mt_bf[:, :], mt_sb[:, :]).then_inc(s_ve, 1)), n=2)
            queue(task_w("ve", "pe", f15), e16)
            psfree["G"] = f16
            return f16

        def queue_tail(ib, fmov, prev_out):
            rbuf = ib % 2
            obuf = ib % 2
            proj_last = None
            for fb in range(8):
                bank = "G" if fb % 2 == 0 else "H"
                ps = psG if bank == "G" else psH
                e1, f1 = task_i("pe", "pe", lambda e, fb=fb, ps=ps, obuf=obuf: (
                    [e.matmul(ps[:, :], woT[:, hb, fb * 128:(fb + 1) * 128],
                              oib[:, obuf, hb, :], start=(hb == 0), stop=(hb == 7))
                     for hb in range(8)][-1].then_inc(s_pe, 1)))
                e2, f2 = task_i("ve", "ve", lambda e, fb=fb, ps=ps, rbuf=rbuf: (
                    e.scalar_tensor_tensor(rres[:, rbuf, fb, :], ps[:, :], bcol[:, fb, 3:4],
                                           rres[:, rbuf, fb, :], ALU.add,
                                           ALU.add).then_inc(s_ve, 1)))
                queue(task_w("pe", "g", fmov), task_w("pe", "ve", psfree[bank]), e1,
                      task_w("ve", "pe", f1), e2)
                psfree[bank] = f2
                proj_last = f2
                if fb == 7:
                    hist["proj_pe"][ib] = f1
            ln2 = queue_lnfm(0, proj_last, rbuf, prev_out)
            ffn1 = queue_ffn(mr1s[0:1, 0, :], rbuf, ln2)
            ln3 = queue_lnfm(1, ffn1, rbuf, None)
            last_c = ln3
            for c in (range(4) if KCUT >= 3 else ()):
                last_c = queue_chunk(rbuf, c, ln3, last_c)
            ln4 = queue_lnfm(2, last_c, rbuf, None)
            ffn2 = queue_ffn(mr1s[0:1, 2, :], rbuf, ln4)
            fo = None
            stg_pe_last = None
            for k in range(4):
                stg_ve_last = None
                for half in range(2):
                    bank = "G" if half == 0 else "H"
                    ps = psG if bank == "G" else psH
                    e1, f1 = task_i("pe", "pe",
                                    lambda e, k=k, half=half, ps=ps, rbuf=rbuf: (
                        e.drain(),
                        [e.matmul(ps[:, q2 * 128:(q2 + 1) * 128],
                                  rres[:, rbuf, half * 4 + q2, k * 128:(k + 1) * 128],
                                  identf[:], start=True, stop=True)
                         for q2 in range(4)][-1].then_inc(s_pe, 1),
                        e.drain()))
                    e2, f2 = task_i("ve", "ve", lambda e, half=half, ps=ps: (
                        e.tensor_copy(stg[:, half * 512:(half + 1) * 512],
                                      ps[:, :]).then_inc(s_ve, 1)))
                    ents = [task_w("pe", "ve", ffn2), task_w("pe", "ve", psfree[bank]), e1,
                            task_w("ve", "pe", f1)]
                    if fo is not None:
                        ents.append(task_w("ve", "g", fo))
                    ents.append(e2)
                    queue(*ents)
                    psfree[bank] = f2
                    stg_pe_last = f1
                    stg_ve_last = f2
                oi = ib % 2
                prev_tot = od_state["tot"][oi]
                od_state["tot"][oi] += 16
                new_tot = od_state["tot"][oi]
                eo = ("d", "gp", lambda e, ib=ib, k=k, s=ods[oi]: e.dma_start(
                    out=out_t.rearrange("(n p) f -> p n f", p=128)[
                        :, 4 * ib + k:4 * ib + k + 1, :],
                    in_=stg[:, :].rearrange("p (o f) -> p o f", o=1)).then_inc(s, 16))
                ents2 = [task_w("gp", "ve", stg_ve_last)]
                if prev_tot > 0:
                    ents2.append(task_w("gp", "", ("O", oi, prev_tot)))
                ents2.append(eo)
                queue(*ents2)
                fo = ("O", oi, new_tot)
            hist["tail_stg_pe"][ib] = stg_pe_last
            return fo

        out_futs = []
        for ib in range(NIB_EMIT):
            rbuf = ib % 2
            obuf = ib % 2
            if ib == 0:
                W("ve", "g", t_weff)
                W("sc", "g", t_weff)
                W("pe", "g", t_vstones)
            if ib >= 1:
                WT("ve", hist["omov"][ib - 1])
            if ib >= 2 and KCUT >= 2:
                pump_until(hist["tail_stg_pe"][ib - 2])
                W("ve", "pe", hist["tail_stg_pe"][ib - 2])
                pump_until(hist["proj_pe"][ib - 2])
                W("ve", "pe", hist["proj_pe"][ib - 2])
            W("ve", "pe", hist["qm_last"])

            tA = tB = 0
            tO = 0

            def _sA(v):
                nonlocal tA
                tA = v

            def _sB(v):
                nonlocal tB
                tB = v

            for k in range(4):
                tt = 4 * ib + k
                if ib == 0 and k == 0:
                    W("sy", "ve", t_s1_ve)
                    W("sy", "g", t_weff)
                else:
                    W("sy", "pe", CT["pe"])
                tl = ld(xt, x_in.rearrange("(n p) f -> n p f", p=128)[tt])
                WT("ve", tl)
                def _aff(e, tt=tt):
                    e.tensor_scalar(xlnf[:, :], xt[:, :], lnst[:, tt, 0:1], lnst[:, tt, 1:2],
                                    ALU.subtract, ALU.mult)
                    e.drain()
                    e.tensor_copy(xlnb[:, :], xlnf[:, :])
                    e.drain().then_inc(s_ve, 1)
                E("ve", _aff)
                ta = inc("ve", 1)
                W("pe", "sc", hist["exp_last"])
                xlnT_transposes(k, ta, [
                    (psA[:, 0:512], tA, _sA, 0),
                    (psB[:, 0:512], tB, _sB, 1),
                ])
                for half in range(2):
                    ps_reg = psA[:, 512:1024] if half == 0 else psB[:, 512:1024]
                    gate = tA if half == 0 else tB
                    W("pe", "ve", ta)
                    W("pe", "ve", gate)
                    def _trf(e, ps_reg=ps_reg, half=half):
                        e.drain()
                        for q2 in range(4):
                            fb = half * 4 + q2
                            mm = e.matmul(ps_reg[:, q2 * 128:(q2 + 1) * 128],
                                          xt[:, fb * 128:(fb + 1) * 128], identf[:],
                                          start=True, stop=True)
                        mm.then_inc(s_pe, 1)
                        e.drain()
                    E("pe", _trf)
                    tp = inc("pe", 1)
                    W("ve", "pe", tp)
                    def _evf(e, k=k, half=half, ps_reg=ps_reg, rbuf=rbuf):
                        e.tensor_copy(rres[:, rbuf, half * 4:half * 4 + 4, k * 128:(k + 1) * 128],
                                      ps_reg.rearrange("p (q f) -> p q f",
                                                       q=4)).then_inc(s_ve, 1)
                    E("ve", _evf)
                    tvv = inc("ve", 1)
                    if half == 0:
                        tA = tvv
                    else:
                        tB = tvv
                pump(3)
            t_xprep = CT["ve"]

            for hp in range(HP):
                idx = ib * HP + hp
                sbuf_i = idx % 2
                W("pe", "ve", t_xprep)
                W("pe", "ve", tO)
                W("pe", "ve", t_wq)
                def _qm(e, hp=hp):
                    for fb in range(8):
                        mm = e.matmul(psO[:, 0:512], wqT[:, fb, hp * 128:(hp + 1) * 128],
                                      xlnT[:, fb, :], start=(fb == 0), stop=(fb == 7))
                    mm.then_inc(s_pe, 1)
                E("pe", _qm)
                tqm = inc("pe", 1)
                hist["qm_last"] = tqm
                W("ve", "pe", tqm)
                def _qev(e):
                    e.tensor_scalar(qbt[:, :], psO[:, 0:512], bcol[:, hp, 0:1], None, ALU.add) \
                        if False else None
                E("ve", lambda e, hp=hp: (
                    e.tensor_scalar(qbt[:, :], psO[:, 0:512], bcol[:, hp, 0:1], None,
                                    ALU.add).then_inc(s_ve, 1)))
                tqev = inc("ve", 1)

                W("pe", "ve", tqev)
                WT("pe", t_kst[idx])
                WT("pe", t_vst[idx][0])
                WT("pe", t_vst[idx][1])

                exp_cnt = [0] * 32
                av_cnt = [0] * 32

                def _avj(j):
                    W("pe", "sc", exp_cnt[j])
                    def _av(e, j=j, sbuf_i=sbuf_i):
                        e.matmul(psO[0:65, 0:512], vst[:, sbuf_i, j, 0:65],
                                 ptile[:, j % 2, 0:512], start=(j == 0), stop=(j == 31))
                        e.matmul(psO[0:65, 512:1024], vst[:, sbuf_i, j, 65:130],
                                 ptile[:, j % 2, 512:1024], start=(j == 0),
                                 stop=(j == 31)).then_inc(s_pe, 1)
                    E("pe", _av)
                    av_cnt[j] = inc("pe", 1)

                for j in range(32):
                    ps_s = psA if j % 2 == 0 else psB
                    if j >= 2:
                        W("pe", "sc", exp_cnt[j - 2])
                    def _scj(e, j=j, ps_s=ps_s, sbuf_i=sbuf_i):
                        e.matmul(ps_s[:, 0:512], kst[0:64, sbuf_i, j * 128:(j + 1) * 128],
                                 qbt[0:64, :], start=True, stop=True, tile_position=(0, 0))
                        e.matmul(ps_s[:, 512:1024], kst[64:128, sbuf_i, j * 128:(j + 1) * 128],
                                 qbt[64:128, :], start=True, stop=True,
                                 tile_position=(64, 0)).then_inc(s_pe, 1)
                    E("pe", _scj)
                    t_scj = inc("pe", 1)
                    W("sc", "pe", t_scj)
                    if j >= 2:
                        W("sc", "pe", av_cnt[j - 2])
                    def _exp(e, j=j, ps_s=ps_s):
                        e.activation(ptile[:, j % 2, :], ps_s[:, :], AF.Exp,
                                     bias=cpool[:, 0:1], scale=INV_SQRT_D).then_inc(s_sc, 1)
                    E("sc", _exp)
                    exp_cnt[j] = inc("sc", 1)
                    if j >= 1:
                        _avj(j - 1)
                _avj(31)
                hist["exp_last"] = exp_cnt[31]
                t_av = av_cnt[31]
                kst_free[sbuf_i] = t_av
                vst_free[sbuf_i] = t_av
                if idx + 2 < NIB_EMIT * HP:
                    stream_loads(idx + 2)
                W("ve", "pe", t_av)
                def _rc(e):
                    with nc.allow_low_precision(reason="softmax denominators; probs are bf16"):
                        e.reciprocal(rcp[0:1, 0:512], psO[64:65, 0:512]).then_inc(s_ve, 1)
                        e.reciprocal(rcp[0:1, 512:1024],
                                     psO[64:65, 512:1024]).then_inc(s_ve, 1)
                E("ve", _rc)
                trc = inc("ve", 2)
                W("pe", "ve", trc)
                W("pe", "sc", exp_cnt[31])
                def _rb(e):
                    e.matmul(psA[:, 0:512], onesrow[:, :],
                             rcp[0:1, 0:512], start=True, stop=True)
                    e.matmul(psB[:, 0:512], onesrow[:, :],
                             rcp[0:1, 512:1024], start=True, stop=True).then_inc(s_pe, 1)
                E("pe", _rb)
                trb = inc("pe", 1)
                W("ve", "pe", trb)
                def _nrm(e, hp=hp, obuf=obuf):
                    e.tensor_copy(rcpB[:, 0:512], psA[:, 0:512])
                    e.tensor_copy(rcpB[:, 512:1024], psB[:, 0:512])
                    e.drain()
                    e.tensor_tensor(oib[0:64, obuf, hp, :], psO[0:64, 0:512],
                                    rcpB[0:64, 0:512], ALU.mult).then_inc(s_ve, 1)
                    e.tensor_tensor(ohp[0:64, hp, :], psO[0:64, 512:1024],
                                    rcpB[0:64, 512:1024], ALU.mult).then_inc(s_ve, 1)
                E("ve", _nrm)
                tnrm = inc("ve", 2)
                tA = tB = tO = tnrm
                pump(14)
            if KCUT >= 2:
                W("gp", "ve", CT["ve"])
                mi = ib % 2
                if om_state["tot"][mi] > 0:
                    Q["gp"].append(lambda e, s=oms[mi], v=om_state["tot"][mi]: e.wait_ge(s, v))
                E("gp", lambda e, s=oms[mi], obuf=obuf: e.dma_start(
                    out=oib[64:128, obuf, :, :], in_=ohp[0:64, :, :]).then_inc(s, 16))
                om_state["tot"][mi] += 16
                fmov = ("M", mi, om_state["tot"][mi])
                hist["omov"][ib] = fmov
                prev_out = out_futs[-1] if out_futs else None
                out_futs.append(queue_tail(ib, fmov, prev_out))
        pump_all()

        W("gp", "g", CT["g"])
        WT("gp", ("O", 0, od_state["tot"][0]))
        WT("gp", ("O", 1, od_state["tot"][1]))
        def _fin(e):
            e.drain().then_inc(s_g, 1)
        E("gp", _fin)
        inc("g", 1)

        @blk.sync
        def _(eng):
            for fn in Q["sy"]:
                fn(eng)

        @blk.gpsimd
        def _(eng):
            for fn in Q["gp"]:
                fn(eng)

        @blk.tensor
        def _(eng):
            for fn in Q["pe"]:
                fn(eng)

        @blk.vector
        def _(eng):
            for fn in Q["ve"]:
                fn(eng)

        @blk.scalar
        def _(eng):
            for fn in Q["sc"]:
                fn(eng)

    return nc


def kernel(**inputs):
    x = np.ascontiguousarray(inputs["L0_attn_a_x"], dtype=np.float32)
    biases = np.stack([inputs["L0_attn_a_bq"], inputs["L0_attn_a_bk"],
                       inputs["L0_attn_a_bv"], inputs["L0_attn_a_bo"]], 0).astype(np.float32)
    base = {
        "wq": np.ascontiguousarray(inputs["L0_attn_a_Wq"], np.float32),
        "wk": np.ascontiguousarray(inputs["L0_attn_a_Wk"], np.float32),
        "wv": np.ascontiguousarray(inputs["L0_attn_a_Wv"], np.float32),
        "wo": np.ascontiguousarray(inputs["L0_attn_a_Wo"], np.float32),
        "biases": biases,
        "fw": np.ascontiguousarray(inputs["L0_ffn_W"], np.float32),
        "fbm": np.ascontiguousarray(inputs["L0_ffn_b"], np.float32),
        "flog": np.ascontiguousarray(inputs["L0_ffn_logits"], np.float32).reshape(1, NEX),
        "fu": np.ascontiguousarray(inputs["L0_ffn_u"], np.float32).reshape(1, NEX),
        "wqb": np.ascontiguousarray(inputs["L0_attn_b_Wq"], np.float32),
        "wkb": np.ascontiguousarray(inputs["L0_attn_b_Wk"], np.float32),
        "wvb": np.ascontiguousarray(inputs["L0_attn_b_Wv"], np.float32),
        "wob": np.ascontiguousarray(inputs["L0_attn_b_Wo"], np.float32),
        "bqb": np.ascontiguousarray(inputs["L0_attn_b_bq"], np.float32).reshape(1, HB),
        "bkb": np.ascontiguousarray(inputs["L0_attn_b_bk"], np.float32).reshape(1, HB),
        "bvb": np.ascontiguousarray(inputs["L0_attn_b_bv"], np.float32).reshape(1, HB),
        "bob": np.ascontiguousarray(inputs["L0_attn_b_bo"], np.float32).reshape(1, D),
        "bcolh": np.ascontiguousarray(
            biases.reshape(4, 8, 128).transpose(2, 1, 0).reshape(128, 32)),
        "bobch": np.ascontiguousarray(
            np.asarray(inputs["L0_attn_b_bo"], np.float32).reshape(8, 128).T),
    }
    nc = build_program()
    in_maps = []
    for b in range(8):
        m = dict(base)
        m["x_in"] = np.ascontiguousarray(x[b])
        in_maps.append(m)
    res = run_bass_kernel_spmd(nc, in_maps, core_ids=list(range(8)))
    return np.stack([res.results[b]["out"] for b in range(8)], 0)


# version 16
# speedup vs baseline: 1.0121x; 1.0003x over previous
import collections
import contextlib
import numpy as np
import concourse.bass as bass
import concourse.mybir as mybir
from concourse.bass_utils import run_bass_kernel_spmd

F32 = mybir.dt.float32
BF16 = mybir.dt.bfloat16
AF = mybir.ActivationFunctionType
ALU = mybir.AluOpType
AX = mybir.AxisListType

T, D, HB = 4096, 1024, 128
NIB, IB, HP = 8, 512, 8
NEX = 8
EPS_LN, EPS_B = 1e-5, 1e-6
SHIFT = 2.0
INV_SQRT_D = 1.0 / 32.0
INV_D = 1.0 / float(D)


class Fut:
    __slots__ = ("v",)

    def __init__(self, v=None):
        self.v = v


def build_program():
    import os as _os0
    NIB_EMIT = int(_os0.environ.get("KLIM", NIB))
    KCUT = int(_os0.environ.get("KCUT", 3))
    KCH = int(_os0.environ.get("KCH", 9))
    nc = bass.Bass("TRN2", target_bir_lowering=False)

    x_in = nc.dram_tensor("x_in", [T, D], F32, kind="ExternalInput").ap()
    out_t = nc.dram_tensor("out", [T, D], F32, kind="ExternalOutput").ap()
    wq_d = nc.dram_tensor("wq", [D, D], F32, kind="ExternalInput").ap()
    wk_d = nc.dram_tensor("wk", [D, D], F32, kind="ExternalInput").ap()
    wv_d = nc.dram_tensor("wv", [D, D], F32, kind="ExternalInput").ap()
    wo_d = nc.dram_tensor("wo", [D, D], F32, kind="ExternalInput").ap()
    bias_d = nc.dram_tensor("biases", [4, D], F32, kind="ExternalInput").ap()
    fw_d = nc.dram_tensor("fw", [NEX, D, D], F32, kind="ExternalInput").ap()
    fb_d = nc.dram_tensor("fbm", [NEX, D], F32, kind="ExternalInput").ap()
    flog_d = nc.dram_tensor("flog", [1, NEX], F32, kind="ExternalInput").ap()
    fu_d = nc.dram_tensor("fu", [1, NEX], F32, kind="ExternalInput").ap()
    wqb_d = nc.dram_tensor("wqb", [HB, D], F32, kind="ExternalInput").ap()
    wkb_d = nc.dram_tensor("wkb", [HB, D], F32, kind="ExternalInput").ap()
    wvb_d = nc.dram_tensor("wvb", [HB, D], F32, kind="ExternalInput").ap()
    wob_d = nc.dram_tensor("wob", [D, HB], F32, kind="ExternalInput").ap()
    bqb_d = nc.dram_tensor("bqb", [1, HB], F32, kind="ExternalInput").ap()
    bkb_d = nc.dram_tensor("bkb", [1, HB], F32, kind="ExternalInput").ap()
    bvb_d = nc.dram_tensor("bvb", [1, HB], F32, kind="ExternalInput").ap()
    bob_d = nc.dram_tensor("bob", [1, D], F32, kind="ExternalInput").ap()
    bcolh_d = nc.dram_tensor("bcolh", [128, 32], F32, kind="ExternalInput").ap()
    bobch_d = nc.dram_tensor("bobch", [128, 8], F32, kind="ExternalInput").ap()
    kt_scr = nc.dram_tensor("kt_scr", [D, T], BF16, kind="Internal").ap()
    v_scr = nc.dram_tensor("v_scr", [T, D], BF16, kind="Internal").ap()

    Q = {"pe": [], "ve": [], "sc": [], "sy": [], "gp": []}
    CT = {"ld": 0, "g": 0, "pe": 0, "ve": 0, "sc": 0}

    with contextlib.ExitStack() as _st:
        def sbt(name, shape, dt):
            return _st.enter_context(nc.sbuf_tensor(name, shape, dt))

        POOLB = 118272
        PARB = 94560
        _pool = sbt("pool", [128, POOLB // 2], BF16)
        _parena = sbt("parena", [128, PARB // 2], BF16)

        class Alloc2:
            def __init__(self, buf, cap):
                self.buf = buf
                self.cap = cap
                self.off = 0

            def take(self, shape, dt):
                nel = 1
                for s in shape[1:]:
                    nel *= s
                esz = 4 if dt == F32 else 2
                nb = nel * esz
                off = self.off
                self.off = (off + nb + 63) // 64 * 64
                assert self.off <= self.cap, f"pool overflow {self.off}"
                v = self.buf[:, off // 2:(off + nb) // 2]
                if dt == F32:
                    v = v.bitcast(F32)
                if len(shape) == 3:
                    v = v.rearrange("p (a b) -> p a b", a=shape[1])
                elif len(shape) == 4:
                    v = v.rearrange("p (a b c) -> p a b c", a=shape[1], b=shape[2])
                if shape[0] != 128:
                    v = v[0:shape[0]]
                return v

        _pa = Alloc2(_parena, PARB)
        wqT = _pa.take([128, 8, D], BF16)
        woT = _pa.take([128, 8, D], BF16)
        weff = _pa.take([128, 8, D], BF16)
        wbT = _pa.take([128, 8, 3 * HB], BF16)
        wobT = _pa.take([128, 8, HB], BF16)
        bcol = _pa.take([128, 8, 4], F32)
        beffc = _pa.take([128, 8], F32)
        bobc = _pa.take([128, 8], F32)
        bqkvB = _pa.take([128, 3, HB], F32)
        negcsW = _pa.take([1, D], BF16)
        negcsB = _pa.take([1, 3 * HB], BF16)
        onesrow = _pa.take([1, 128], BF16)
        onescol = _pa.take([128, 1], BF16)
        identb = _pa.take([128, 128], BF16)
        identf = _pa.take([128, 128], F32)
        ident32 = _pa.take([128, 32], F32)
        masku = _pa.take([128, 128], F32)
        cpool = _pa.take([128, 4], F32)
        hrow = _pa.take([1, 16], F32)
        hrowb = _pa.take([1, 8], BF16)
        hcol = _pa.take([8, 64], BF16)[:, 0:1]
        hBc = _pa.take([128, 8], F32)
        lnst = _pa.take([128, 32, 2], F32)
        a_sb = _pa.take([128, HB], F32)
        mt_sb = _pa.take([128, HB], F32)
        a_bf = _pa.take([128, HB], BF16)
        mt_bf = _pa.take([128, HB], BF16)
        xt = _pa.take([128, D], F32)
        xlnf = _pa.take([128, D], F32)
        xlnb = _pa.take([128, D], BF16)
        xlnT = _pa.take([128, 8, IB], BF16)
        srow = _pa.take([1, 3, IB], F32)
        mr1s = _pa.take([1, 3, IB], BF16)
        mr2t = _pa.take([1, IB], BF16)

        import os as _os
        if _os.environ.get("KSIZE"):
            print("parena used:", _pa.off)
        _s1 = Alloc2(_pool, POOLB)
        wkvT = _s1.take([128, 16, D], BF16)
        wtmp = _s1.take([128, 4, D], F32)
        wtmpb = _s1.take([128, 4, D], BF16)
        eacc = _s1.take([128, 4, D], F32)
        ksb = _s1.take([128, 8, 512], BF16)
        vsb = _s1.take([128, 4, D], BF16)
        brow = _s1.take([1, D], F32)
        browb = _s1.take([1, D], BF16)
        babrow = _s1.take([1, 512], F32)
        babrowb = _s1.take([1, 512], BF16)
        fbmat = _s1.take([8, D], F32)
        fbmatb = _s1.take([8, D], BF16)
        bvB = _s1.take([128, D], F32)

        if _os.environ.get("KSIZE"):
            print("s1 pool used:", _s1.off)
        _mn = Alloc2(_pool, POOLB)
        kst = _mn.take([128, 2, T], BF16)
        vst = _mn.take([128, 2, 32, 130], BF16)
        qbt = _mn.take([128, IB], BF16)
        ptile = _mn.take([128, 2, 1024], BF16)
        oib = _mn.take([128, 2, 8, IB], BF16)
        ohp = _mn.take([64, 8, IB], BF16)
        rres = _mn.take([128, 2, 8, IB], F32)
        xbf = _mn.take([128, 8, IB], BF16)
        rcp = _mn.take([1, 1024], BF16)
        stg = _mn.take([128, D], F32)
        sqb = stg[:, 0:512].bitcast(BF16).rearrange("p (a b) -> p a b", a=2)
        rcpB = ptile[:, 0, :]
        abq = _mn.take([128, HB], F32)
        abk = _mn.take([128, HB], F32)
        abv = _mn.take([128, HB], F32)
        abu = _mn.take([128, HB], F32)
        abscr = _mn.take([128, HB], F32)
        absml = _mn.take([128, 8], F32)
        ubf = _mn.take([128, HB], BF16)
        uft = _mn.take([128, HB], BF16)
        qbf = _mn.take([128, HB], BF16)
        qft = _mn.take([128, HB], BF16)
        vsbf = _mn.take([128, HB], BF16)
        pfbf = _mn.take([128, HB], BF16)
        ptbf = _mn.take([128, HB], BF16)
        wms = _mn.take([128, 128], F32)
        prt = _mn.take([128, 32], F32)
        rowb = _mn.take([128, 33], F32)
        rtl = _mn.take([128, 32], F32)
        altc = _mn.take([128, 32], F32)
        ltbf = _mn.take([128, 128], BF16)
        prbf = _mn.take([128, 32], BF16)
        panbf = _mn.take([128, 128], BF16)
        yts = _mn.take([128, HB], F32)
        ytb = _mn.take([128, HB], BF16)
        accb = _mn.take([128, HB], BF16)
        ztb = _mn.take([128, HB], BF16)
        zft = _mn.take([128, HB], BF16)
        nyb = _mn.take([128, HB], BF16)
        pzb = _mn.take([128, 128], BF16)
        obf = _mn.take([128, HB], BF16)

        if _os.environ.get("KSIZE"):
            print("mn pool used:", _mn.off)
        psA = _st.enter_context(nc.psum_tensor("psA", [128, 1024], F32))
        psB = _st.enter_context(nc.psum_tensor("psB", [128, 1024], F32))
        psO = _st.enter_context(nc.psum_tensor("psO", [128, 1024], F32))
        psG = _st.enter_context(nc.psum_tensor("psG", [128, 512], F32))
        psH = _st.enter_context(nc.psum_tensor("psH", [128, 512], F32))
        s_ld = _st.enter_context(nc.semaphore("s_ld"))
        s_g = _st.enter_context(nc.semaphore("s_g"))
        lds = [_st.enter_context(nc.semaphore(f"s_l{i}")) for i in range(6)]
        gds = [_st.enter_context(nc.semaphore(f"s_gd{i}")) for i in range(4)]
        ods = [_st.enter_context(nc.semaphore(f"s_od{i}")) for i in range(2)]
        oms = [_st.enter_context(nc.semaphore(f"s_om{i}")) for i in range(2)]
        s_pe = _st.enter_context(nc.semaphore("s_pe"))
        s_ve = _st.enter_context(nc.semaphore("s_ve"))
        s_sc = _st.enter_context(nc.semaphore("s_sc"))
        blk = _st.enter_context(nc.Block())

        SEM = {"ld": s_ld, "g": s_g, "pe": s_pe, "ve": s_ve, "sc": s_sc}

        def W(eng, cname, val):
            if isinstance(val, Fut):
                val = val.v
            if val is not None and val > 0:
                Q[eng].append(lambda e, s=SEM[cname], v=val: e.wait_ge(s, v))

        def E(eng, fn):
            Q[eng].append(fn)

        def inc(c, n=1):
            CT[c] += n
            return CT[c]

        ld_state = {"rr": 0, "tot": [0] * 6}
        gd_state = {"rr": 0, "tot": [0] * 4}
        om_state = {"tot": [0] * 2}
        od_state = {"tot": [0] * 2}

        def WT(eng, tok):
            if tok is None:
                return
            if isinstance(tok, tuple):
                kind, i, v = tok
                sem = {"L": lds, "G": gds, "O": ods, "M": oms}[kind][i]
                if v > 0:
                    Q[eng].append(lambda e, s=sem, v=v: e.wait_ge(s, v))
            else:
                raise AssertionError(tok)

        def ld(out_ap, in_ap):
            i = ld_state["rr"] % 6
            ld_state["rr"] += 1
            if ld_state["tot"][i] > 0:
                Q["sy"].append(lambda e, s=lds[i], v=ld_state["tot"][i]: e.wait_ge(s, v))
            E("sy", lambda e, o=out_ap, i2=in_ap, s=lds[i]: e.dma_start(out=o, in_=i2)
              .then_inc(s, 16))
            ld_state["tot"][i] += 16
            return ("L", i, ld_state["tot"][i])

        def gdma(out_ap, in_ap):
            i = gd_state["rr"] % 4
            gd_state["rr"] += 1
            if gd_state["tot"][i] > 0:
                Q["gp"].append(lambda e, s=gds[i], v=gd_state["tot"][i]: e.wait_ge(s, v))
            E("gp", lambda e, o=out_ap, i2=in_ap, s=gds[i]: e.dma_start(out=o, in_=i2)
              .then_inc(s, 16))
            gd_state["tot"][i] += 16
            return ("G", i, gd_state["tot"][i])

        TQ = collections.deque()

        def task_w(eng, cname, val):
            return ("w", eng, cname, val)

        def task_i(eng, cname, fn, n=1):
            f = Fut()
            return ("i", eng, cname, fn, f, n), f

        def queue(*entries):
            TQ.append(list(entries))

        def pump(n):
            done = 0
            while TQ and done < n:
                for ent in TQ.popleft():
                    if ent[0] == "w":
                        if isinstance(ent[3], tuple):
                            WT(ent[1], ent[3])
                        else:
                            W(ent[1], ent[2], ent[3])
                    elif ent[0] == "d":
                        E(ent[1], ent[2])
                    else:
                        _, eng, cname, fn, fut, nn = ent
                        E(eng, fn)
                        fut.v = inc(cname, nn)
                done += 1

        def pump_all():
            pump(len(TQ) + 1)

        def pump_until(fut):
            while TQ and fut.v is None:
                pump(1)

        psfree = {"G": 0, "H": 0}

        _ftog = [0]

        def filler(e):
            c = 12 + _ftog[0]
            _ftog[0] ^= 1
            e.tensor_copy(hrow[0:1, c:c + 1], cpool[0:1, 3:4])

        t_bcol = ld(bcol[:].rearrange("p a c -> p (a c)"), bcolh_d[:, :])
        t_bvrow = ld(brow[:, :], bias_d[2:3, :])
        t_log = ld(hrow[:, 0:8], flog_d[:, :])
        t_u = ld(hrow[:, 8:16], fu_d[:, :])
        t_fb = ld(fbmat[:, :], fb_d[:, :])
        t_bab0 = ld(babrow[:, 0:HB], bqb_d[:, :])
        t_bab1 = ld(babrow[:, HB:2 * HB], bkb_d[:, :])
        t_bab = ld(babrow[:, 2 * HB:3 * HB], bvb_d[:, :])
        t_bob = ld(bobc[:], bobch_d[:, :])

        def _constsv(e):
            e.memset(cpool[:, 0:1], -SHIFT)
            e.memset(cpool[:, 1:2], 0.0)
            e.memset(cpool[:, 2:3], EPS_LN)
            e.memset(cpool[:, 3:4], 1.0)
            e.memset(onesrow[:], 1.0)
            e.memset(onescol[:], 1.0)
            e.memset(identb[:], 0.0)
            e.memset(identf[:], 0.0)
            e.memset(ident32[:], 0.0)
            e.memset(masku[:], 1.0)
            for cc in range(32):
                e.memset(altc[:, cc:cc + 1], 1.0 if cc % 2 == 0 else -1.0)
            e.drain().then_inc(s_ve, 1)
        E("ve", _constsv)
        t_constsv = inc("ve", 1)
        W("gp", "ve", t_constsv)
        def _consts(e):
            e.affine_select(out=identb[:], in_=identb[:], compare_op=ALU.not_equal,
                            fill=1.0, base=0, pattern=[[-1, 128]], channel_multiplier=1)
            e.affine_select(out=identf[:], in_=identf[:], compare_op=ALU.not_equal,
                            fill=1.0, base=0, pattern=[[-1, 128]], channel_multiplier=1)
            for q0 in range(0, 128, 32):
                e.affine_select(out=ident32[q0:q0 + 32, :], in_=ident32[q0:q0 + 32, :],
                                compare_op=ALU.not_equal, fill=1.0, base=0,
                                pattern=[[-1, 32]], channel_multiplier=1)
            e.affine_select(out=masku[:], in_=masku[:], compare_op=ALU.is_ge,
                            fill=0.0, base=0, pattern=[[1, 128]], channel_multiplier=-1)
            e.drain().then_inc(s_g, 1)
        E("gp", _consts)
        t_consts = inc("g", 1)

        W("ve", "g", t_consts)
        def _ainit(e):
            e.tensor_copy(a_sb[:, :], identf[:, :])
            e.tensor_copy(a_bf[:, :], identb[:, :])
            e.memset(mt_sb[:, :], 0.0)
            e.memset(mt_bf[:, :], 0.0)
            e.memset(ytb[:, :], 0.0)
            e.tensor_tensor(rowb[:, 0:32], ident32[:, :], altc[:, :], ALU.mult)
            e.drain()
            e.reduce_sum(rtl[:, 1:2], rowb[:, 0:32], AX.X)
            e.drain()
            e.tensor_scalar(rtl[:, 1:2], rtl[:, 1:2], -1.0, None, ALU.mult)
            e.drain().then_inc(s_ve, 1)
        E("ve", _ainit)
        inc("ve", 1)

        WT("sc", t_log)
        WT("sc", t_u)
        W("sc", "g", t_consts)
        def _gum(e):
            e.activation(hrow[:, 8:16], hrow[:, 8:16], AF.Ln, bias=cpool[0:1, 1:2], scale=1.0)
            e.drain()
            e.activation(hrow[:, 8:16], hrow[:, 8:16], AF.Ln,
                         bias=cpool[0:1, 1:2], scale=-1.0).then_inc(s_sc, 1)
        E("sc", _gum)
        t_gum = inc("sc", 1)
        W("ve", "sc", t_gum)
        def _hsel(e):
            e.tensor_tensor(hrow[:, 0:8], hrow[:, 0:8], hrow[:, 8:16], ALU.subtract)
            e.drain()
            e.reduce_max(hrow[:, 8:9], hrow[:, 0:8], AX.X)
            e.drain()
            e.tensor_scalar(hrow[:, 0:8], hrow[:, 0:8], hrow[0:1, 8:9], None, ALU.is_ge)
            e.drain()
            e.tensor_copy(hrowb[:, 0:8], hrow[:, 0:8])
            e.drain().then_inc(s_ve, 1)
        E("ve", _hsel)
        t_hsel = inc("ve", 1)

        W("pe", "ve", t_hsel)
        def _hb(e):
            e.matmul(psG[:, 0:8], onesrow[:, :], hrowb[:, 0:8], start=True, stop=True)
            e.drain()
            e.matmul(psH[0:8, 0:128], hrow[:, 0:8], identf[0:1, 0:128],
                     start=True, stop=True).then_inc(s_pe, 1)
            e.drain()
        E("pe", _hb)
        t_hb = inc("pe", 1)
        W("ve", "pe", t_hb)
        def _hbe(e):
            e.tensor_copy(hBc[:, :], psG[:, 0:8])
            e.tensor_copy(hcol[0:8, 0:1], psH[0:8, 0:1])
            e.drain().then_inc(s_ve, 1)
        E("ve", _hbe)
        t_hbe = inc("ve", 1)

        WT("ve", t_fb)
        def _fbc(e):
            e.tensor_copy(fbmatb[:, :], fbmat[:, :]).then_inc(s_ve, 1)
        E("ve", _fbc)
        t_fbc = inc("ve", 1)
        W("pe", "ve", t_hbe)
        W("pe", "ve", t_fbc)
        def _beff(e):
            for eb in range(8):
                mm = e.matmul(psG[:, eb:eb + 1], fbmatb[:, eb * 128:(eb + 1) * 128],
                              hcol[0:8, 0:1], start=True, stop=True)
            mm.then_inc(s_pe, 1)
        E("pe", _beff)
        t_beffp = inc("pe", 1)
        W("ve", "pe", t_beffp)
        def _beffe(e):
            e.tensor_copy(beffc[:, :], psG[:, 0:8]).then_inc(s_ve, 1)
        E("ve", _beffe)
        inc("ve", 1)

        wt_full = wtmp[:].rearrange("p a f -> p (a f)")
        wtb_full = wtmpb[:].rearrange("p a f -> p (a f)")

        def lc(src_ap, out_f, out_b):
            W("sy", "ve", CT["ve"])
            tl = ld(out_f, src_ap)
            WT("ve", tl)
            W("ve", "pe", CT["pe"])
            def _cast(e, out_b=out_b, out_f=out_f):
                e.tensor_copy(out_b, out_f).then_inc(s_ve, 1)
            E("ve", _cast)
            return inc("ve", 1)

        def transpose_blocks(blocks, tc):
            for idx, (src, dst) in enumerate(blocks):
                bank = "G" if idx % 2 == 0 else "H"
                ps = psG if bank == "G" else psH
                W("pe", "ve", tc)
                W("pe", "ve", psfree[bank])
                def _t(e, src=src, ps=ps):
                    e.matmul(ps[:, 0:128], src, identb[:],
                             start=True, stop=True).then_inc(s_pe, 1)
                E("pe", _t)
                tp = inc("pe", 1)
                W("ve", "pe", tp)
                def _e(e, dst=dst, ps=ps):
                    e.tensor_copy(dst, ps[:, 0:128]).then_inc(s_ve, 1)
                E("ve", _e)
                psfree[bank] = inc("ve", 1)
            return CT["ve"]

        def load_weightT(src_d, dst_fn):
            last = 0
            for h2 in range(2):
                tc = lc(src_d.rearrange("(a p) f -> p a f", p=128)[:, 4 * h2:4 * h2 + 4, :],
                        wtmp[:], wtb_full)
                last = transpose_blocks(
                    [(wtmpb[:, a, fb * 128:(fb + 1) * 128], dst_fn(4 * h2 + a, fb))
                     for a in range(4) for fb in range(8)], tc)
            return last

        t_wk = load_weightT(wk_d, lambda a, fb: wkvT[:, fb, a * 128:(a + 1) * 128])
        t_wv = load_weightT(wv_d, lambda a, fb: wkvT[:, 8 + fb, a * 128:(a + 1) * 128])
        t_wq = load_weightT(wq_d, lambda a, fb: wqT[:, fb, a * 128:(a + 1) * 128])
        t_wo = load_weightT(wo_d, lambda a, fb: woT[:, fb, a * 128:(a + 1) * 128])
        for slot, wsrc in enumerate((wqb_d, wkb_d, wvb_d)):
            tc = lc(wsrc[:, :], wtmp[:, 0, :], wtmpb[:, 0, :])
            t_wb = transpose_blocks(
                [(wtmpb[:, 0, a * 128:(a + 1) * 128], wbT[:, a, slot * 128:(slot + 1) * 128])
                 for a in range(8)], tc)
        tc = lc(wob_d.rearrange("(a p) h -> p a h", p=128)[:, 0:4, :], wtmp[:, 0:4, 0:HB],
                wtmpb[:, 0:4, 0:HB])
        t_wob = transpose_blocks([(wtmpb[:, a, 0:HB], wobT[:, a, :]) for a in range(4)], tc)
        tc = lc(wob_d.rearrange("(a p) h -> p a h", p=128)[:, 4:8, :], wtmp[:, 0:4, 0:HB],
                wtmpb[:, 0:4, 0:HB])
        t_wob = transpose_blocks([(wtmpb[:, a, 0:HB], wobT[:, 4 + a, :]) for a in range(4)], tc)

        WT("ve", t_bvrow)
        def _bvc(e):
            e.tensor_copy(browb[:, :], brow[:, :]).then_inc(s_ve, 1)
        E("ve", _bvc)
        t_bvc = inc("ve", 1)
        W("pe", "ve", t_bvc)
        W("pe", "ve", psfree["G"])
        W("pe", "ve", psfree["H"])
        def _bvb(e):
            e.matmul(psG[:, :], onesrow[:, :], browb[0:1, 0:512], start=True, stop=True)
            e.matmul(psH[:, :], onesrow[:, :], browb[0:1, 512:1024],
                     start=True, stop=True).then_inc(s_pe, 1)
        E("pe", _bvb)
        t_bvb = inc("pe", 1)
        W("ve", "pe", t_bvb)
        def _bvbe(e):
            e.tensor_copy(bvB[:, 0:512], psG[:, :])
            e.tensor_copy(bvB[:, 512:1024], psH[:, :])
            e.drain().then_inc(s_ve, 1)
        E("ve", _bvbe)
        psfree["G"] = psfree["H"] = inc("ve", 1)

        WT("ve", t_bab0)
        WT("ve", t_bab1)
        WT("ve", t_bab)
        def _babc(e):
            e.tensor_copy(babrowb[:, 0:3 * HB], babrow[:, 0:3 * HB]).then_inc(s_ve, 1)
        E("ve", _babc)
        t_babc = inc("ve", 1)
        W("pe", "ve", t_babc)
        W("pe", "ve", psfree["G"])
        def _babb(e):
            e.matmul(psG[:, 0:128], onesrow[:, :], babrowb[0:1, 0:128], start=True, stop=True)
            e.matmul(psG[:, 128:256], onesrow[:, :], babrowb[0:1, 128:256], start=True, stop=True)
            e.matmul(psG[:, 256:384], onesrow[:, :], babrowb[0:1, 256:384],
                     start=True, stop=True).then_inc(s_pe, 1)
        E("pe", _babb)
        t_babb = inc("pe", 1)
        W("ve", "pe", t_babb)
        def _babe(e):
            e.tensor_copy(bqkvB[:].rearrange("p c h -> p (c h)"),
                          psG[:, 0:384]).then_inc(s_ve, 1)
        E("ve", _babe)
        psfree["G"] = inc("ve", 1)

        W("pe", "ve", t_wb)
        W("pe", "ve", psfree["G"])
        def _ncb(e):
            for fb in range(8):
                mm = e.matmul(psG[0:1, 0:384], onescol[:, :], wbT[:, fb, :],
                              start=(fb == 0), stop=(fb == 7))
            mm.then_inc(s_pe, 1)
        E("pe", _ncb)
        t_ncb = inc("pe", 1)
        W("ve", "pe", t_ncb)
        def _ncbe(e):
            e.tensor_scalar(negcsB[:, :], psG[0:1, 0:384], -1.0, None,
                            ALU.mult).then_inc(s_ve, 1)
        E("ve", _ncbe)
        psfree["G"] = inc("ve", 1)
        t_allcasts = CT["ve"]

        W("gp", "ve", max(t_allcasts, t_hbe))
        t_cmb = 0
        for h2 in range(2):
            for i in range(NEX):
                W("gp", "ve", t_cmb)
                t_exp = gdma(wtmp[:], fw_d[i].rearrange("(a p) f -> p a f", p=128)[
                    :, 4 * h2:4 * h2 + 4, :])
                WT("ve", t_exp)
                if i == 0:
                    def _c0(e):
                        e.tensor_scalar(eacc[:].rearrange("p a f -> p (a f)"), wt_full,
                                        hBc[:, 0:1], None, ALU.mult)
                        e.drain().then_inc(s_ve, 1)
                    E("ve", _c0)
                else:
                    def _ci(e, i=i):
                        e.scalar_tensor_tensor(eacc[:].rearrange("p a f -> p (a f)"), wt_full,
                                               hBc[:, i:i + 1],
                                               eacc[:].rearrange("p a f -> p (a f)"),
                                               ALU.mult, ALU.add)
                        e.drain().then_inc(s_ve, 1)
                    E("ve", _ci)
                t_cmb = inc("ve", 1)
            def _ecast(e, h2=h2):
                e.tensor_copy(weff[:, 4 * h2:4 * h2 + 4, :].rearrange("p a f -> p (a f)"),
                              eacc[:].rearrange("p a f -> p (a f)"))
                e.drain().then_inc(s_ve, 1)
            E("ve", _ecast)
            t_cmb = inc("ve", 1)
        t_weffv = t_cmb

        W("pe", "ve", t_weffv)
        W("pe", "ve", psfree["G"])
        W("pe", "ve", psfree["H"])
        def _ncw(e):
            for fb in range(8):
                e.matmul(psG[0:1, 0:512], onescol[:, :], weff[:, fb, 0:512],
                         start=(fb == 0), stop=(fb == 7))
            for fb in range(8):
                mm = e.matmul(psH[0:1, 0:512], onescol[:, :], weff[:, fb, 512:1024],
                              start=(fb == 0), stop=(fb == 7))
            mm.then_inc(s_pe, 1)
        E("pe", _ncw)
        t_ncw = inc("pe", 1)
        W("ve", "pe", t_ncw)
        def _ncwe(e):
            e.tensor_scalar(negcsW[:, 0:512], psG[0:1, :], -1.0, None, ALU.mult)
            e.tensor_scalar(negcsW[:, 512:1024], psH[0:1, :], -1.0, None, ALU.mult)
            e.drain().then_inc(s_ve, 1)
        E("ve", _ncwe)
        psfree["G"] = psfree["H"] = inc("ve", 1)

        def emit_ln1_tile(tt, t_load):
            WT("ve", t_load)
            def _stats(e, tt=tt):
                e.tensor_tensor(xlnf[:, :], xt[:, :], xt[:, :], ALU.mult)
                e.reduce_sum(lnst[:, tt, 0:1], xt[:, :], AX.X)
                e.drain()
                e.reduce_sum(lnst[:, tt, 1:2], xlnf[:, :], AX.X)
                e.tensor_scalar(lnst[:, tt, 0:1], lnst[:, tt, 0:1], INV_D, None, ALU.mult)
                e.drain()
                e.tensor_scalar(xlnf[:, 0:1], lnst[:, tt, 1:2], INV_D, None, ALU.mult)
                e.drain()
                e.scalar_tensor_tensor(xlnf[:, 1:2], lnst[:, tt, 0:1], lnst[:, tt, 0:1],
                                       xlnf[:, 0:1], ALU.mult, ALU.subtract)
                e.drain().then_inc(s_ve, 1)
            E("ve", _stats)
            tv = inc("ve", 1)
            W("sc", "ve", tv)
            def _rstd(e, tt=tt):
                e.activation(xlnf[:, 2:3], xlnf[:, 1:2], AF.Ln, bias=cpool[:, 2:3], scale=-1.0)
                e.drain()
                e.activation(lnst[:, tt, 1:2], xlnf[:, 2:3], AF.Exp,
                             bias=cpool[:, 1:2], scale=-0.5).then_inc(s_sc, 1)
            E("sc", _rstd)
            ts_ = inc("sc", 1)
            W("ve", "sc", ts_)
            def _aff(e, tt=tt):
                e.tensor_scalar(xlnf[:, :], xt[:, :], lnst[:, tt, 0:1], lnst[:, tt, 1:2],
                                ALU.subtract, ALU.mult)
                e.drain()
                e.tensor_copy(xlnb[:, :], xlnf[:, :])
                e.drain().then_inc(s_ve, 1)
            E("ve", _aff)
            return inc("ve", 1)

        def xlnT_transposes(k, ta, regions):
            outs = []
            for (ps_reg, gate, set_free, half) in regions:
                W("pe", "ve", ta)
                W("pe", "ve", gate)
                def _tr(e, ps_reg=ps_reg, half=half):
                    for q2 in range(4):
                        fb = half * 4 + q2
                        mm = e.matmul(ps_reg[:, q2 * 128:(q2 + 1) * 128],
                                      xlnb[:, fb * 128:(fb + 1) * 128], identb[:],
                                      start=True, stop=True)
                    mm.then_inc(s_pe, 1)
                E("pe", _tr)
                tp = inc("pe", 1)
                W("ve", "pe", tp)
                def _ev(e, k=k, ps_reg=ps_reg, half=half):
                    e.tensor_copy(xlnT[:, half * 4:half * 4 + 4, k * 128:(k + 1) * 128],
                                  ps_reg[:, 0:512].rearrange("p (q f) -> p q f", q=4))
                    e.drain().then_inc(s_ve, 1)
                E("ve", _ev)
                tv = inc("ve", 1)
                set_free(tv)
                outs.append(tv)
            return max(outs)

        def _setG(v):
            psfree["G"] = v

        def _setH(v):
            psfree["H"] = v

        ksb_free = None
        vsb_free = None
        for g in range(8):
            for k in range(4):
                tt = 4 * g + k
                W("sy", "pe", CT["pe"])
                tl = ld(xt, x_in.rearrange("(n p) f -> n p f", p=128)[tt])
                ta = emit_ln1_tile(tt, tl)
                xlnT_transposes(k, ta, [
                    (psG[:, :], psfree["G"], _setG, 0),
                    (psH[:, :], psfree["H"], _setH, 1),
                ])
            t_xlnT = CT["ve"]
            W("pe", "ve", t_xlnT)
            W("pe", "ve", t_wk)
            for rb in range(8):
                bank = "G" if rb % 2 == 0 else "H"
                ps = psG if bank == "G" else psH
                W("pe", "ve", psfree[bank])
                def _km(e, rb=rb, ps=ps):
                    for fb in range(8):
                        mm = e.matmul(ps[:, :], wkvT[:, fb, rb * 128:(rb + 1) * 128],
                                      xlnT[:, fb, :], start=(fb == 0), stop=(fb == 7))
                    mm.then_inc(s_pe, 1)
                E("pe", _km)
                tp = inc("pe", 1)
                W("ve", "pe", tp)
                WT("ve", ksb_free)
                def _kev(e, rb=rb, ps=ps):
                    e.tensor_scalar(ksb[:, rb, :], ps[:, :], bcol[:, rb, 1:2], None,
                                    ALU.add).then_inc(s_ve, 1)
                E("ve", _kev)
                psfree[bank] = inc("ve", 1)
            t_kdone = CT["ve"]
            W("pe", "ve", t_wv)
            for k in range(4):
                for dv in range(2):
                    bank = "G" if (k * 2 + dv) % 2 == 0 else "H"
                    ps = psG if bank == "G" else psH
                    W("pe", "ve", psfree[bank])
                    def _vm(e, k=k, dv=dv, ps=ps):
                        for fb in range(8):
                            mm = e.matmul(ps[:, :], xlnT[:, fb, k * 128:(k + 1) * 128],
                                          wkvT[:, 8 + fb, dv * 512:(dv + 1) * 512],
                                          start=(fb == 0), stop=(fb == 7))
                        mm.then_inc(s_pe, 1)
                    E("pe", _vm)
                    tp = inc("pe", 1)
                    W("ve", "pe", tp)
                    WT("ve", vsb_free)
                    def _vev(e, k=k, dv=dv, ps=ps):
                        e.tensor_tensor(vsb[:, k, dv * 512:(dv + 1) * 512], ps[:, :],
                                        bvB[:, dv * 512:(dv + 1) * 512],
                                        ALU.add).then_inc(s_ve, 1)
                    E("ve", _vev)
                    psfree[bank] = inc("ve", 1)
            t_vdone = CT["ve"]
            W("gp", "ve", t_kdone)
            ksb_free = gdma(
                kt_scr.rearrange("(rb p) t -> p rb t", p=128)[:, :, g * 512:(g + 1) * 512],
                ksb[:, :, :])
            W("gp", "ve", t_vdone)
            vsb_free = gdma(
                v_scr.rearrange("(n p) d -> p n d", p=128)[:, 4 * g:4 * g + 4, :],
                vsb[:, :, :])
        t_s1_ve = CT["ve"]

        def _vstones(e):
            e.memset(vst[:, 0, :, 64:65], 1.0)
            e.memset(vst[:, 0, :, 129:130], 1.0)
            e.memset(vst[:, 1, :, 64:65], 1.0)
            e.memset(vst[:, 1, :, 129:130], 1.0)
            e.drain().then_inc(s_g, 1)
        W("gp", "ve", t_s1_ve)
        E("gp", _vstones)
        t_vstones = inc("g", 1)
        t_weff = t_vstones

        kst_free = [0, 0]
        vst_free = [0, 0]
        t_kst = {}
        t_vst = {}

        def stream_loads(idx):
            ib, hp = divmod(idx, HP)
            buf = idx % 2
            WT("sy", ksb_free)
            WT("sy", vsb_free)
            W("sy", "g", t_weff)
            W("sy", "pe", kst_free[buf])
            t_kst[idx] = ld(kst[:, buf, :], kt_scr.rearrange("(hh p) t -> hh p t", p=128)[hp])
            W("sy", "pe", vst_free[buf])
            vv = vst[:, buf].rearrange("p j (hh c) -> p j hh c", hh=2)
            src3 = v_scr.rearrange("(j p) (h d) -> p j h d", p=128, d=64)
            tv0 = ld(vv[:, :, 0, 0:64], src3[:, :, 2 * hp, :])
            tv1 = ld(vv[:, :, 1, 0:64], src3[:, :, 2 * hp + 1, :])
            t_vst[idx] = (tv0, tv1)

        stream_loads(0)
        stream_loads(1)

        hist = {
            "tail_stg_pe": [None] * NIB,
            "proj_pe": [None] * NIB,
            "omov": [None] * NIB,
            "qm_last": 0,
            "exp_last": 0,
        }

        def queue_lnfm(slot, prev_done, rbuf, prev_out):
            mr1 = mr1s[0:1, slot, :]
            mr2 = mr2t[:, :]
            eg, fg = task_i("gp", "g", lambda e, rbuf=rbuf: (
                [e.tensor_copy(xbf[:, fb, :],
                               rres[:, rbuf, fb, :]).then_inc(s_g, 1)
                 for fb in range(8)]), n=8)
            queue(task_w("gp", "ve", prev_done), eg)
            e1, f1 = task_i("pe", "pe", lambda e: (
                [e.matmul(psG[0:1, :], onescol[:, :], xbf[:, fb, :],
                          start=(fb == 0), stop=(fb == 7)) for fb in range(8)][-1]
                .then_inc(s_pe, 1)))
            queue(task_w("pe", "g", fg), task_w("pe", "ve", psfree["G"]), e1)
            psfree["G"] = None
            mm2_f = None
            sqv = [None, None]
            for fb in range(8):
                ev, fv = task_i("gp", "g", lambda e, fb=fb: (
                    e.tensor_tensor(sqb[:, fb % 2, :], xbf[:, fb, :], xbf[:, fb, :],
                                    ALU.mult).then_inc(s_g, 1)))
                ents = [task_w("gp", "g", fg)]
                if prev_out is not None and fb < 2:
                    ents.append(task_w("gp", "g", prev_out))
                if sqv[fb % 2] is not None:
                    ents.append(task_w("gp", "pe", mm2_f))
                ents.append(ev)
                queue(*ents)
                sqv[fb % 2] = fv
                wts = [task_w("pe", "g", fv)]
                if fb == 0:
                    wts.append(task_w("pe", "ve", psfree["H"]))
                em_, fm_ = task_i("pe", "pe", lambda e, fb=fb: (
                    e.matmul(psH[0:1, :], onescol[:, :], sqb[:, fb % 2, :],
                             start=(fb == 0), stop=(fb == 7)).then_inc(s_pe, 1)))
                queue(*wts, em_)
                mm2_f = fm_
            psfree["H"] = None
            er, fr = task_i("ve", "ve", lambda e: (
                e.tensor_scalar(srow[0:1, 0, :], psG[0:1, :], INV_D, None, ALU.mult),
                e.tensor_scalar(srow[0:1, 1, :], psH[0:1, :], INV_D, None, ALU.mult),
                e.drain(),
                e.tensor_tensor(srow[0:1, 2, :], srow[0:1, 0, :], srow[0:1, 0, :], ALU.mult),
                e.drain(),
                e.tensor_tensor(srow[0:1, 1, :], srow[0:1, 1, :], srow[0:1, 2, :],
                                ALU.subtract).then_inc(s_ve, 1)))
            queue(task_w("ve", "pe", mm2_f), er)
            psfree["G"] = fr
            psfree["H"] = fr
            es, fs = task_i("sc", "sc", lambda e: (
                e.activation(srow[0:1, 2, :], srow[0:1, 1, :], AF.Ln,
                             bias=cpool[0:1, 2:3], scale=1.0),
                e.drain(),
                e.activation(srow[0:1, 1, :], srow[0:1, 2, :], AF.Exp,
                             bias=cpool[0:1, 1:2], scale=-0.5).then_inc(s_sc, 1)))
            queue(task_w("sc", "ve", fr), es)
            em2, fm2v = task_i("ve", "ve", lambda e, mr1=mr1, mr2=mr2: (
                e.tensor_tensor(srow[0:1, 2, :], srow[0:1, 0, :], srow[0:1, 1, :], ALU.mult),
                e.tensor_copy(mr2, srow[0:1, 1, :]),
                e.drain(),
                e.tensor_copy(mr1, srow[0:1, 2, :]).then_inc(s_ve, 1)))
            queue(task_w("ve", "sc", fs), em2)
            eb_, fb_ = task_i("pe", "pe", lambda e, mr2=mr2: (
                e.matmul(psG[:, :], onesrow[:, :], mr2,
                         start=True, stop=True).then_inc(s_pe, 1)))
            queue(task_w("pe", "ve", fm2v), task_w("pe", "ve", psfree["G"]), eb_)
            ea, fa = task_i("ve", "ve", lambda e, rbuf=rbuf: (
                [e.tensor_tensor(xbf[:, fb, :], rres[:, rbuf, fb, :],
                                 psG[:, :], ALU.mult).then_inc(s_ve, 1)
                 for fb in range(8)]), n=8)
            queue(task_w("ve", "pe", fb_), ea)
            psfree["G"] = fa
            return fa

        def queue_ffn(mr1, rbuf, ready):
            last = None
            for eb in range(8):
                bank = "G" if eb % 2 == 0 else "H"
                ps = psG if bank == "G" else psH
                e1, f1 = task_i("pe", "pe", lambda e, eb=eb, ps=ps, mr1=mr1: (
                    [e.matmul(ps[:, :], weff[:, fb, eb * 128:(eb + 1) * 128], xbf[:, fb, :],
                              start=(fb == 0), stop=False) for fb in range(8)],
                    e.matmul(ps[:, :], negcsW[0:1, eb * 128:(eb + 1) * 128], mr1,
                             start=False, stop=True).then_inc(s_pe, 1)))
                e2, f2 = task_i("ve", "ve", lambda e, eb=eb, ps=ps, rbuf=rbuf: (
                    e.scalar_tensor_tensor(rres[:, rbuf, eb, :], ps[:, :], beffc[:, eb:eb + 1],
                                           rres[:, rbuf, eb, :], ALU.add,
                                           ALU.add).then_inc(s_ve, 1)))
                queue(task_w("pe", "ve", ready), task_w("pe", "ve", psfree[bank]),
                      task_w("pe", "g", t_weff), e1, task_w("ve", "pe", f1), e2)
                psfree[bank] = f2
                last = f2
            return last

        def queue_chunk(rbuf, c, ln3f, prev_pe):
            cs = c * 128
            fv2 = None
            for slot, dstf in ((0, abq), (1, abk), (2, abv)):
                bank = "G" if slot % 2 == 0 else "H"
                ps = psG if bank == "G" else psH
                e1, f1 = task_i("pe", "pe", lambda e, slot=slot, ps=ps, cs=cs: (
                    [e.matmul(ps[:, 0:128], xbf[:, fb, cs:cs + 128],
                              wbT[:, fb, slot * 128:(slot + 1) * 128],
                              start=(fb == 0), stop=False) for fb in range(8)],
                    e.matmul(ps[:, 0:128], mr1s[0:1, 1, cs:cs + 128],
                             negcsB[0:1, slot * 128:(slot + 1) * 128],
                             start=False, stop=True).then_inc(s_pe, 1)))
                e2, f2 = task_i("ve", "ve", lambda e, slot=slot, ps=ps, dstf=dstf: (
                    e.tensor_tensor(dstf[:, :], ps[:, 0:128], bqkvB[:, slot, :],
                                    ALU.add).then_inc(s_ve, 1)))
                ents = [task_w("pe", "ve", ln3f), task_w("pe", "ve", psfree[bank])]
                if prev_pe is not None:
                    ents.append(task_w("pe", "ve", prev_pe))
                queue(*ents, e1, task_w("ve", "pe", f1), e2)
                psfree[bank] = f2
                if slot == 2:
                    fv2 = f2
            if KCH <= 1:
                return fv2

            ev1, fv1 = task_i("ve", "ve", lambda e: (
                e.tensor_tensor(abscr[:, :], abk[:, :], abk[:, :], ALU.mult),
                e.tensor_tensor(yts[:, :], abk[:, :], abq[:, :], ALU.mult),
                e.drain(),
                e.reduce_sum(absml[:, 0:1], abscr[:, :], AX.X).then_inc(s_ve, 1),
                e.reduce_sum(absml[:, 1:2], yts[:, :], AX.X).then_inc(s_ve, 1)), n=2)
            queue(task_w("ve", "ve", fv2), ev1)
            es1, fs1 = task_i("sc", "sc", lambda e: (
                e.activation(absml[:, 2:3], absml[:, 0:1], AF.Ln, bias=cpool[:, 1:2], scale=1.0),
                e.drain(),
                e.activation(absml[:, 2:3], absml[:, 2:3], AF.Exp,
                             bias=cpool[:, 1:2], scale=0.5).then_inc(s_sc, 1)))
            queue(task_w("sc", "ve", fv1), es1)
            ev2, fv2b = task_i("ve", "ve", lambda e: (
                e.tensor_scalar(absml[:, 3:4], absml[:, 2:3], EPS_B, None, ALU.add),
                e.tensor_scalar(vsbf[:, :], abv[:, :], absml[:, 1:2], None, ALU.mult),
                e.tensor_copy(qbf[:, :], abq[:, :]),
                e.drain(),
                e.reciprocal(absml[:, 3:4], absml[:, 3:4]),
                e.drain(),
                e.tensor_scalar(abu[:, :], abk[:, :], absml[:, 3:4], None,
                                ALU.mult).then_inc(s_ve, 1),
                e.tensor_scalar(ubf[:, :], abk[:, :], absml[:, 3:4], None,
                                ALU.mult).then_inc(s_ve, 1)), n=2)
            queue(task_w("ve", "sc", fs1), ev2)

            e3, f3 = task_i("pe", "pe", lambda e: (
                e.matmul(psG[:, 0:128], ubf[:, :], identb[:], start=True, stop=True),
                e.matmul(psG[:, 128:256], qbf[:, :], identb[:],
                         start=True, stop=True).then_inc(s_pe, 1)))
            e4, f4 = task_i("ve", "ve", lambda e: (
                e.tensor_copy(uft[:, :], psG[:, 0:128]).then_inc(s_ve, 1),
                e.tensor_copy(qft[:, :], psG[:, 128:256]).then_inc(s_ve, 1)), n=2)
            queue(task_w("pe", "ve", fv2b), task_w("pe", "ve", psfree["G"]), e3,
                  task_w("ve", "pe", f3), e4)
            psfree["G"] = f4

            e5, f5 = task_i("pe", "pe", lambda e: (
                e.matmul(psH[:, 0:128], uft[:, :], a_bf[:, :], start=True, stop=True),
                e.matmul(psH[:, 128:256], a_bf[:, :], uft[:, :],
                         start=True, stop=True).then_inc(s_pe, 1)))
            e6, f6 = task_i("ve", "ve", lambda e: (
                e.tensor_copy(ptbf[:, :], psH[:, 0:128]).then_inc(s_ve, 1),
                e.tensor_copy(pfbf[:, :], psH[:, 128:256]).then_inc(s_ve, 1)), n=2)
            queue(task_w("pe", "ve", f4), task_w("pe", "ve", psfree["H"]), e5,
                  task_w("ve", "pe", f5), e6)
            psfree["H"] = f6

            e7, f7 = task_i("pe", "pe", lambda e: (
                e.matmul(psG[:, 0:128], pfbf[:, :], uft[:, :],
                         start=True, stop=True).then_inc(s_pe, 1)))
            queue(task_w("pe", "ve", f6), task_w("pe", "ve", psfree["G"]), e7)

            e8, f8 = task_i("ve", "ve", lambda e: (
                e.scalar_tensor_tensor(wms[:, :], identf[:, :], -1.0, psG[:, 0:128],
                                       ALU.mult, ALU.subtract).then_inc(s_ve, 1),
                e.tensor_copy(prt[:, :], ident32[:, :]).then_inc(s_ve, 1),
                e.memset(ltbf[:, :], 0.0).then_inc(s_ve, 1)), n=3)
            queue(task_w("ve", "pe", f7), e8)
            psfree["G"] = f8
            if KCH <= 2:
                return f8

            yf = [None] * 4
            dvf = [None] * 4
            leaf_gate = f8
            for jl in range(4):
                q0 = jl * 32
                for (g0, g1) in ((0, 4), (4, 8), (8, 12), (12, 16),
                                 (16, 20), (20, 24), (24, 28), (28, 32)):
                    def _leafgrp(e, q0=q0, g0=g0, g1=g1):
                        for i in range(g0, g1):
                            e.stream_shuffle(rowb[q0:q0 + 32, i:32],
                                             wms[q0:q0 + 32, q0 + i:q0 + 32], [i] * 32)
                            e.drain()
                            e.reciprocal(absml[q0:q0 + 32, 6:7],
                                         rowb[q0:q0 + 32, i:i + 1])
                            e.drain()
                            e.tensor_tensor(absml[q0:q0 + 32, 7:8],
                                            wms[q0:q0 + 32, q0 + i:q0 + i + 1],
                                            absml[q0:q0 + 32, 6:7], ALU.mult)
                            if i < 31:
                                e.tensor_tensor(rowb[q0:q0 + 32, 32:33],
                                                prt[q0:q0 + 32, i:i + 1],
                                                absml[q0:q0 + 32, 6:7], ALU.mult)
                            e.drain()
                            if i < 31:
                                e.scalar_tensor_tensor(wms[q0:q0 + 32, q0 + i + 1:q0 + 32],
                                                       rowb[q0:q0 + 32, i + 1:32],
                                                       absml[q0:q0 + 32, 7:8],
                                                       wms[q0:q0 + 32, q0 + i + 1:q0 + 32],
                                                       ALU.mult, ALU.subtract)
                            if i < 31:
                                e.scalar_tensor_tensor(prt[q0:q0 + 32, i + 1:32],
                                                       rowb[q0:q0 + 32, i + 1:32],
                                                       rowb[q0:q0 + 32, 32:33],
                                                       prt[q0:q0 + 32, i + 1:32],
                                                       ALU.mult, ALU.subtract)
                            if i < g1 - 1:
                                e.drain()
                        e.drain().then_inc(s_ve, 1)
                    el, fl = task_i("ve", "ve", _leafgrp)
                    if g0 == 0:
                        queue(task_w("ve", "ve", leaf_gate), el)
                    else:
                        queue(el)
                if KCH <= 3:
                    continue
                ec, fc = task_i("ve", "ve", lambda e, q0=q0: (
                    e.tensor_tensor(prbf[q0:q0 + 32, :], prt[q0:q0 + 32, :],
                                    altc[q0:q0 + 32, :], ALU.mult).then_inc(s_ve, 1),
                    (e.tensor_copy(panbf[q0:q0 + 32, q0 + 32:128],
                                   wms[q0:q0 + 32, q0 + 32:128]).then_inc(s_ve, 1)
                     if q0 + 32 < 128 else e.engine_nop().then_inc(s_ve, 1))), n=2)
                queue(ec)

                ed, fd = task_i("ve", "ve", lambda e, q0=q0: (
                    e.tensor_tensor(rowb[q0:q0 + 32, 0:32], wms[q0:q0 + 32, q0:q0 + 32],
                                    ident32[q0:q0 + 32, :], ALU.mult),
                    e.drain(),
                    e.reduce_sum(rtl[q0:q0 + 32, 0:1], rowb[q0:q0 + 32, 0:32], AX.X),
                    e.drain(),
                    e.reciprocal(rtl[q0:q0 + 32, 0:1], rtl[q0:q0 + 32, 0:1]),
                    e.drain(),
                    e.tensor_tensor(rtl[q0:q0 + 32, 0:1], rtl[q0:q0 + 32, 0:1],
                                    rtl[q0:q0 + 32, 1:2], ALU.mult).then_inc(s_ve, 1)))
                queue(ed)
                dvf[jl] = fd
                if q0 + 32 < 128:
                    wdt = 128 - q0 - 32
                    em3, fm3 = task_i("pe", "pe", lambda e, q0=q0, wdt=wdt: (
                        e.matmul(psG[0:32, 0:wdt], prbf[q0:q0 + 32, :],
                                 panbf[q0:q0 + 32, q0 + 32:128],
                                 start=True, stop=True).then_inc(s_pe, 1)))
                    queue(task_w("pe", "ve", fc), task_w("pe", "ve", psfree["G"]), em3)
                    ep, fp = task_i("ve", "ve", lambda e, q0=q0, wdt=wdt: (
                        e.tensor_scalar(ltbf[q0:q0 + 32, q0 + 32:128], psG[0:32, 0:wdt],
                                        rtl[q0:q0 + 32, 0:1], None,
                                        ALU.mult).then_inc(s_ve, 1),
                        e.tensor_copy(panbf[q0:q0 + 32, q0 + 32:128],
                                      psG[0:32, 0:wdt]).then_inc(s_ve, 1)), n=2)
                    queue(task_w("ve", "pe", fm3), task_w("ve", "ve", fd), ep)
                    psfree["G"] = fp
                    em4, fm4 = task_i("pe", "pe", lambda e, q0=q0, wdt=wdt: (
                        e.matmul(psG[0:wdt, 0:wdt], ltbf[q0:q0 + 32, q0 + 32:128],
                                 panbf[q0:q0 + 32, q0 + 32:128],
                                 start=True, stop=True).then_inc(s_pe, 1)))
                    queue(task_w("pe", "ve", fp), em4)
                    base = q0 + 32
                    es2, fs2 = task_i("ve", "ve", lambda e, base=base, wdt=wdt: (
                        [e.tensor_tensor(wms[qq:qq + 32, base:128], wms[qq:qq + 32, base:128],
                                         psG[qq - base:qq - base + 32, 0:wdt],
                                         ALU.add).then_inc(s_ve, 1)
                         for qq in range(base, 128, 32)]), n=(128 - base) // 32)
                    queue(task_w("ve", "pe", fm4), es2)
                    psfree["G"] = fs2
                    leaf_gate = fs2

                if jl == 0:
                    em1, fm1 = task_i("pe", "pe", lambda e: (
                        e.matmul(psH[0:32, 0:HB], prbf[0:32, :], ptbf[0:32, :],
                                 start=True, stop=True).then_inc(s_pe, 1)))
                    queue(task_w("pe", "ve", fc), task_w("pe", "ve", psfree["H"]), em1)
                    ey, fy = task_i("ve", "ve", lambda e: (
                        e.tensor_copy(yts[0:32, :], psH[0:32, 0:HB]).then_inc(s_ve, 1),
                        e.tensor_copy(ytb[0:32, :], psH[0:32, 0:HB]).then_inc(s_ve, 1)),
                        n=2)
                    queue(task_w("ve", "pe", fm1), ey)
                    psfree["H"] = fy
                    yf[0] = fy
                else:
                    em1, fm1 = task_i("pe", "pe", lambda e, jl=jl, q0=q0: (
                        e.matmul(psH[0:32, 0:HB], ltbf[:, q0:q0 + 32], ytb[:, :],
                                 start=True, stop=True).then_inc(s_pe, 1)))
                    queue(task_w("pe", "ve", fc), task_w("pe", "ve", psfree["H"]), em1)
                    ea2, fa2 = task_i("ve", "ve", lambda e, q0=q0: (
                        e.tensor_tensor(accb[q0:q0 + 32, :], ptbf[q0:q0 + 32, :],
                                        psH[0:32, 0:HB], ALU.add).then_inc(s_ve, 1)))
                    queue(task_w("ve", "pe", fm1), ea2)
                    em2, fm2 = task_i("pe", "pe", lambda e, q0=q0: (
                        e.matmul(psH[0:32, 0:HB], prbf[q0:q0 + 32, :], accb[q0:q0 + 32, :],
                                 start=True, stop=True,
                                 tile_position=(q0, 0)).then_inc(s_pe, 1)))
                    queue(task_w("pe", "ve", fa2), em2)
                    ey, fy = task_i("ve", "ve", lambda e, q0=q0: (
                        e.tensor_copy(yts[q0:q0 + 32, :], psH[0:32, 0:HB]).then_inc(s_ve, 1),
                        e.tensor_copy(ytb[q0:q0 + 32, :], psH[0:32, 0:HB]).then_inc(s_ve, 1)),
                        n=2)
                    queue(task_w("ve", "pe", fm2), ey)
                    psfree["H"] = fy
                    yf[jl] = fy

            if KCH <= 4:
                return fl
            ez, fz = task_i("ve", "ve", lambda e: (
                e.tensor_scalar(ztb[:, :], yts[:, :], rtl[:, 0:1], None,
                                ALU.mult).then_inc(s_ve, 1),
                e.tensor_scalar(nyb[:, :], yts[:, :], -1.0, None,
                                ALU.mult).then_inc(s_ve, 1)), n=2)
            queue(task_w("ve", "ve", dvf[3]), task_w("ve", "ve", yf[3]), ez)

            e9, f9 = task_i("pe", "pe", lambda e: (
                e.matmul(psG[:, 0:128], ztb[:, :], identb[:],
                         start=True, stop=True).then_inc(s_pe, 1)))
            queue(task_w("pe", "ve", fz), task_w("pe", "ve", psfree["G"]), e9)
            e10, f10 = task_i("ve", "ve", lambda e: (
                e.tensor_copy(zft[:, :], psG[:, 0:128]).then_inc(s_ve, 1)))
            queue(task_w("ve", "pe", f9), e10)
            psfree["G"] = f10

            e11, f11 = task_i("pe", "pe", lambda e: (
                e.matmul(psG[:, 0:128], zft[:, :], qft[:, :],
                         start=True, stop=True).then_inc(s_pe, 1)))
            queue(task_w("pe", "ve", f10), e11)
            e12, f12 = task_i("ve", "ve", lambda e: (
                e.tensor_tensor(pzb[:, :], psG[:, 0:128], masku[:, :],
                                ALU.mult).then_inc(s_ve, 1)))
            queue(task_w("ve", "pe", f11), e12)
            psfree["G"] = f12

            e13, f13 = task_i("pe", "pe", lambda e: (
                e.matmul(psH[:, 0:128], mt_bf[:, :], qft[:, :], start=True, stop=False),
                e.matmul(psH[:, 0:128], vsbf[:, :], pzb[:, :],
                         start=False, stop=True).then_inc(s_pe, 1)))
            queue(task_w("pe", "ve", f12), task_w("pe", "ve", psfree["H"]), e13)
            e14, f14 = task_i("ve", "ve", lambda e: (
                e.tensor_copy(obf[:, :], psH[:, 0:128]).then_inc(s_ve, 1)))
            queue(task_w("ve", "pe", f13), e14)
            psfree["H"] = f14

            for fb2 in range(8):
                bank = "G" if fb2 % 2 == 0 else "H"
                ps = psG if bank == "G" else psH
                ep1, fp1 = task_i("pe", "pe", lambda e, fb2=fb2, ps=ps: (
                    e.matmul(ps[:, 0:128], wobT[:, fb2, :], obf[:, :],
                             start=True, stop=True).then_inc(s_pe, 1)))
                ep2, fp2 = task_i("ve", "ve", lambda e, fb2=fb2, ps=ps, rbuf=rbuf, cs=cs: (
                    e.scalar_tensor_tensor(rres[:, rbuf, fb2, cs:cs + 128], ps[:, 0:128],
                                           bobc[:, fb2:fb2 + 1],
                                           rres[:, rbuf, fb2, cs:cs + 128], ALU.add,
                                           ALU.add).then_inc(s_ve, 1)))
                queue(task_w("pe", "ve", f14), task_w("pe", "ve", psfree[bank]), ep1,
                      task_w("ve", "pe", fp1), ep2)
                psfree[bank] = fp2

            e15, f15 = task_i("pe", "pe", lambda e: (
                e.matmul(psG[:, 0:128], nyb[:, :], ztb[:, :], start=True, stop=True),
                e.matmul(psG[:, 128:256], ztb[:, :], vsbf[:, :],
                         start=True, stop=True).then_inc(s_pe, 1)))
            queue(task_w("pe", "ve", f14), task_w("pe", "ve", psfree["G"]), e15)
            e16, f16 = task_i("ve", "ve", lambda e: (
                e.tensor_tensor(a_sb[:, :], a_sb[:, :], psG[:, 0:128], ALU.add),
                e.tensor_tensor(mt_sb[:, :], mt_sb[:, :], psG[:, 128:256], ALU.add),
                e.drain(),
                e.tensor_copy(a_bf[:, :], a_sb[:, :]).then_inc(s_ve, 1),
                e.tensor_copy(mt_bf[:, :], mt_sb[:, :]).then_inc(s_ve, 1)), n=2)
            queue(task_w("ve", "pe", f15), e16)
            psfree["G"] = f16
            return f16

        def queue_tail(ib, fmov, prev_out):
            rbuf = ib % 2
            obuf = ib % 2
            proj_last = None
            for fb in range(8):
                bank = "G" if fb % 2 == 0 else "H"
                ps = psG if bank == "G" else psH
                e1, f1 = task_i("pe", "pe", lambda e, fb=fb, ps=ps, obuf=obuf: (
                    [e.matmul(ps[:, :], woT[:, hb, fb * 128:(fb + 1) * 128],
                              oib[:, obuf, hb, :], start=(hb == 0), stop=(hb == 7))
                     for hb in range(8)][-1].then_inc(s_pe, 1)))
                e2, f2 = task_i("ve", "ve", lambda e, fb=fb, ps=ps, rbuf=rbuf: (
                    e.scalar_tensor_tensor(rres[:, rbuf, fb, :], ps[:, :], bcol[:, fb, 3:4],
                                           rres[:, rbuf, fb, :], ALU.add,
                                           ALU.add).then_inc(s_ve, 1)))
                queue(task_w("pe", "g", fmov), task_w("pe", "ve", psfree[bank]), e1,
                      task_w("ve", "pe", f1), e2)
                psfree[bank] = f2
                proj_last = f2
                if fb == 7:
                    hist["proj_pe"][ib] = f1
            ln2 = queue_lnfm(0, proj_last, rbuf, prev_out)
            ffn1 = queue_ffn(mr1s[0:1, 0, :], rbuf, ln2)
            ln3 = queue_lnfm(1, ffn1, rbuf, None)
            last_c = ln3
            for c in (range(4) if KCUT >= 3 else ()):
                last_c = queue_chunk(rbuf, c, ln3, last_c)
            ln4 = queue_lnfm(2, last_c, rbuf, None)
            ffn2 = queue_ffn(mr1s[0:1, 2, :], rbuf, ln4)
            fo = None
            stg_pe_last = None
            for k in range(4):
                stg_ve_last = None
                for half in range(2):
                    bank = "G" if half == 0 else "H"
                    ps = psG if bank == "G" else psH
                    e1, f1 = task_i("pe", "pe",
                                    lambda e, k=k, half=half, ps=ps, rbuf=rbuf: (
                        e.drain(),
                        [e.matmul(ps[:, q2 * 128:(q2 + 1) * 128],
                                  rres[:, rbuf, half * 4 + q2, k * 128:(k + 1) * 128],
                                  identf[:], start=True, stop=True)
                         for q2 in range(4)][-1].then_inc(s_pe, 1),
                        e.drain()))
                    e2, f2 = task_i("ve", "ve", lambda e, half=half, ps=ps: (
                        e.tensor_copy(stg[:, half * 512:(half + 1) * 512],
                                      ps[:, :]).then_inc(s_ve, 1)))
                    ents = [task_w("pe", "ve", ffn2), task_w("pe", "ve", psfree[bank]), e1,
                            task_w("ve", "pe", f1)]
                    if fo is not None:
                        ents.append(task_w("ve", "g", fo))
                    ents.append(e2)
                    queue(*ents)
                    psfree[bank] = f2
                    stg_pe_last = f1
                    stg_ve_last = f2
                oi = ib % 2
                prev_tot = od_state["tot"][oi]
                od_state["tot"][oi] += 16
                new_tot = od_state["tot"][oi]
                eo = ("d", "gp", lambda e, ib=ib, k=k, s=ods[oi]: e.dma_start(
                    out=out_t.rearrange("(n p) f -> p n f", p=128)[
                        :, 4 * ib + k:4 * ib + k + 1, :],
                    in_=stg[:, :].rearrange("p (o f) -> p o f", o=1)).then_inc(s, 16))
                ents2 = [task_w("gp", "ve", stg_ve_last)]
                if prev_tot > 0:
                    ents2.append(task_w("gp", "", ("O", oi, prev_tot)))
                ents2.append(eo)
                queue(*ents2)
                fo = ("O", oi, new_tot)
            hist["tail_stg_pe"][ib] = stg_pe_last
            return fo

        out_futs = []
        for ib in range(NIB_EMIT):
            rbuf = ib % 2
            obuf = ib % 2
            if ib == 0:
                W("ve", "g", t_weff)
                W("sc", "g", t_weff)
                W("pe", "g", t_vstones)
            if ib >= 1:
                WT("ve", hist["omov"][ib - 1])
            if ib >= 2 and KCUT >= 2:
                pump_until(hist["tail_stg_pe"][ib - 2])
                W("ve", "pe", hist["tail_stg_pe"][ib - 2])
                pump_until(hist["proj_pe"][ib - 2])
                W("ve", "pe", hist["proj_pe"][ib - 2])
            W("ve", "pe", hist["qm_last"])

            tA = tB = 0
            tO = 0

            def _sA(v):
                nonlocal tA
                tA = v

            def _sB(v):
                nonlocal tB
                tB = v

            for k in range(4):
                tt = 4 * ib + k
                if ib == 0 and k == 0:
                    W("sy", "ve", t_s1_ve)
                    W("sy", "g", t_weff)
                else:
                    W("sy", "pe", CT["pe"])
                tl = ld(xt, x_in.rearrange("(n p) f -> n p f", p=128)[tt])
                WT("ve", tl)
                def _aff(e, tt=tt):
                    e.tensor_scalar(xlnf[:, :], xt[:, :], lnst[:, tt, 0:1], lnst[:, tt, 1:2],
                                    ALU.subtract, ALU.mult)
                    e.drain()
                    e.tensor_copy(xlnb[:, :], xlnf[:, :])
                    e.drain().then_inc(s_ve, 1)
                E("ve", _aff)
                ta = inc("ve", 1)
                W("pe", "sc", hist["exp_last"])
                xlnT_transposes(k, ta, [
                    (psA[:, 0:512], tA, _sA, 0),
                    (psB[:, 0:512], tB, _sB, 1),
                ])
                for half in range(2):
                    ps_reg = psA[:, 512:1024] if half == 0 else psB[:, 512:1024]
                    gate = tA if half == 0 else tB
                    W("pe", "ve", ta)
                    W("pe", "ve", gate)
                    def _trf(e, ps_reg=ps_reg, half=half):
                        e.drain()
                        for q2 in range(4):
                            fb = half * 4 + q2
                            mm = e.matmul(ps_reg[:, q2 * 128:(q2 + 1) * 128],
                                          xt[:, fb * 128:(fb + 1) * 128], identf[:],
                                          start=True, stop=True)
                        mm.then_inc(s_pe, 1)
                        e.drain()
                    E("pe", _trf)
                    tp = inc("pe", 1)
                    W("ve", "pe", tp)
                    def _evf(e, k=k, half=half, ps_reg=ps_reg, rbuf=rbuf):
                        e.tensor_copy(rres[:, rbuf, half * 4:half * 4 + 4, k * 128:(k + 1) * 128],
                                      ps_reg.rearrange("p (q f) -> p q f",
                                                       q=4)).then_inc(s_ve, 1)
                    E("ve", _evf)
                    tvv = inc("ve", 1)
                    if half == 0:
                        tA = tvv
                    else:
                        tB = tvv
                pump(3)
            t_xprep = CT["ve"]

            for hp in range(HP):
                idx = ib * HP + hp
                sbuf_i = idx % 2
                W("pe", "ve", t_xprep)
                W("pe", "ve", tO)
                W("pe", "ve", t_wq)
                def _qm(e, hp=hp):
                    for fb in range(8):
                        mm = e.matmul(psO[:, 0:512], wqT[:, fb, hp * 128:(hp + 1) * 128],
                                      xlnT[:, fb, :], start=(fb == 0), stop=(fb == 7))
                    mm.then_inc(s_pe, 1)
                E("pe", _qm)
                tqm = inc("pe", 1)
                hist["qm_last"] = tqm
                W("ve", "pe", tqm)
                def _qev(e):
                    e.tensor_scalar(qbt[:, :], psO[:, 0:512], bcol[:, hp, 0:1], None, ALU.add) \
                        if False else None
                E("ve", lambda e, hp=hp: (
                    e.tensor_scalar(qbt[:, :], psO[:, 0:512], bcol[:, hp, 0:1], None,
                                    ALU.add).then_inc(s_ve, 1)))
                tqev = inc("ve", 1)

                W("pe", "ve", tqev)
                WT("pe", t_kst[idx])
                WT("pe", t_vst[idx][0])
                WT("pe", t_vst[idx][1])

                exp_cnt = [0] * 32
                av_cnt = [0] * 32

                def _avj(j):
                    W("pe", "sc", exp_cnt[j])
                    def _av(e, j=j, sbuf_i=sbuf_i):
                        e.matmul(psO[0:65, 0:512], vst[:, sbuf_i, j, 0:65],
                                 ptile[:, j % 2, 0:512], start=(j == 0), stop=(j == 31))
                        e.matmul(psO[0:65, 512:1024], vst[:, sbuf_i, j, 65:130],
                                 ptile[:, j % 2, 512:1024], start=(j == 0),
                                 stop=(j == 31)).then_inc(s_pe, 1)
                    E("pe", _av)
                    av_cnt[j] = inc("pe", 1)

                for j in range(32):
                    ps_s = psA if j % 2 == 0 else psB
                    if j >= 2:
                        W("pe", "sc", exp_cnt[j - 2])
                    def _scj(e, j=j, ps_s=ps_s, sbuf_i=sbuf_i):
                        e.matmul(ps_s[:, 0:512], kst[0:64, sbuf_i, j * 128:(j + 1) * 128],
                                 qbt[0:64, :], start=True, stop=True, tile_position=(0, 0))
                        e.matmul(ps_s[:, 512:1024], kst[64:128, sbuf_i, j * 128:(j + 1) * 128],
                                 qbt[64:128, :], start=True, stop=True,
                                 tile_position=(64, 0)).then_inc(s_pe, 1)
                    E("pe", _scj)
                    t_scj = inc("pe", 1)
                    W("sc", "pe", t_scj)
                    if j >= 2:
                        W("sc", "pe", av_cnt[j - 2])
                    def _exp(e, j=j, ps_s=ps_s):
                        e.activation(ptile[:, j % 2, :], ps_s[:, :], AF.Exp,
                                     bias=cpool[:, 0:1], scale=INV_SQRT_D).then_inc(s_sc, 1)
                    E("sc", _exp)
                    exp_cnt[j] = inc("sc", 1)
                    if j >= 1:
                        _avj(j - 1)
                _avj(31)
                hist["exp_last"] = exp_cnt[31]
                t_av = av_cnt[31]
                kst_free[sbuf_i] = t_av
                vst_free[sbuf_i] = t_av
                if idx + 2 < NIB_EMIT * HP:
                    stream_loads(idx + 2)
                W("ve", "pe", t_av)
                def _rc(e):
                    with nc.allow_low_precision(reason="softmax denominators; probs are bf16"):
                        e.reciprocal(rcp[0:1, 0:512], psO[64:65, 0:512]).then_inc(s_ve, 1)
                        e.reciprocal(rcp[0:1, 512:1024],
                                     psO[64:65, 512:1024]).then_inc(s_ve, 1)
                E("ve", _rc)
                trc = inc("ve", 2)
                W("pe", "ve", trc)
                W("pe", "sc", exp_cnt[31])
                def _rb(e):
                    e.matmul(psA[:, 0:512], onesrow[:, :],
                             rcp[0:1, 0:512], start=True, stop=True)
                    e.matmul(psB[:, 0:512], onesrow[:, :],
                             rcp[0:1, 512:1024], start=True, stop=True).then_inc(s_pe, 1)
                E("pe", _rb)
                trb = inc("pe", 1)
                W("ve", "pe", trb)
                def _nrm(e, hp=hp, obuf=obuf):
                    e.tensor_copy(rcpB[:, 0:512], psA[:, 0:512])
                    e.tensor_copy(rcpB[:, 512:1024], psB[:, 0:512])
                    e.drain()
                    e.tensor_tensor(oib[0:64, obuf, hp, :], psO[0:64, 0:512],
                                    rcpB[0:64, 0:512], ALU.mult).then_inc(s_ve, 1)
                    e.tensor_tensor(ohp[0:64, hp, :], psO[0:64, 512:1024],
                                    rcpB[0:64, 512:1024], ALU.mult).then_inc(s_ve, 1)
                E("ve", _nrm)
                tnrm = inc("ve", 2)
                tA = tB = tO = tnrm
                pump(14)
            if KCUT >= 2:
                W("gp", "ve", CT["ve"])
                mi = ib % 2
                if om_state["tot"][mi] > 0:
                    Q["gp"].append(lambda e, s=oms[mi], v=om_state["tot"][mi]: e.wait_ge(s, v))
                E("gp", lambda e, s=oms[mi], obuf=obuf: e.dma_start(
                    out=oib[64:128, obuf, :, :], in_=ohp[0:64, :, :]).then_inc(s, 16))
                om_state["tot"][mi] += 16
                fmov = ("M", mi, om_state["tot"][mi])
                hist["omov"][ib] = fmov
                prev_out = out_futs[-1] if out_futs else None
                out_futs.append(queue_tail(ib, fmov, prev_out))
        pump_all()

        W("gp", "g", CT["g"])
        WT("gp", ("O", 0, od_state["tot"][0]))
        WT("gp", ("O", 1, od_state["tot"][1]))
        def _fin(e):
            e.drain().then_inc(s_g, 1)
        E("gp", _fin)
        inc("g", 1)

        @blk.sync
        def _(eng):
            for fn in Q["sy"]:
                fn(eng)

        @blk.gpsimd
        def _(eng):
            for fn in Q["gp"]:
                fn(eng)

        @blk.tensor
        def _(eng):
            for fn in Q["pe"]:
                fn(eng)

        @blk.vector
        def _(eng):
            for fn in Q["ve"]:
                fn(eng)

        @blk.scalar
        def _(eng):
            for fn in Q["sc"]:
                fn(eng)

    return nc


def kernel(**inputs):
    x = np.ascontiguousarray(inputs["L0_attn_a_x"], dtype=np.float32)
    biases = np.stack([inputs["L0_attn_a_bq"], inputs["L0_attn_a_bk"],
                       inputs["L0_attn_a_bv"], inputs["L0_attn_a_bo"]], 0).astype(np.float32)
    base = {
        "wq": np.ascontiguousarray(inputs["L0_attn_a_Wq"], np.float32),
        "wk": np.ascontiguousarray(inputs["L0_attn_a_Wk"], np.float32),
        "wv": np.ascontiguousarray(inputs["L0_attn_a_Wv"], np.float32),
        "wo": np.ascontiguousarray(inputs["L0_attn_a_Wo"], np.float32),
        "biases": biases,
        "fw": np.ascontiguousarray(inputs["L0_ffn_W"], np.float32),
        "fbm": np.ascontiguousarray(inputs["L0_ffn_b"], np.float32),
        "flog": np.ascontiguousarray(inputs["L0_ffn_logits"], np.float32).reshape(1, NEX),
        "fu": np.ascontiguousarray(inputs["L0_ffn_u"], np.float32).reshape(1, NEX),
        "wqb": np.ascontiguousarray(inputs["L0_attn_b_Wq"], np.float32),
        "wkb": np.ascontiguousarray(inputs["L0_attn_b_Wk"], np.float32),
        "wvb": np.ascontiguousarray(inputs["L0_attn_b_Wv"], np.float32),
        "wob": np.ascontiguousarray(inputs["L0_attn_b_Wo"], np.float32),
        "bqb": np.ascontiguousarray(inputs["L0_attn_b_bq"], np.float32).reshape(1, HB),
        "bkb": np.ascontiguousarray(inputs["L0_attn_b_bk"], np.float32).reshape(1, HB),
        "bvb": np.ascontiguousarray(inputs["L0_attn_b_bv"], np.float32).reshape(1, HB),
        "bob": np.ascontiguousarray(inputs["L0_attn_b_bo"], np.float32).reshape(1, D),
        "bcolh": np.ascontiguousarray(
            biases.reshape(4, 8, 128).transpose(2, 1, 0).reshape(128, 32)),
        "bobch": np.ascontiguousarray(
            np.asarray(inputs["L0_attn_b_bo"], np.float32).reshape(8, 128).T),
    }
    nc = build_program()
    in_maps = []
    for b in range(8):
        m = dict(base)
        m["x_in"] = np.ascontiguousarray(x[b])
        in_maps.append(m)
    res = run_bass_kernel_spmd(nc, in_maps, core_ids=list(range(8)))
    return np.stack([res.results[b]["out"] for b in range(8)], 0)
